# version 01
# baseline (speedup 1.0000x reference)
import math
from contextlib import ExitStack

import numpy as np
import ml_dtypes

import concourse.bass as bass
import concourse.mybir as mybir
from concourse.bass_utils import run_bass_kernel_spmd

F32 = mybir.dt.float32
I16 = mybir.dt.int16
BF16 = mybir.dt.bfloat16
AF = mybir.ActivationFunctionType
ALU = mybir.AluOpType
AX = mybir.AxisListType

T = 4096
D = 1024
C = 8
TQ = 512
NTQ = T // TQ
NT = T // 128
BF = ml_dtypes.bfloat16
SHIFT = 15.0

DEBUG_TAPS = []
NO_CC = False



ENGINES = ("tensor", "vector", "scalar", "gpsimd", "sync")


class Op:
    __slots__ = ("idx", "engine", "fn", "after", "dma", "sig", "sigval", "tag",
                 "semkey", "cc")

    def __init__(self, idx, engine, fn, after, dma, tag):
        self.idx = idx
        self.engine = engine
        self.fn = fn
        self.after = after
        self.dma = dma
        self.cc = False
        self.sig = False
        self.sigval = None
        self.semkey = None
        self.tag = tag


DMA_CHANNELS = {"sync": 6, "gpsimd": 6, "scalar": 2, "vector": 2, "tensor": 1}


class Em:

    def __init__(self, ctx, name):
        self.ctx = ctx
        self.name = name
        self.ops = []
        self.opset = set()

    def op(self, engine, fn, after=(), dma=False, tag=""):
        kept = []
        for a in after:
            if a is None:
                continue
            if id(a) in self.opset or a.sigval is not None:
                kept.append(a)
        o = Op(len(self.ops), engine, fn, tuple(kept), dma, tag)
        self.ops.append(o)
        self.opset.add(id(o))
        return o

    def dma(self, engine, out, in_, after=(), tag=""):
        return self.op(engine, lambda e: e.dma_start(out=out, in_=in_),
                       after=after, dma=True, tag=tag)

    def emit(self, drain_dmas=True):
        if drain_dmas:
            for e in ENGINES:
                dops = [o for o in self.ops if o.engine == e and o.dma]
                if dops:
                    self.op(e, lambda _e: None, after=dops, tag="dma_drain")
        sems = self.ctx.sems
        for o in self.ops:
            if o.dma:
                o.sig = True
            for a in o.after:
                if a.dma or a.engine != o.engine or o.engine != "tensor":
                    a.sig = True
        rr = self.ctx.dma_rr
        for o in self.ops:
            if not o.sig:
                continue
            if o.cc:
                key = "Dcc"
            elif o.dma:
                ch = rr[o.engine] % DMA_CHANNELS[o.engine]
                rr[o.engine] += 1
                key = f"D{o.engine}{ch}"
            else:
                key = "E" + o.engine
            o.semkey = key
            amt = 16 if (o.dma and not o.cc) else 1
            o.sigval = sems.v[key] + amt
            sems.v[key] = o.sigval
        nc = self.ctx.nc
        with nc.Block(self.name) as blk:
            for eng_name in ENGINES:
                eng_ops = [o for o in self.ops if o.engine == eng_name]
                if not eng_ops:
                    continue

                def body(eng, eng_ops=eng_ops, eng_name=eng_name):
                    for o in eng_ops:
                        waited = {}
                        deps = list(o.after)
                        for a in deps:
                            if not a.sig:
                                continue
                            if a.dma or a.engine != o.engine or o.engine != "tensor":
                                key = a.semkey
                                if waited.get(key, -1) >= a.sigval:
                                    continue
                                waited[key] = a.sigval
                                eng.wait_ge(sems.h[key], a.sigval)
                        if o.dma and not o.cc and o.sigval > 16:
                            if waited.get(o.semkey, -1) < o.sigval - 16:
                                eng.wait_ge(sems.h[o.semkey], o.sigval - 16)
                        inst = o.fn(eng)
                        if o.sig:
                            if inst is None:
                                inst = eng.engine_nop() if hasattr(eng, "engine_nop") \
                                    else eng.nop()
                            if o.cc:
                                inst.then_inc(sems.h[o.semkey])
                            else:
                                inst.then_inc(sems.h[o.semkey], 16 if o.dma else 1)

                getattr(blk, eng_name)(body)


class Sems:
    def __init__(self, nc, stack, names):
        self.h = {n: stack.enter_context(nc.semaphore(n)) for n in names}
        self.v = {n: 0 for n in names}


class Ctx:
    def __init__(self, nc, stack):
        self.nc = nc
        self.stack = stack
        self.t = {}

    def sb(self, name, shape, dtype, stack=None):
        self._uniq = getattr(self, "_uniq", 0) + 1
        h = (stack or self.stack).enter_context(
            self.nc.sbuf_tensor(f"sb{self._uniq}_{name}", shape, dtype))
        self.t[name] = h
        return h



def host_constants(core):
    cn = {}
    eye = np.eye(128, dtype=np.float32)
    cn["EYEB"] = eye.astype(BF)
    cn["EYEF"] = eye.copy()
    M = np.zeros((64, 64), np.float32)
    M[0:32, 32:64] = -np.eye(32)
    M[32:64, 0:32] = np.eye(32)
    PN = np.zeros((128, 128), np.float32)
    PN[0:64, 0:64] = M.T
    PN[64:128, 64:128] = M.T
    cn["PN"] = PN.astype(BF)
    PJ = np.zeros((128, 128), np.float32)
    for i in range(64):
        PJ[2 * i, 2 * i + 1] = 1.0
        PJ[2 * i + 1, 2 * i] = 1.0
    cn["PJ"] = PJ.astype(BF)
    thetas = 10000.0 ** (-np.arange(0, 64, 2, dtype=np.float32) / 64.0)
    ticks = np.outer(np.arange(T, dtype=np.float32), thetas)
    ticks = np.tile(ticks, (1, 2))
    cosL0 = np.cos(ticks).T.astype(np.float32)
    sinL0 = np.sin(ticks).T.astype(np.float32)
    cn["COS0"] = np.concatenate([cosL0, cosL0], 0).astype(BF)
    cn["SIN0"] = np.concatenate([sinL0, sinL0], 0).astype(BF)
    half = 32
    inv_freq = 1.0 / 10000.0 ** (np.arange(half, dtype=np.float32) * 2.0 / 64.0)
    ang = np.arange(T, dtype=np.float32)[:, None] * inv_freq[None, :]
    cosj = np.cos(ang)
    sinj = np.sin(ang)
    CJ = np.zeros((64, T), np.float32)
    SJ = np.zeros((64, T), np.float32)
    for i in range(half):
        CJ[2 * i] = cosj[:, i]
        CJ[2 * i + 1] = cosj[:, i]
        SJ[2 * i] = -sinj[:, i]
        SJ[2 * i + 1] = sinj[:, i]
    cn["COS1"] = np.concatenate([CJ, CJ], 0).astype(BF)
    cn["SIN1"] = np.concatenate([SJ, SJ], 0).astype(BF)
    p = np.arange(128)[:, None]
    f = np.arange(TQ)[None, :]
    masks = [(f >= p + 128 * d).astype(np.float32) for d in range(4)]
    cn["CMASK"] = np.concatenate(masks, 1).astype(BF)
    PAT = np.zeros((128, 128), np.float32)
    PAT[0:64, 0] = 1.0
    PAT[64:128, 64] = 1.0
    cn["PAT2F"] = PAT.copy()
    cn["PAT2B"] = PAT.astype(BF)
    cn["ONESCB"] = np.ones((128, 1), np.float32).astype(BF)
    cn["ONESC"] = np.ones((128, 1), np.float32)
    cn["MEANCB"] = np.full((128, 1), 1.0 / D, np.float32).astype(BF)
    cn["IOTA8"] = np.arange(8, dtype=np.float32)[:, None]
    cn["IOTA8B"] = np.arange(8, dtype=np.float32)[:, None].astype(BF)
    cn["ONES8B"] = np.ones((8, 1), np.float32).astype(BF)
    cn["ONES8"] = np.ones((8, 1), np.float32)
    pmv = np.zeros((64, 8), np.float32)
    for r in range(core):
        for e in range(8):
            pmv[8 * r + e, e] = 1.0
    cn["PM64"] = pmv
    E = np.zeros((16, 16 * 128), np.float32)
    for e in range(16):
        E[e, 128 * e:128 * (e + 1)] = 1.0
    cn["E16"] = E.astype(BF)
    cn["ONESR"] = np.ones((1, 128), np.float32).astype(BF)
    E8 = np.zeros((8, 8 * 128), np.float32)
    for e in range(8):
        E8[e, 128 * e:128 * (e + 1)] = 1.0
    cn["E8"] = E8.astype(BF)
    cn["ONESRF"] = np.ones((1, 128), np.float32)
    return cn


TABLE_SHAPES = {
    "COS0": ([128, T], BF16), "SIN0": ([128, T], BF16),
    "COS1": ([128, T], BF16), "SIN1": ([128, T], BF16),
    "CMASK": ([128, 2048], BF16),
}

CONST_SHAPES = {
    "EYEB": ([128, 128], BF16), "EYEF": ([128, 128], F32),
    "PN": ([128, 128], BF16), "PJ": ([128, 128], BF16),
    "PAT2F": ([128, 128], F32),
    "PAT2B": ([128, 128], BF16), "ONESCB": ([128, 1], BF16),
    "ONESC": ([128, 1], F32), "MEANCB": ([128, 1], BF16),
    "IOTA8": ([8, 1], F32), "ONES8": ([8, 1], F32),
    "IOTA8B": ([8, 1], BF16), "ONES8B": ([8, 1], BF16),
    "PM64": ([64, 8], F32), "E16": ([16, 16 * 128], BF16),
    "ONESR": ([1, 128], BF16), "ONESRF": ([1, 128], F32),
    "E8": ([8, 8 * 128], BF16),
}

WEIGHTS = {
    "L0_attn_w_q": [D, D], "L0_attn_w_k": [64, D], "L0_attn_w_v": [64, D],
    "L0_attn_w_out": [D, D], "L0_attn_softmax_scale": [1],
    "L0_ffn_g1": [D], "L0_ffn_g2": [D],
    "L0_ffn_wq": [D, D], "L0_ffn_wk": [D, 256], "L0_ffn_wv": [D, 256],
    "L0_ffn_wo": [D, D], "L0_ffn_router_w": [D, 8], "L0_ffn_router_b": [8],
    "L0_ffn_w1": [8, D, 2048], "L0_ffn_w2": [8, D, 2048], "L0_ffn_w3": [8, 2048, D],
    "L1_attn_Wq": [D, D], "L1_attn_bq": [D], "L1_attn_Wk": [D, D], "L1_attn_bk": [D],
    "L1_attn_Wv": [D, D], "L1_attn_bv": [D], "L1_attn_Wp": [D, D], "L1_attn_bp": [D],
    "L1_ffn_Wm": [D, D], "L1_ffn_bm": [D], "L1_ffn_expert_emb": [128, 16],
    "L1_ffn_W1": [16, 128, 512], "L1_ffn_b1": [16, 512],
    "L1_ffn_W2": [16, 512, 128], "L1_ffn_b2": [16, 128],
    "L1_ffn_Wo": [D, D], "L1_ffn_bo": [D],
}

BLOCK_WEIGHTS = {
    "A": ["L0_attn_w_q", "L0_attn_w_k", "L0_attn_w_v", "L0_attn_w_out",
          "L0_attn_softmax_scale"],
    "B": ["L0_ffn_g1", "L0_ffn_g2", "L0_ffn_wq", "L0_ffn_wk", "L0_ffn_wv",
          "L0_ffn_wo", "L0_ffn_router_w", "L0_ffn_router_b", "L0_ffn_w1",
          "L0_ffn_w2", "L0_ffn_w3"],
    "C": ["L1_attn_Wq", "L1_attn_bq", "L1_attn_Wk", "L1_attn_bk", "L1_attn_Wv",
          "L1_attn_bv", "L1_attn_Wp", "L1_attn_bp"],
    "D": ["L1_ffn_Wm", "L1_ffn_bm", "L1_ffn_expert_emb", "L1_ffn_W1",
          "L1_ffn_b1", "L1_ffn_W2", "L1_ffn_b2", "L1_ffn_Wo", "L1_ffn_bo"],
}



def bk(ctx, i, w=TQ, p0=0, p=128):
    return ctx.t["PSUM"][p0:p0 + p, 512 * i:512 * i + w]


def bkspan(ctx, i, nb):
    return ctx.t["PSUM"][:, 512 * i:512 * (i + nb)]


def xsl(ctx, c, tqc, w=TQ, start=0):
    return ctx.t["X"][:, T * c + TQ * tqc + start: T * c + TQ * tqc + start + w]


def mm_chain(ctx, em, psum_view, lhsT_tiles, rhs_views, after=(), tag="mm",
             extra_first=None):
    def fn(t):
        inst = None
        if extra_first is not None:
            inst = extra_first(t)
        n = len(lhsT_tiles)
        for i, (lt, rv) in enumerate(zip(lhsT_tiles, rhs_views)):
            inst = t.matmul(psum_view, lt, rv,
                            start=(i == 0 and extra_first is None),
                            stop=(i == n - 1))
        return inst
    return em.op("tensor", fn, after=after, tag=tag)


def emit_weight_load_natural(ctx, em, wname, sbname, after=(), sub=None):
    src = ctx.dram[wname] if sub is None else sub
    rows, cols = src.shape
    dst = ctx.t[sbname]
    rp = min(128, rows)
    nro = max(rows // 128, 1)
    ap = bass.AP(src.tensor, src.offset, [[cols, rp], [rp * cols, nro], [1, cols]])
    return em.dma("gpsimd", dst[0:rp, 0:nro * cols],
                  ap, after=after, tag=f"wload:{wname}")


def emit_weight_load_transposed(ctx, em, wname, sbname, rows, cols, after=(),
                                psb_after=None):
    raw = ctx.t[sbname + "_raw"]
    dst = ctx.t[sbname]
    nro = max(rows // 128, 1)
    rp = min(rows, 128)
    nci = cols // 128
    src_ap = ctx.dram[wname]
    halves = 2 if nci > 4 else 1
    cph = cols // halves
    ncih = nci // halves
    evs = []
    for hf in range(halves):
        ap = bass.AP(src_ap.tensor, cph * hf,
                     [[cols, rp], [rp * cols, nro], [1, cph]])
        dep = list(after) + (evs[-1:] if evs else [])
        ld = em.dma("gpsimd", raw[0:rp, 0:nro * cph], ap, after=dep,
                    tag=f"wload:{wname}:{hf}")
        for cl in range(ncih):
            ci = hf * ncih + cl
            trs = []
            for ro in range(nro):
                def tfn(t, cl=cl, ro=ro):
                    return t.transpose(
                        ctx.t["PSB"][:, rp * ro:rp * ro + rp],
                        raw[0:rp, cph * ro + 128 * cl: cph * ro + 128 * cl + 128],
                        ctx.t["EYEB"][0:rp, 0:rp])
                dep = [ld] + (evs[-1:] if evs else ([psb_after] if psb_after else []))
                trs.append(em.op("tensor", tfn, after=dep, tag="wtr"))
            def efn(v, ci=ci, nro=nro, rp=rp):
                return v.tensor_copy(dst[:, rp * nro * ci: rp * nro * (ci + 1)],
                                     ctx.t["PSB"][:, 0:rp * nro])
            evs.append(em.op("vector", efn, after=[trs[-1]], tag="wtr_ev"))
    return evs[-1]


def emit_ln_update(ctx, em, tqc, pre, deps, sc, src_bf16=False):
    stat_ops = []
    mean_last = None
    msq_last = None
    for c in range(C):
        psrc = pre(c)
        if src_bf16:
            bsrc = psrc
            cp = None
        else:
            cp = em.op("scalar", lambda s, psrc=psrc: s.copy(
                sc["PREBC"][:, 0:TQ], psrc), after=deps, tag="ln_cp")
            bsrc = sc["PREBC"][:, 0:TQ]
        sqo = em.op("scalar", lambda s, c=c, bsrc=bsrc: s.activation(
            sc["SQBC"][:, 0:TQ], bsrc, AF.Square), after=[cp] if cp else deps,
            tag="ln_sq")
        mean_last = em.op("tensor", lambda t, c=c, bsrc=bsrc: t.matmul(
            bk(ctx, 4, p=1), ctx.t["MEANCB"][:], bsrc,
            start=(c == 0), stop=(c == C - 1)),
            after=[cp if cp else None, sqo] + (deps if c == 0 else []),
            tag="ln_mean")
        msq_last = em.op("tensor", lambda t, c=c: t.matmul(
            bk(ctx, 5, p=1), ctx.t["MEANCB"][:], sc["SQBC"][:, 0:TQ],
            start=(c == 0), stop=(c == C - 1)), after=[sqo], tag="ln_msq")
        deps = deps + [msq_last] if src_bf16 else [mean_last, msq_last]
    r_m2 = em.op("scalar", lambda s: s.activation(
        sc["ROWA"][0:1, 0:TQ], bk(ctx, 4, p=1), AF.Square),
        after=[mean_last], tag="ln_m2")
    r_var = em.op("vector", lambda v: v.scalar_tensor_tensor(
        sc["ROWB"][0:1, 0:TQ], bk(ctx, 5, p=1), 1e-5, sc["ROWA"][0:1, 0:TQ],
        ALU.add, ALU.subtract), after=[msq_last, r_m2], tag="ln_var")
    r_lv = em.op("scalar", lambda s: s.activation(
        sc["ROWA"][0:1, 0:TQ], sc["ROWB"][0:1, 0:TQ], AF.Ln),
        after=[r_var], tag="ln_lv")
    r_rstd = em.op("scalar", lambda s: s.activation(
        sc["RSTDB"][0:1, 0:TQ], sc["ROWA"][0:1, 0:TQ], AF.Exp, scale=-0.5),
        after=[r_lv], tag="ln_rstd")
    r_mr = em.op("vector", lambda v: v.tensor_tensor(
        sc["MRB"][0:1, 0:TQ], bk(ctx, 4, p=1), sc["RSTDB"][0:1, 0:TQ], ALU.mult),
        after=[mean_last, r_rstd], tag="ln_mr")
    b_r = em.op("tensor", lambda t: t.matmul(
        bk(ctx, 2), ctx.t["ONESR"][:], sc["RSTDB"][0:1, 0:TQ], start=True, stop=True),
        after=[r_rstd], tag="ln_bcr")
    b_mr = em.op("tensor", lambda t: t.matmul(
        bk(ctx, 3), ctx.t["ONESR"][:], sc["MRB"][0:1, 0:TQ], start=True, stop=True),
        after=[r_mr], tag="ln_bcmr")
    outs = []
    for c in range(C):
        psrc = pre(c)
        o1 = em.op("vector", lambda v, psrc=psrc: v.tensor_tensor(
            sc["APPA"][:, 0:TQ], psrc, bk(ctx, 2), ALU.mult),
            after=[b_r] + (outs[-1:] if outs else []), tag="ln_ap1")
        o2 = em.op("vector", lambda v, c=c: v.tensor_tensor(
            xsl(ctx, c, tqc), sc["APPA"][:, 0:TQ], bk(ctx, 3), ALU.subtract),
            after=[o1, b_mr], tag="ln_ap2")
        outs.append(o2)
    return outs



def emit_load_constants_and_x(ctx):
    nc = ctx.nc
    lst = ExitStack()
    for i in range(2):
        ctx.sb(f"XT{i}", [128, D], F32, lst)
        ctx.sb(f"XB{i}", [128, D], BF16, lst)
    em = Em(ctx, "init")
    lds = []
    for cname in CONST_SHAPES:
        lds.append(em.dma("sync", ctx.t[cname][:], ctx.dram[cname],
                          tag=f"c:{cname}"))
    xt = [ctx.t[f"XT{i}"] for i in range(2)]
    xb = [ctx.t[f"XB{i}"] for i in range(2)]
    prev_tr = [None, None]
    prev_ev = None
    eyeld = lds[0]
    for tt in range(NT):
        b = tt % 2
        ld = em.dma("sync", xt[b][:], ctx.dram["xin"][128 * tt:128 * (tt + 1), :],
                    after=[prev_tr[b]], tag=f"xld{tt}")
        cast = em.op("vector", lambda v, b=b: v.tensor_copy(xb[b][:], xt[b][:]),
                     after=[ld, prev_tr[b]], tag="xcast")
        trs = []
        for c in range(C):
            def tfn(t, b=b, c=c):
                return t.transpose(ctx.t["PSB"][:, 128 * c:128 * (c + 1)],
                                   xb[b][:, 128 * c:128 * (c + 1)],
                                   ctx.t["EYEB"][:])
            trs.append(em.op("tensor", tfn,
                             after=[cast, eyeld, prev_ev], tag="xtr"))
        def efn(v, tt=tt):
            ap = bass.AP(ctx.t["X"], 128 * tt, [[C * T, 128], [T, 8], [1, 128]])
            return v.tensor_copy(ap, ctx.t["PSB"][:, 0:1024])
        ev = em.op("vector", efn, after=[trs[-1]], tag="xev")
        prev_ev = ev
        prev_tr[b] = trs[-1]
    em.emit()
    lst.close()


def emit_store_x(ctx):
    nc = ctx.nc
    lst = ExitStack()
    for i in range(2):
        ctx.sb(f"STG{i}", [128, D], F32, lst)
    em = Em(ctx, "store")
    stg = [ctx.t["STG0"], ctx.t["STG1"]]
    prev_dma = [None, None]
    prev_ev = None
    last = None
    for tt in range(NT):
        b = tt % 2
        trs = []
        for c in range(C):
            def tfn(t, tt=tt, c=c):
                return t.transpose(
                    ctx.t["PSB"][:, 128 * c:128 * (c + 1)],
                    ctx.t["X"][:, T * c + 128 * tt: T * c + 128 * (tt + 1)],
                    ctx.t["EYEB"][:])
            trs.append(em.op("tensor", tfn, after=[prev_ev], tag="str"))
        ev = em.op("vector",
                   lambda v, b=b: v.tensor_copy(stg[b][:], ctx.t["PSB"][:, 0:1024]),
                   after=[trs[-1], prev_dma[b]], tag="sev")
        dma = em.dma("sync", ctx.dram["xout"][128 * tt:128 * (tt + 1), :],
                     stg[b][:], after=[ev], tag="sdma")
        prev_dma[b] = dma
        prev_ev = ev
        last = dma
    em.op("gpsimd", lambda g: None, after=[last], tag="final_wait")
    em.emit()
    lst.close()


def emit_taps(ctx, em, tname, src_ap, after):
    if tname in ctx.taps:
        return em.dma("sync", ctx.taps[tname], src_ap, after=after, tag=f"tap:{tname}")
    return None




def emit_attn_sweep(ctx, em, sc, nheads, jn_fn, k2_fn, vp_fn, q2_fn, aon_fn,
                    masked, first_deps, tap_pt=False):
    pv_op = None
    last_exp = None
    prev_n1 = prev_n6 = None
    for h in range(nheads):
        jn = jn_fn(h)
        ngrp = jn // 4
        for jb in range(ngrp):
            mm_ops = []
            for j4 in range(4):
                j = 4 * jb + j4
                def sfn(t, h=h, j=j, j4=j4):
                    return t.matmul(bk(ctx, j4), k2_fn(h, j), q2_fn(h),
                                    start=True, stop=True)
                dep = [last_exp]
                if h == 0 and jb == 0:
                    dep = dep + list(first_deps)
                mm_ops.append(em.op("tensor", sfn, after=dep, tag="smm"))
            def efn(s):
                return s.activation(sc["PT"][:, 0:2048], bkspan(ctx, 0, 4), AF.Exp)
            eop = em.op("scalar", efn, after=mm_ops + [pv_op], tag="exp")
            last_exp = eop
            consume = eop
            if masked and jb == ngrp - 1:
                consume = em.op("vector", lambda v: v.tensor_tensor(
                    sc["PT"][:, 0:2048], sc["PT"][:, 0:2048],
                    ctx.t["CMASK"][:, 0:2048], ALU.mult), after=[eop], tag="cmask")
                if tap_pt and h == 0 and "PT0" in ctx.taps:
                    tp = emit_taps(ctx, em, "PT0", sc["PT"][:, 0:2048],
                                   [consume])
                    consume = em.op("vector", lambda v: None, after=[tp, consume],
                                    tag="pt_tap_gate")
            for j4 in range(4):
                j = 4 * jb + j4
                def pvfn(t, h=h, j=j, j4=j4, jn=jn):
                    return t.matmul(bk(ctx, 6, p=65), vp_fn(h, j),
                                    sc["PT"][:, TQ * j4:TQ * (j4 + 1)],
                                    start=(j == 0), stop=(j == jn - 1))
                dep = [consume]
                if j == 0:
                    dep += [prev_n1, prev_n6]
                pv_op = em.op("tensor", pvfn, after=dep, tag="pv")
        n1 = em.op("vector", lambda v: v.tensor_scalar(
            sc["SROW"][0:1, :], bk(ctx, 6, p0=64, p=1), 1e-30, None,
            ALU.max), after=[pv_op], tag="n1")
        n2 = em.op("scalar", lambda s: s.activation(sc["SROW2"][0:1, :],
                                                    sc["SROW"][0:1, :], AF.Ln),
                   after=[n1], tag="n2")
        n3 = em.op("scalar", lambda s: s.activation(sc["RROW"][0:1, :],
                                                    sc["SROW2"][0:1, :],
                                                    AF.Exp, scale=-1.0),
                   after=[n2], tag="n3")
        n4 = em.op("tensor", lambda t: t.matmul(
            bk(ctx, 5, p=64), ctx.t["ONESR"][0:1, 0:64], sc["RROW"][0:1, :],
            start=True, stop=True), after=[n3], tag="n4")
        n5 = em.op("scalar", lambda s: s.copy(sc["RBC"][:], bk(ctx, 5, p=64)),
                   after=[n4], tag="n5")
        def n6fn(v, h=h):
            return v.tensor_tensor(aon_fn(h), bk(ctx, 6, p0=0, p=64),
                                   sc["RBC"][:], ALU.mult)
        n6 = em.op("vector", n6fn, after=[n5, pv_op, n1], tag="n6")
        prev_n1, prev_n6 = n1, n6
    return prev_n1, prev_n6, last_exp


def emit_rms_chunkstats(ctx, em, sc, src_fn, eps, deps):
    msq_last = None
    dd = list(deps)
    for c in range(C):
        src_ap = src_fn(c)
        sqo = em.op("scalar", lambda s, src_ap=src_ap: s.activation(
            sc["SQBC"][:, 0:TQ], src_ap, AF.Square), after=dd, tag="rms_sq")
        msq_last = em.op("tensor", lambda t, c=c: t.matmul(
            bk(ctx, 4, p=1), ctx.t["MEANCB"][:], sc["SQBC"][:, 0:TQ],
            start=(c == 0), stop=(c == C - 1)),
            after=[sqo] + (list(deps) if c == 0 else []), tag="rms_mm")
        dd = [msq_last]
    r_eps = em.op("vector", lambda v: v.tensor_scalar(
        sc["ROWB"][0:1, 0:TQ], bk(ctx, 4, p=1), eps, None, ALU.add),
        after=[msq_last], tag="rms_eps")
    r_ln = em.op("scalar", lambda s: s.activation(
        sc["ROWA"][0:1, 0:TQ], sc["ROWB"][0:1, 0:TQ], AF.Ln),
        after=[r_eps], tag="rms_ln")
    r_rstd = em.op("scalar", lambda s: s.activation(
        sc["RSTDB"][0:1, 0:TQ], sc["ROWA"][0:1, 0:TQ], AF.Exp, scale=-0.5),
        after=[r_ln], tag="rms_rstd")
    return r_rstd


def emit_cnorm(ctx, em, sc, q2r0, q2r1, kc0, kc1, cdst0, cdst1, deps,
               prev_crows):
    sq0 = em.op("scalar", lambda s: s.activation(
        sc["SQS"][0:64, 0:TQ], q2r0, AF.Square), after=deps, tag="cn_sq0")
    sq1 = em.op("scalar", lambda s: s.activation(
        sc["SQS"][64:128, 0:TQ], q2r1, AF.Square), after=deps, tag="cn_sq1")
    nmm = em.op("tensor", lambda t: t.matmul(
        bk(ctx, 4), ctx.t["PAT2B"][:], sc["SQS"][:, 0:TQ],
        start=True, stop=True), after=[sq0, sq1] + list(prev_crows),
        tag="cn_mm")
    s0r = em.op("scalar", lambda s: s.activation(
        sc["SNQ0"][0:1, :], bk(ctx, 4, p=1), AF.Sqrt),
        after=[nmm, prev_crows[0]], tag="cn_s0")
    s1r = em.op("scalar", lambda s: s.activation(
        sc["SNQ1"][0:1, :], bk(ctx, 4, p0=64, p=1), AF.Sqrt),
        after=[nmm, prev_crows[1]], tag="cn_s1")
    c0 = em.op("vector", lambda v: v.tensor_scalar(
        cdst0, sc["SNQ0"][0:1, :], kc0, SHIFT, ALU.mult, ALU.add),
        after=[s0r], tag="cn_c0")
    c1 = em.op("vector", lambda v: v.tensor_scalar(
        cdst1, sc["SNQ1"][0:1, :], kc1, SHIFT, ALU.mult, ALU.add),
        after=[s1r], tag="cn_c1")
    return c0, c1



A_PHASES = "012"


def emit_block_a(ctx):
    nc = ctx.nc
    st = ExitStack()
    ctx.sb("WRAW", [128, C * D // 2], BF16, st)
    ctx.t["WQT_raw"] = ctx.t["WRAW"]
    ctx.t["WKT_raw"] = ctx.t["WRAW"]
    ctx.t["WVT_raw"] = ctx.t["WRAW"]
    ctx.t["WOT_raw"] = ctx.t["WRAW"]
    ctx.sb("WQT", [128, C * D], BF16, st)
    ctx.sb("WKT", [128, C * 64], BF16, st)
    ctx.sb("WVT", [128, C * 64], BF16, st)
    ctx.sb("WOT", [128, C * D], BF16, st)
    ctx.sb("K2", [65, T], BF16, st)
    ctx.sb("VP", [128, NT * 65], BF16, st)
    ctx.sb("VTMP", [64, TQ], BF16, st)
    ctx.sb("KMAX", [1, 1], F32, st)
    ctx.sb("KROW", [1, TQ], F32, st)
    ctx.sb("S0SB", [1, 1], F32, st)
    ctx.sb("S0COL", [128, 1], F32, st)
    ctx.sb("KCSN", [1, 1], F32, st)
    ctx.sb("Q2", [65, 16 * TQ], BF16, st)
    ctx.sb("SQS", [128, TQ], F32, st)
    ctx.sb("SNQ0", [1, TQ], F32, st)
    ctx.sb("SNQ1", [1, TQ], F32, st)
    ctx.sb("PT", [128, 2048], BF16, st)
    ctx.sb("SROW", [1, TQ], F32, st)
    ctx.sb("SROW2", [1, TQ], F32, st)
    ctx.sb("RROW", [1, TQ], BF16, st)
    ctx.sb("RBC", [64, TQ], F32, st)
    ctx.sb("AON", [128, C * TQ], BF16, st)
    ctx.sb("PREF", [128, C * TQ], F32, st)

    sc = ctx.t

    em = Em(ctx, "A0")
    wq = emit_weight_load_transposed(ctx, em, "L0_attn_w_q", "WQT", D, D)
    wk = emit_weight_load_transposed(ctx, em, "L0_attn_w_k", "WKT", 64, D,
                                     after=[wq], psb_after=wq)
    wv = emit_weight_load_transposed(ctx, em, "L0_attn_w_v", "WVT", 64, D,
                                     after=[wk], psb_after=wk)
    wo = emit_weight_load_transposed(ctx, em, "L0_attn_w_out", "WOT", D, D,
                                     after=[wv], psb_after=wv)
    s0ld = em.dma("sync", sc["S0SB"][:],
                  bass.AP(ctx.dram["L0_attn_softmax_scale"].tensor, 0,
                          [[1, 1], [1, 1]]), tag="s0")
    s0mm = em.op("tensor", lambda t: t.matmul(bk(ctx, 6, w=1), ctx.t["ONESRF"][:],
                                              sc["S0SB"][:], start=True, stop=True),
                 after=[s0ld], tag="s0mm")
    s0ev = em.op("vector", lambda v: v.tensor_copy(sc["S0COL"][:], bk(ctx, 6, w=1)),
                 after=[s0mm], tag="s0ev")
    ones_vp = em.op("gpsimd", lambda g: g.memset(sc["VP"][:], 1.0), tag="vp1")
    ones_k2 = em.op("gpsimd", lambda g: g.memset(sc["K2"][64:65, :], 1.0), tag="k21")
    em.emit()

    if "1" not in A_PHASES:
        st.close()
        return
    em = Em(ctx, "A1")
    kmax_ops = []
    km0 = em.op("gpsimd", lambda g: g.memset(sc["KMAX"][:], 0.0), tag="km0")
    prev = {}
    for tqc in range(NTQ):
        kmm = mm_chain(ctx, em, bk(ctx, 0, p=64),
                       [sc["WKT"][:, 64 * ci:64 * (ci + 1)] for ci in range(C)],
                       [xsl(ctx, ci, tqc) for ci in range(C)],
                       after=[prev.get("kev"), prev.get("ksq")], tag="kmm")
        kev = em.op("scalar", lambda s, tqc=tqc: s.copy(
            sc["K2"][0:64, TQ * tqc:TQ * (tqc + 1)], bk(ctx, 0, p=64)),
            after=[kmm], tag="kev")
        ksq = em.op("scalar", lambda s: s.activation(
            sc["SQS"][0:64, 0:TQ], bk(ctx, 0, p=64), AF.Square), after=[kmm], tag="ksq")
        knm = em.op("tensor", lambda t: t.matmul(
            bk(ctx, 1, p=1), ctx.t["ONESC"][0:64, :], sc["SQS"][0:64, 0:TQ],
            start=True, stop=True), after=[ksq, prev.get("kred")], tag="knm")
        kred = em.op("vector", lambda v: v.tensor_reduce(
            sc["KROW"][0:1, 0:1], bk(ctx, 1, p=1), AX.X, ALU.max),
            after=[knm, prev.get("kacc")], tag="kred")
        kacc = em.op("vector", lambda v: v.tensor_tensor(
            sc["KMAX"][:], sc["KMAX"][:], sc["KROW"][0:1, 0:1], ALU.max),
            after=[kred, km0], tag="kacc")
        vmm = mm_chain(ctx, em, bk(ctx, 2, p=64),
                       [sc["WVT"][:, 64 * ci:64 * (ci + 1)] for ci in range(C)],
                       [xsl(ctx, ci, tqc) for ci in range(C)],
                       after=[prev.get("vev")], tag="vmm")
        vev = em.op("scalar", lambda s: s.copy(sc["VTMP"][:], bk(ctx, 2, p=64)),
                    after=[vmm], tag="vev")
        vpe = prev.get("vpe")
        for j in range(4):
            def tfn(t, j=j, tqc=tqc):
                return t.transpose(ctx.t["PSB"][:, 64 * j:64 * (j + 1)],
                                   sc["VTMP"][0:64, 128 * j:128 * (j + 1)],
                                   ctx.t["EYEB"][0:64, 0:64])
            vtr = em.op("tensor", tfn, after=[vev, vpe], tag="vtr")
            def efn(v, j=j, tqc=tqc):
                jj = 4 * tqc + j
                return v.tensor_copy(sc["VP"][:, 65 * jj:65 * jj + 64],
                                     ctx.t["PSB"][:, 64 * j:64 * (j + 1)])
            vpe = em.op("vector", efn, after=[vtr, ones_vp], tag="vpe")
        prev = dict(kev=kev, ksq=ksq, kred=kred, vev=vev, vpe=vpe, kacc=kacc)
        kmax_ops.append(kacc)
    kroot = em.op("scalar", lambda s: s.activation(sc["KROW"][0:1, 0:1],
                                                   sc["KMAX"][:], AF.Sqrt),
                  after=[kmax_ops[-1]], tag="kroot")
    kcsn = em.op("vector", lambda v: v.tensor_scalar(
        sc["KCSN"][:], sc["KROW"][0:1, 0:1], sc["S0SB"][:], -1.0,
        ALU.mult, ALU.mult), after=[kroot, s0ev], tag="kcsn")
    em.emit()

    if "2" not in A_PHASES and "q" not in A_PHASES and "a" not in A_PHASES \
            and "f" not in A_PHASES:
        st.close()
        return
    for tqc in range(NTQ):
        em = Em(ctx, f"A2_{tqc}")
        qdeps = []
        bank_readers = {0: [], 1: [], 4: [], 5: []}
        prev_c0 = prev_c1 = None
        for ro in range(C):
            b = ro % 2
            qmm = mm_chain(
                ctx, em, bk(ctx, b),
                [sc["WQT"][:, D * ci + 128 * ro: D * ci + 128 * (ro + 1)]
                 for ci in range(C)],
                [xsl(ctx, ci, tqc) for ci in range(C)],
                after=bank_readers[b], tag=f"qmm{ro}")
            bank_readers[b] = []
            ev0 = em.op("vector", lambda v, ro=ro, b=b: v.tensor_scalar(
                sc["Q2"][0:64, TQ * 2 * ro:TQ * (2 * ro + 1)],
                bk(ctx, b, p0=0, p=64), sc["S0COL"][0:64, :], None, ALU.mult),
                after=[qmm], tag="qev0")
            ev1 = em.op("vector", lambda v, ro=ro, b=b: v.tensor_scalar(
                sc["Q2"][0:64, TQ * (2 * ro + 1):TQ * (2 * ro + 2)],
                bk(ctx, b, p0=64, p=64), sc["S0COL"][64:128, :], None, ALU.mult),
                after=[qmm], tag="qev1")
            if "n" in A_PHASES:
                bank_readers[b] = [ev0, ev1]
                qdeps += [ev0, ev1]
                continue
            sq = em.op("scalar", lambda s, b=b: s.activation(
                sc["SQS"][:], bk(ctx, b), AF.Square), after=[qmm, ev0, ev1],
                tag="qsq")
            bank_readers[b] = [ev0, ev1, sq]
            nb = 4 + b
            if "x" in A_PHASES:
                bank_readers[nb] = []
                qdeps += [ev0, ev1, sq]
                continue
            nmm = em.op("tensor", lambda t, b=b, nb=nb: t.matmul(
                bk(ctx, nb), ctx.t["PAT2F"][:], sc["SQS"][:],
                start=True, stop=True), after=[sq] + bank_readers[nb],
                tag="qnmm")
            if "m" in A_PHASES:
                bank_readers[nb] = []
                qdeps += [ev0, ev1]
                ctx._nmmlast = nmm
                continue
            s0r = em.op("scalar", lambda s, nb=nb: s.activation(
                sc["SNQ0"][0:1, :], bk(ctx, nb, p=1), AF.Sqrt),
                after=[nmm, prev_c0], tag="qsn0")
            s1r = em.op("scalar", lambda s, nb=nb: s.activation(
                sc["SNQ1"][0:1, :], bk(ctx, nb, p0=64, p=1), AF.Sqrt),
                after=[nmm, prev_c1], tag="qsn1")
            bank_readers[nb] = [s0r, s1r]
            if "s" in A_PHASES:
                bank_readers[nb] = [s0r, s1r]
                qdeps += [ev0, ev1, s0r, s1r]
                continue
            c0 = em.op("vector", lambda v, ro=ro: v.tensor_scalar(
                sc["Q2"][64:65, TQ * 2 * ro:TQ * (2 * ro + 1)],
                sc["SNQ0"][0:1, :], sc["KCSN"][:], SHIFT, ALU.mult, ALU.add),
                after=[s0r], tag="qc0")
            c1 = em.op("vector", lambda v, ro=ro: v.tensor_scalar(
                sc["Q2"][64:65, TQ * (2 * ro + 1):TQ * (2 * ro + 2)],
                sc["SNQ1"][0:1, :], sc["KCSN"][:], SHIFT, ALU.mult, ALU.add),
                after=[s1r], tag="qc1")
            qdeps += [ev0, ev1, c0, c1]
            prev_c0, prev_c1 = c0, c1
        qdeps += bank_readers[0] + bank_readers[1] + bank_readers[4] + bank_readers[5]

        if "q" in A_PHASES and "a" not in A_PHASES and "f" not in A_PHASES:
            em.emit()
            continue
        pv_op = None
        last_exp = None
        prev_n1 = None
        prev_n6 = None
        for h in range(16):
            first_mm_dep = qdeps + [prev_n1, prev_n6]
            for jb in range(8):
                mm_ops = []
                for j4 in range(4):
                    j = 4 * jb + j4
                    def sfn(t, j=j, j4=j4, h=h):
                        return t.matmul(
                            bk(ctx, j4),
                            sc["K2"][:, 128 * j:128 * (j + 1)],
                            sc["Q2"][:, TQ * h:TQ * (h + 1)],
                            start=True, stop=True)
                    dep = [last_exp]
                    if h == 0 and jb == 0:
                        dep = dep + first_mm_dep
                    mm_ops.append(em.op("tensor", sfn, after=dep, tag="smm"))
                pb = jb % 2
                def efn(s, pb=pb):
                    return s.activation(sc["PT"][:, 0:2048],
                                        bkspan(ctx, 0, 4), AF.Exp)
                eop = em.op("scalar", efn, after=mm_ops + [pv_op], tag="exp")
                last_exp = eop
                for j4 in range(4):
                    j = 4 * jb + j4
                    def pvfn(t, j=j, j4=j4, pb=pb):
                        return t.matmul(
                            bk(ctx, 6, p=65),
                            sc["VP"][:, 65 * j:65 * (j + 1)],
                            sc["PT"][:, TQ * j4:TQ * (j4 + 1)],
                            start=(j == 0), stop=(j == NT - 1))
                    dep = [eop]
                    if j == 0:
                        dep += [prev_n1, prev_n6]
                    pv_op = em.op("tensor", pvfn, after=dep, tag="pv")
            n1 = em.op("vector", lambda v: v.tensor_scalar(
                sc["SROW"][0:1, :], bk(ctx, 6, p0=64, p=1), 1e-30, None,
                ALU.max), after=[pv_op], tag="n1")
            n2 = em.op("scalar", lambda s: s.activation(sc["SROW2"][0:1, :],
                                                        sc["SROW"][0:1, :], AF.Ln),
                       after=[n1], tag="n2")
            n3 = em.op("scalar", lambda s: s.activation(sc["RROW"][0:1, :],
                                                        sc["SROW2"][0:1, :],
                                                        AF.Exp, scale=-1.0),
                       after=[n2], tag="n3")
            n4 = em.op("tensor", lambda t: t.matmul(
                bk(ctx, 5, p=64), ctx.t["ONESR"][0:1, 0:64], sc["RROW"][0:1, :],
                start=True, stop=True), after=[n3], tag="n4")
            n5 = em.op("scalar", lambda s: s.copy(sc["RBC"][:], bk(ctx, 5, p=64)),
                       after=[n4], tag="n5")
            def n6fn(v, h=h):
                ro, half = h // 2, h % 2
                return v.tensor_tensor(
                    sc["AON"][64 * half:64 * half + 64, TQ * ro:TQ * (ro + 1)],
                    bk(ctx, 6, p0=0, p=64), sc["RBC"][:], ALU.mult)
            n6 = em.op("vector", n6fn, after=[n5, pv_op], tag="n6")
            prev_n1, prev_n6 = n1, n6

        if "a" in A_PHASES and "f" not in A_PHASES:
            em.emit()
            continue
        pre_deps = []
        for co in range(C):
            b = co % 2
            dep = [prev_n6, last_exp]
            if len(pre_deps) >= 2:
                dep.append(pre_deps[-2])
            omm = mm_chain(
                ctx, em, bk(ctx, b),
                [sc["WOT"][:, D * ci + 128 * co: D * ci + 128 * (co + 1)]
                 for ci in range(C)],
                [sc["AON"][:, TQ * ci:TQ * (ci + 1)] for ci in range(C)],
                after=dep, tag=f"omm{co}")
            r1 = em.op("vector", lambda v, co=co, b=b: v.tensor_tensor(
                sc["PREF"][:, TQ * co:TQ * (co + 1)], bk(ctx, b),
                xsl(ctx, co, tqc), ALU.add), after=[omm], tag="resadd")
            pre_deps.append(r1)
        emit_ln_update(ctx, em, tqc,
                       lambda c: sc["PREF"][:, TQ * c:TQ * (c + 1)],
                       pre_deps + [prev_n1], sc)
        em.emit()
    st.close()



def build_program(blocks, core):
    nc = bass.Bass("TRN2", target_bir_lowering=False)
    stack = ExitStack()
    ctx = Ctx(nc, stack)
    ctx.core = core
    ctx.blocks = blocks
    ctx.dram = {}

    ctx.dram["xin"] = nc.dram_tensor("xin", [T, D], F32, kind="ExternalInput").ap()
    ctx.dram["xout"] = nc.dram_tensor("xout", [T, D], F32, kind="ExternalOutput").ap()
    input_names = ["xin"]
    for cname, (shape, dt) in CONST_SHAPES.items():
        ctx.dram[cname] = nc.dram_tensor(cname, shape, dt, kind="ExternalInput").ap()
        input_names.append(cname)
    tabs = []
    if "B" in blocks:
        tabs += ["COS0", "SIN0"]
    if "C" in blocks:
        tabs += ["COS1", "SIN1"]
    if "B" in blocks or "C" in blocks:
        tabs += ["CMASK"]
    for cname in tabs:
        shape, dt = TABLE_SHAPES[cname]
        ctx.dram[cname] = nc.dram_tensor(cname, shape, dt, kind="ExternalInput").ap()
        input_names.append(cname)
    needed = []
    for b in blocks:
        needed += BLOCK_WEIGHTS[b]
    for wname in needed:
        ctx.dram[wname] = nc.dram_tensor(wname, WEIGHTS[wname], F32,
                                         kind="ExternalInput").ap()
        input_names.append(wname)
    ctx.taps = {}
    for tname, shape, dt in DEBUG_TAPS:
        ctx.taps[tname] = nc.dram_tensor(tname, shape, dt, kind="ExternalOutput").ap()
    ctx.dram["rowscratch"] = nc.dram_tensor("rowscratch", [32, 128], F32).ap()
    if "B" in blocks:
        ctx.dram["cc_in"] = nc.dram_tensor("cc_in", [8, 2], F32).ap()
        ctx.dram["cc_out"] = nc.dram_tensor("cc_out", [64, 2], F32).ap()

    ctx.sb("X", [128, C * T], BF16)
    ctx.sb("PREBC", [128, TQ], BF16)
    ctx.sb("SQBC", [128, TQ], BF16)
    ctx.sb("ROWA", [1, TQ], F32)
    ctx.sb("ROWB", [1, TQ], F32)
    ctx.sb("RSTDB", [1, TQ], BF16)
    ctx.sb("MRB", [1, TQ], BF16)
    ctx.sb("APPA", [128, TQ], F32)
    ctx.t["PSUM"] = stack.enter_context(nc.psum_tensor("PSUM", [128, 3584], F32))
    ctx.t["PSB"] = stack.enter_context(nc.psum_tensor("PSB", [128, 1024], BF16))
    for cname, (shape, dt) in CONST_SHAPES.items():
        ctx.sb(cname, shape, dt)
    names = ["Dcc"]
    for e in ENGINES:
        names.append("E" + e)
        for ch in range(DMA_CHANNELS[e]):
            names.append(f"D{e}{ch}")
    ctx.sems = Sems(nc, stack, names)
    ctx.dma_rr = {e: 0 for e in ENGINES}

    emit_load_constants_and_x(ctx)
    for b in blocks:
        if b == "A":
            emit_block_a(ctx)
        elif b == "B":
            emit_block_b(ctx)
        elif b == "C":
            emit_block_c(ctx)
        elif b == "D":
            emit_block_d(ctx)
    emit_store_x(ctx)
    stack.close()
    return nc, input_names


B_PHASES = "all"


def emit_block_b(ctx):
    nc = ctx.nc
    st = ExitStack()
    sc = ctx.t
    ctx.sb("G1C", [128, C], F32, st)
    ctx.sb("G2C", [128, C], F32, st)
    ctx.sb("RW", [128, C * 8], F32, st)
    ctx.sb("RB8", [8, 1], F32, st)
    ctx.sb("SCR8", [128, C * T], BF16, st)

    st0 = ExitStack()
    ctx.sb("GROW", [1, D], F32, st0)
    em = Em(ctx, "B0")
    emit_weight_load_natural(ctx, em, "L0_ffn_router_w", "RW")
    em.dma("sync", sc["RB8"][:], bass.AP(ctx.dram["L0_ffn_router_b"].tensor, 0,
                                         [[1, 8], [1, 1]]), tag="rb")
    gprev = None
    for gname, gsb in [("L0_ffn_g1", "G1C"), ("L0_ffn_g2", "G2C")]:
        gld = em.dma("sync", sc["GROW"][:],
                     bass.AP(ctx.dram[gname].tensor, 0, [[D, 1], [1, D]]),
                     after=[gprev], tag=f"g:{gname}")
        gtr = []
        for c in range(C):
            def tfn(t, c=c):
                return t.transpose(ctx.t["PSUM"][:, c:c + 1],
                                   sc["GROW"][0:1, 128 * c:128 * (c + 1)],
                                   ctx.t["EYEF"][0:1, 0:1])
            gtr.append(em.op("tensor", tfn,
                             after=[gld] + (gtr[-1:] or [gprev]), tag="gtr"))
        gprev = em.op("vector", lambda v, gsb=gsb: v.tensor_copy(
            sc[gsb][:, 0:C], ctx.t["PSUM"][:, 0:C]), after=[gtr[-1]], tag="gev")
    em.op("gpsimd", lambda g: g.memset(sc["SCR8"][:], 0.0), tag="scr0")
    em.emit()
    st0.close()

    stA = ExitStack()
    ctx.sb("COS0S", [128, TQ], BF16, stA)
    ctx.sb("SIN0S", [128, TQ], BF16, stA)
    ctx.sb("CMASK", [128, 2048], BF16, stA)
    ctx.sb("K2G", [65, T], BF16, stA)
    ctx.sb("VPG", [128, 65 * NT], BF16, stA)
    ctx.sb("WQG", [128, C * 256], BF16, stA)
    ctx.sb("WOG", [128, 2 * D], BF16, stA)
    ctx.sb("WKG", [128, C * 64], BF16, stA)
    ctx.sb("WVG", [128, C * 64], BF16, stA)
    ctx.sb("XR1C", [128, C * TQ], BF16, stA)
    ctx.sb("QA", [128, TQ], BF16, stA)
    ctx.sb("QAS", [128, TQ], BF16, stA)
    ctx.sb("RT1", [128, TQ], F32, stA)
    ctx.sb("Q2G", [65, 4 * TQ], BF16, stA)
    ctx.sb("SQS", [128, TQ], BF16, stA)
    ctx.t["SNQ0"] = ctx.t["ROWA"]
    ctx.t["SNQ1"] = ctx.t["ROWB"]
    ctx.t["SROW"] = ctx.t["ROWB"]
    ctx.t["SROW2"] = ctx.t["ROWA"]
    ctx.sb("KM4", [1, 4], F32, stA)
    ctx.sb("KC4", [1, 4], F32, stA)
    ctx.sb("PT", [128, 2048], BF16, stA)
    ctx.sb("RROW", [1, TQ], BF16, stA)
    ctx.sb("RBC", [64, TQ], BF16, stA)
    ctx.sb("AONG", [128, 2 * TQ], BF16, stA)

    em = Em(ctx, "Bpre")
    em.dma("sync", sc["CMASK"][:], ctx.dram["CMASK"], tag="cmask")
    em.op("gpsimd", lambda g: g.memset(sc["K2G"][64:65, :], 1.0), tag="k2g1")
    em.op("gpsimd", lambda g: g.memset(sc["VPG"][:], 1.0), tag="vpg1")
    em.op("gpsimd", lambda g: g.memset(sc["KM4"][:], 0.0), tag="km4")
    em.emit()

    def rope_pair(em, psum_bank, tq0, dst0, dst1, deps, scale, prev, p=128,
                  extra_ev=()):
        ev = em.op("scalar", lambda s: s.activation(
            sc["QA"][0:p, 0:TQ], bk(ctx, psum_bank, p=p), AF.Copy, scale=scale),
            after=deps + [prev.get("t1"), prev.get("rot")] + list(extra_ev),
            tag="rp_ev")
        rot = em.op("tensor", lambda t: t.matmul(
            bk(ctx, 3, p=p), ctx.t["PN"][0:p, 0:p], sc["QA"][0:p, 0:TQ],
            start=True, stop=True), after=[ev, prev.get("t2")], tag="rp_rot")
        t1 = em.op("vector", lambda v: v.tensor_tensor(
            sc["RT1"][0:p, 0:TQ], sc["QA"][0:p, 0:TQ],
            sc["COS0S"][0:p, 0:TQ], ALU.mult),
            after=[ev, prev.get("o0"), prev.get("o1"), prev.get("tbl")],
            tag="rp_t1")
        t2 = em.op("vector", lambda v: v.tensor_tensor(
            sc["QAS"][0:p, 0:TQ], bk(ctx, 3, p=p),
            sc["SIN0S"][0:p, 0:TQ], ALU.mult),
            after=[rot, prev.get("o0"), prev.get("o1"), prev.get("tbl")],
            tag="rp_t2")
        o0 = em.op("vector", lambda v: v.tensor_tensor(
            dst0, sc["RT1"][0:64, 0:TQ], sc["QAS"][0:64, 0:TQ], ALU.add),
            after=[t1, t2], tag="rp_o0")
        o1 = o0
        if p == 128:
            o1 = em.op("vector", lambda v: v.tensor_tensor(
                dst1, sc["RT1"][64:128, 0:TQ], sc["QAS"][64:128, 0:TQ],
                ALU.add), after=[t1, t2], tag="rp_o1")
        return dict(ev=ev, rot=rot, t1=t1, t2=t2, o0=o0, o1=o1)

    wq_t = ctx.dram["L0_ffn_wq"].tensor
    wo_t = ctx.dram["L0_ffn_wo"].tensor
    wk_t = ctx.dram["L0_ffn_wk"].tensor
    wv_t = ctx.dram["L0_ffn_wv"].tensor

    for gp in range(4):
        em = Em(ctx, f"Bw_{gp}")
        for hl in range(4):
            em.dma("gpsimd",
                   bass.AP(sc["WQG"], 64 * hl, [[C * 256, 128], [256, C], [1, 64]]),
                   bass.AP(wq_t, 64 * (gp + 4 * hl),
                           [[D, 128], [128 * D, C], [1, 64]]),
                   tag=f"wqg{hl}")
        for hl in range(4):
            em.dma("gpsimd",
                   sc["WOG"][64 * (hl % 2):64 * (hl % 2) + 64,
                             D * (hl // 2):D * (hl // 2) + D],
                   bass.AP(wo_t, 64 * (gp + 4 * hl) * D,
                           [[D, 64], [1, 1], [1, D]]),
                   tag=f"wog{hl}")
        em.dma("gpsimd", sc["WKG"][:, 0:C * 64],
               bass.AP(wk_t, 64 * gp, [[256, 128], [128 * 256, C], [1, 64]]),
               tag="wkg")
        em.dma("gpsimd", sc["WVG"][:, 0:C * 64],
               bass.AP(wv_t, 64 * gp, [[256, 128], [128 * 256, C], [1, 64]]),
               tag="wvg")
        em.emit()

        em = Em(ctx, f"Bkv_{gp}")
        prevb = {}
        rp = {}
        kn_prev = None
        for tqc in range(NTQ):
            tq0 = TQ * tqc
            tl0 = em.dma("sync", sc["COS0S"][:],
                         ctx.dram["COS0"][:, tq0:tq0 + TQ],
                         after=[rp.get("t1")], tag="tbl_c")
            tl1 = em.dma("sync", sc["SIN0S"][:],
                         ctx.dram["SIN0"][:, tq0:tq0 + TQ],
                         after=[rp.get("t2")], tag="tbl_s")
            tbl = em.op("sync", lambda e: None, after=[tl0, tl1], tag="tblw")
            rp["tbl"] = tbl
            rstd = emit_rms_chunkstats(
                ctx, em, sc, lambda c: xsl(ctx, c, tqc), 1e-6,
                [prevb.get("xr_last")])
            bcm = em.op("tensor", lambda t: t.matmul(
                bk(ctx, 2), ctx.t["ONESR"][:], sc["RSTDB"][0:1, 0:TQ],
                start=True, stop=True), after=[rstd, prevb.get("xr_last")],
                tag="b1_bc")
            xrs = []
            for c in range(C):
                xin = xsl(ctx, c, tqc)
                xrs.append(em.op("vector", lambda v, c=c, xin=xin: v.scalar_tensor_tensor(
                    sc["XR1C"][:, TQ * c:TQ * (c + 1)], xin,
                    sc["G1C"][:, c:c + 1], bk(ctx, 2), ALU.mult, ALU.mult),
                    after=[bcm] + xrs[-1:], tag="b1_xr"))
            if gp == 0 and tqc == 0 and "XR1C" in ctx.taps:
                tp1 = emit_taps(ctx, em, "XR1C", sc["XR1C"][:, 0:C * TQ], xrs)
                tp2 = emit_taps(ctx, em, "G1CT", sc["G1C"][:, 0:C], xrs)
                tp3 = emit_taps(ctx, em, "WKGT", sc["WKG"][:, 0:C * 64], xrs)
                tp4 = emit_taps(ctx, em, "RSTDT", sc["RSTDB"][0:1, 0:TQ], xrs)
                tp5 = emit_taps(ctx, em, "XTAP", ctx.t["X"][:, 0:TQ], xrs)
                xrs = xrs + [em.op("vector", lambda v: None,
                                   after=[tp1, tp2, tp3, tp4, tp5, xrs[-1]],
                                   tag="tapgate")]
            kmm = mm_chain(ctx, em, bk(ctx, 0, p=64),
                           [sc["WKG"][:, 64 * ci:64 * (ci + 1)]
                            for ci in range(C)],
                           [sc["XR1C"][:, TQ * ci:TQ * (ci + 1)]
                            for ci in range(C)],
                           after=xrs + [rp.get("ev")], tag="kmm")
            rp["tbl"] = tbl
            rp = rope_pair(em, 0, tq0,
                           sc["K2G"][0:64, tq0:tq0 + TQ], None,
                           [kmm], 1.0, rp, p=64,
                           extra_ev=[prevb.get("vpe")])
            qq = em.op("scalar", lambda s, tq0=tq0: s.activation(
                sc["SQS"][0:64, 0:TQ], sc["K2G"][0:64, tq0:tq0 + TQ],
                AF.Square), after=[rp["o0"], kn_prev], tag="kn_sq")
            nm = em.op("tensor", lambda t: t.matmul(
                bk(ctx, 5, p=1), ctx.t["ONESCB"][0:64, :],
                sc["SQS"][0:64, 0:TQ], start=True, stop=True),
                after=[qq, kn_prev], tag="kn_mm")
            rd = em.op("vector", lambda v: v.tensor_reduce(
                sc["SNQ0"][0:1, 0:1], bk(ctx, 5, p=1), AX.X, ALU.max),
                after=[nm, kn_prev], tag="kn_rd")
            ka = em.op("vector", lambda v, gp=gp: v.tensor_tensor(
                sc["KM4"][0:1, gp:gp + 1], sc["KM4"][0:1, gp:gp + 1],
                sc["SNQ0"][0:1, 0:1], ALU.max), after=[rd], tag="kn_acc")
            kn_prev = ka
            vmm = mm_chain(ctx, em, bk(ctx, 1, p=64),
                           [sc["WVG"][:, 64 * ci:64 * (ci + 1)]
                            for ci in range(C)],
                           [sc["XR1C"][:, TQ * ci:TQ * (ci + 1)]
                            for ci in range(C)],
                           after=xrs + [prevb.get("vev")], tag="vmm")
            vev = em.op("scalar", lambda s: s.copy(sc["QA"][0:64, 0:TQ],
                                                   bk(ctx, 1, p=64)),
                        after=[vmm, rp.get("t1"), rp.get("rot"),
                               prevb.get("vpe")], tag="vev")
            prevb["vev"] = vev
            vpe = prevb.get("vpe")
            for j in range(4):
                def tfn(t, j=j):
                    return t.transpose(
                        ctx.t["PSB"][:, 64 * j:64 * (j + 1)],
                        sc["QA"][0:64, 128 * j:128 * (j + 1)],
                        ctx.t["EYEB"][0:64, 0:64])
                vtr = em.op("tensor", tfn, after=[vev, vpe], tag="vtr")
                def efn(v, j=j, tqc=tqc):
                    jj = 4 * tqc + j
                    return v.tensor_copy(
                        sc["VPG"][:, 65 * jj:65 * jj + 64],
                        ctx.t["PSB"][:, 64 * j:64 * (j + 1)])
                vpe = em.op("vector", efn, after=[vtr], tag="vpe")
            prevb["vpe"] = vpe
            prevb["xr_last"] = em.op("vector", lambda v: None,
                                     after=[kmm, vmm, xrs[-1]], tag="xr_gate")
        kq = em.op("scalar", lambda s, gp=gp: s.activation(
            sc["KC4"][0:1, gp:gp + 1], sc["KM4"][0:1, gp:gp + 1], AF.Sqrt),
            after=[kn_prev], tag="kq")
        em.op("vector", lambda v, gp=gp: v.tensor_scalar(
            sc["KC4"][0:1, gp:gp + 1], sc["KC4"][0:1, gp:gp + 1], -1.0, None,
            ALU.mult), after=[kq], tag="kneg")
        if B_PHASES == "kv" and "K2GD" in ctx.taps:
            emit_taps(ctx, em, "K2GD", sc["K2G"][0:65, 0:T], [kq, prevb["vpe"]])
            emit_taps(ctx, em, "VPGD", sc["VPG"][:, 0:65 * NT], [kq, prevb["vpe"]])
        em.emit()
        if B_PHASES == "kv":
            stA.close()
            st.close()
            return

        for tqc in range(NTQ):
            tq0 = TQ * tqc
            em = Em(ctx, f"Bq_{gp}_{tqc}")
            tl0 = em.dma("sync", sc["COS0S"][:],
                         ctx.dram["COS0"][:, tq0:tq0 + TQ], tag="tbl_c")
            tl1 = em.dma("sync", sc["SIN0S"][:],
                         ctx.dram["SIN0"][:, tq0:tq0 + TQ], tag="tbl_s")
            tbl = em.op("sync", lambda e: None, after=[tl0, tl1], tag="tblw")
            rstd = emit_rms_chunkstats(ctx, em, sc,
                                       lambda c: xsl(ctx, c, tqc), 1e-6, [])
            bcm = em.op("tensor", lambda t: t.matmul(
                bk(ctx, 2), ctx.t["ONESR"][:], sc["RSTDB"][0:1, 0:TQ],
                start=True, stop=True), after=[rstd], tag="b2_bc")
            xrs = []
            for c in range(C):
                xrs.append(em.op("vector", lambda v, c=c: v.scalar_tensor_tensor(
                    sc["XR1C"][:, TQ * c:TQ * (c + 1)], xsl(ctx, c, tqc),
                    sc["G1C"][:, c:c + 1], bk(ctx, 2), ALU.mult, ALU.mult),
                    after=[bcm] + xrs[-1:], tag="b2_xr"))
            rp2 = {"tbl": tbl}
            crows = [None, None]
            qdeps = []
            prev_rd = []
            for ro2 in range(2):
                qmm = mm_chain(ctx, em, bk(ctx, 0),
                               [sc["WQG"][:, 256 * ci + 128 * ro2:
                                         256 * ci + 128 * (ro2 + 1)]
                                for ci in range(C)],
                               [sc["XR1C"][:, TQ * ci:TQ * (ci + 1)]
                                for ci in range(C)],
                               after=xrs + prev_rd + [rp2.get("ev")],
                               tag=f"b2_qmm{ro2}")
                hl0, hl1 = 2 * ro2, 2 * ro2 + 1
                rp2 = rope_pair(em, 0, tq0,
                                sc["Q2G"][0:64, TQ * hl0:TQ * hl0 + TQ],
                                sc["Q2G"][0:64, TQ * hl1:TQ * hl1 + TQ],
                                [qmm], 0.125, rp2)
                prev_rd = [rp2["ev"]]
                c0, c1 = emit_cnorm(
                    ctx, em, sc,
                    sc["Q2G"][0:64, TQ * hl0:TQ * hl0 + TQ],
                    sc["Q2G"][0:64, TQ * hl1:TQ * hl1 + TQ],
                    sc["KC4"][0:1, gp:gp + 1], sc["KC4"][0:1, gp:gp + 1],
                    sc["Q2G"][64:65, TQ * hl0:TQ * hl0 + TQ],
                    sc["Q2G"][64:65, TQ * hl1:TQ * hl1 + TQ],
                    [rp2["o0"], rp2["o1"]], crows)
                crows = [c0, c1]
                qdeps += [rp2["o0"], rp2["o1"], c0, c1]
            jn = 4 * tqc + 4
            pn1, pn6, lexp = emit_attn_sweep(
                ctx, em, sc, 4, lambda h: jn,
                lambda h, j: sc["K2G"][:, 128 * j:128 * (j + 1)],
                lambda h, j: sc["VPG"][:, 65 * j:65 * (j + 1)],
                lambda h: sc["Q2G"][:, TQ * h:TQ * (h + 1)],
                lambda h: sc["AONG"][64 * (h % 2):64 * (h % 2) + 64,
                                     TQ * (h // 2):TQ * (h // 2) + TQ],
                True, qdeps, tap_pt=(gp == 0 and tqc == 0))
            if gp == 0 and tqc == 0 and "AONG0" in ctx.taps:
                emit_taps(ctx, em, "AONG0", sc["AONG"][:, 0:2 * TQ], [pn6])
                if "VPG0" in ctx.taps:
                    emit_taps(ctx, em, "VPG0", sc["VPG"][:, 0:130], [pn6])
                if "K2G0" in ctx.taps:
                    emit_taps(ctx, em, "K2G0", sc["K2G"][0:65, 0:512], [pn6])
            pre_prev = []
            for co in range(C):
                b = co % 2
                dep = [pn6, lexp] + pre_prev[-2:-1]
                omm = mm_chain(ctx, em, bk(ctx, b),
                               [sc["WOG"][:, D * r + 128 * co:
                                         D * r + 128 * (co + 1)]
                                for r in range(2)],
                               [sc["AONG"][:, TQ * r:TQ * (r + 1)]
                                for r in range(2)],
                               after=dep, tag=f"b2_omm{co}")
                r1 = em.op("vector", lambda v, co=co, b=b, tq0=tq0: v.tensor_tensor(
                    sc["SCR8"][:, T * co + tq0:T * co + tq0 + TQ],
                    sc["SCR8"][:, T * co + tq0:T * co + tq0 + TQ],
                    bk(ctx, b), ALU.add), after=[omm], tag="b2_acc")
                pre_prev.append(r1)
            em.emit()
    stA.close()

    em = Em(ctx, "B3")
    pr = []
    for tqc in range(NTQ):
        for co in range(C):
            tq0 = TQ * tqc
            r1 = em.op("vector", lambda v, co=co, tq0=tq0: v.tensor_tensor(
                sc["SCR8"][:, T * co + tq0:T * co + tq0 + TQ],
                sc["SCR8"][:, T * co + tq0:T * co + tq0 + TQ],
                xsl(ctx, co, tq0 // TQ), ALU.add), after=pr[-1:], tag="b3_h")
            r2 = em.op("vector", lambda v, co=co, tq0=tq0: v.tensor_tensor(
                xsl(ctx, co, tq0 // TQ), xsl(ctx, co, tq0 // TQ),
                sc["SCR8"][:, T * co + tq0:T * co + tq0 + TQ], ALU.add),
                after=[r1], tag="b3_s")
            pr.append(r2)
    em.emit()

    if B_PHASES == "att":
        stR = ExitStack()
        stR.close()
        st.close()
        return
    stR = ExitStack()
    ctx.sb("FE8", [8, T], BF16, stR)
    ctx.sb("WXB", [1, T], BF16, stR)
    st3 = ExitStack()
    ctx.sb("RT1", [128, TQ], F32, st3)
    ctx.sb("RT2", [128, TQ], F32, st3)
    ctx.sb("LT2", [8, T], F32, st3)
    em = Em(ctx, "B4")
    prevc = {}
    for tqc in range(NTQ):
        tq0 = TQ * tqc
        rstd = emit_rms_chunkstats(
            ctx, em, sc,
            lambda c: sc["SCR8"][:, T * c + tq0:T * c + tq0 + TQ], 1e-6,
            [prevc.get("x2_last"), prevc.get("lgev")])
        bcm = em.op("tensor", lambda t: t.matmul(
            bk(ctx, 2), ctx.t["ONESR"][:], sc["RSTDB"][0:1, 0:TQ],
            start=True, stop=True), after=[rstd, prevc.get("x2_last")],
            tag="b4_bc")
        lmms = []
        x2s = []
        for c in range(C):
            par = c % 2
            x2 = em.op("vector", lambda v, c=c, par=par, tq0=tq0: v.scalar_tensor_tensor(
                sc["RT1" if par == 0 else "RT2"][:, 0:TQ],
                sc["SCR8"][:, T * c + tq0:T * c + tq0 + TQ],
                sc["G2C"][:, c:c + 1], bk(ctx, 2), ALU.mult, ALU.mult),
                after=[bcm] + lmms[-2:-1] + ([prevc.get("lgev")] if c == 0 else []),
                tag="b4_x2")
            lmm = em.op("tensor", lambda t, c=c, par=par: t.matmul(
                bk(ctx, 3, p=8), sc["RW"][:, 8 * c:8 * (c + 1)],
                sc["RT1" if par == 0 else "RT2"][:, 0:TQ],
                start=(c == 0), stop=(c == C - 1)),
                after=[x2], tag="b4_lmm")
            lmms.append(lmm)
            x2s.append(x2)
        lgev = em.op("scalar", lambda s, tq0=tq0: s.activation(
            sc["LT2"][0:8, tq0:tq0 + TQ], bk(ctx, 3, p=8), AF.Identity,
            bias=sc["RB8"][:]), after=[lmms[-1]], tag="b4_lgev")
        prevc = dict(lgev=lgev, x2_last=x2s[-1])
        emit_taps(ctx, em, "LT2", sc["LT2"][0:8, :], [lgev])
    em.emit()
    if B_PHASES == "logits":
        st3.close()
        stR.close()
        st.close()
        return
    st3.close()

    st4 = ExitStack()
    st4a = ExitStack()
    ctx.sb("OH0T", [8, T], BF16, st4)
    ctx.sb("OH1T", [8, T], BF16, st4)
    ctx.sb("CUM0", [8, T], I16, st4)
    ctx.sb("CUM1", [8, T], I16, st4)
    ctx.sb("ONES8T", [8, TQ], F32, st4)
    ctx.sb("TOT", [8, 2], F32, st4)
    ctx.sb("GT64", [64, 2], F32, st4)
    ctx.sb("OFF", [8, 2], F32, st4)
    ctx.sb("OFFS", [8, 1], F32, st4)
    ctx.sb("DROW", [1, TQ], BF16, st4)
    ctx.t["RTA"] = ctx.t["ROWA"]
    ctx.t["RTB"] = ctx.t["ROWB"]
    ctx.t["MSLB"] = ctx.t["RSTDB"]
    ctx.t["WYB"] = ctx.t["MRB"]
    ctx.sb("G8S", [8, TQ], BF16, st4)
    ctx.sb("LTT", [128, NT * 8], F32, st4a)
    ctx.sb("V8", [128, 8], F32, st4a)
    ctx.sb("OHS", [128, 8], F32, st4a)
    ctx.sb("DD", [128, NT], F32, st4a)
    ctx.sb("TRS", [32, 128], F32, st4a)

    em = Em(ctx, "B5")
    evs = []
    trp = None
    for tqc in range(NTQ):
        trs = []
        for j in range(4):
            tt = 4 * tqc + j
            def tfn(t, tt=tt, j=j):
                return t.transpose(ctx.t["PSUM"][:, 8 * j:8 * (j + 1)],
                                   sc["LT2"][0:8, 128 * tt:128 * (tt + 1)],
                                   ctx.t["EYEF"][0:8, 0:8])
            trs.append(em.op("tensor", tfn, after=[trp], tag="b5_ltr"))
        ev = em.op("vector", lambda v, tqc=tqc: v.tensor_copy(
            sc["LTT"][:, 32 * tqc:32 * (tqc + 1)], ctx.t["PSUM"][:, 0:32]),
            after=[trs[-1]], tag="b5_lev")
        trp = ev
        evs.append(ev)
    ohdeps = []
    p_oh = {}
    for tt in range(NT):
        lt = sc["LTT"][:, 8 * tt:8 * (tt + 1)]
        mx8 = em.op("vector", lambda v, lt=lt: v.max(out=sc["V8"][:], in_=lt),
                    after=[evs[tt // 4], p_oh.get("dd")] + ohdeps[-1:],
                    tag="b5_max8")
        dd = em.op("vector", lambda v, tt=tt: v.tensor_tensor(
            sc["DD"][:, tt:tt + 1], sc["V8"][:, 1:2], sc["V8"][:, 0:1],
            ALU.subtract), after=[mx8], tag="b5_dd")
        p_oh["dd"] = dd
        for k, dst in ((0, "OH0T"), (1, "OH1T")):
            oh = em.op("vector", lambda v, lt=lt, k=k: v.tensor_scalar(
                sc["OHS"][:], lt, sc["V8"][:, k:k + 1], None, ALU.is_equal),
                after=[mx8, p_oh.get(("ev", k))], tag="b5_oh")
            tr = em.op("tensor", lambda t, k=k: t.transpose(
                ctx.t["PSUM"][0:8, 64 * k:64 * k + 128], sc["OHS"][:],
                ctx.t["EYEF"][:]), after=[oh, p_oh.get("ev_any")],
                tag="b5_ohtr")
            ev = em.op("vector", lambda v, tt=tt, k=k, dst=dst: v.tensor_copy(
                sc[dst][0:8, 128 * tt:128 * (tt + 1)],
                ctx.t["PSUM"][0:8, 64 * k:64 * k + 128]),
                after=[tr], tag="b5_ohev")
            p_oh[("ev", k)] = ev
            p_oh["ev_any"] = ev
            ohdeps.append(ev)
    dtr = em.op("tensor", lambda t: t.transpose(
        ctx.t["PSUM"][0:32, 512:512 + 128], sc["DD"][:], ctx.t["EYEF"][:]),
        after=[p_oh["dd"]], tag="b5_dtr")
    dte = em.op("vector", lambda v: v.tensor_copy(
        sc["TRS"][:], ctx.t["PSUM"][0:32, 512:512 + 128]), after=[dtr],
        tag="b5_dte")
    d_out = em.dma("gpsimd", ctx.dram["rowscratch"], sc["TRS"][:],
                   after=[dte], tag="b5_dout")
    d_in = d_out
    st4a.close()
    ctx.sb("RTC", [1, TQ], F32, st4)
    ctx.sb("RTD", [1, TQ], F32, st4)
    o8 = em.op("gpsimd", lambda g: g.memset(sc["ONES8T"][:], 1.0), tag="b5_o8")
    sc0p = sc1p = None
    for cname in ("CUM0", "CUM1"):
        oht = "OH0T" if cname == "CUM0" else "OH1T"
        prev_scan = sc0p if cname == "CUM1" else None
        for ch in range(NTQ):
            t0c = TQ * ch
            def scfn(v, cname=cname, oht=oht, ch=ch, t0c=t0c):
                init = 0.0 if ch == 0 else sc[cname][0:8, t0c - 1:t0c]
                return v.tensor_tensor_scan(
                    sc[cname][0:8, t0c:t0c + TQ], sc["ONES8T"][:],
                    sc[oht][0:8, t0c:t0c + TQ], init, ALU.mult, ALU.add)
            prev_scan = em.op("vector", scfn,
                             after=[o8, prev_scan] + (ohdeps if ch == 0 else []),
                             tag=f"b5_scan_{cname}_{ch}")
        if cname == "CUM0":
            sc0p = prev_scan
        else:
            sc1p = prev_scan
    tt0 = em.op("vector", lambda v: v.tensor_copy(
        sc["TOT"][:, 0:1], sc["CUM0"][:, T - 1:T]), after=[sc0p], tag="b5_t0")
    tt1 = em.op("vector", lambda v: v.tensor_copy(
        sc["TOT"][:, 1:2], sc["CUM1"][:, T - 1:T]), after=[sc1p], tag="b5_t1")
    if NO_CC:
        offe = em.op("gpsimd", lambda g: g.memset(sc["OFF"][:], 0.0),
                     after=[tt0, tt1], tag="b5_offz")
    else:
        ccout = em.dma("gpsimd", ctx.dram["cc_in"], sc["TOT"][:],
                       after=[tt0, tt1], tag="b5_ccout")
        def ccfn(g):
            return g.collective_compute(
                "AllGather", ALU.bypass, replica_groups=[list(range(8))],
                ins=[ctx.dram["cc_in"].opt()], outs=[ctx.dram["cc_out"].opt()])
        cc = em.op("gpsimd", ccfn, after=[ccout], dma=True, tag="b5_cc")
        cc.cc = True
        ccin = em.dma("gpsimd", sc["GT64"][:], ctx.dram["cc_out"], after=[cc],
                      tag="b5_ccin")
        pmld = em.op("tensor", lambda t: t.matmul(
            ctx.t["PSUM"][0:8, 0:2], ctx.t["PM64"][:], sc["GT64"][:],
            start=True, stop=True), after=[ccin, evs[-1]], tag="b5_offmm")
        offe = em.op("vector", lambda v: v.tensor_copy(sc["OFF"][:],
                                                       ctx.t["PSUM"][0:8, 0:2]),
                     after=[pmld], tag="b5_offev")
    offs = em.op("vector", lambda v: v.tensor_tensor(
        sc["OFFS"][:], sc["OFF"][:, 0:1], sc["OFF"][:, 1:2], ALU.add),
        after=[offe], tag="b5_offs")
    ss = em.op("vector", lambda v: v.tensor_tensor(
        sc["CUM1"][:], sc["CUM0"][:], sc["CUM1"][:], ALU.add),
        after=[sc1p, tt0, tt1], tag="b5_ss")
    p0 = em.op("vector", lambda v: v.tensor_scalar(
        sc["CUM0"][:], sc["CUM0"][:], sc["OFF"][:, 0:1], None, ALU.add),
        after=[ss, offe], tag="b5_p0")
    p1 = em.op("vector", lambda v: v.tensor_scalar(
        sc["CUM1"][:], sc["CUM1"][:], sc["OFFS"][:], None, ALU.add),
        after=[ss, offs], tag="b5_p1")
    ka0 = em.op("vector", lambda v: v.tensor_scalar(
        sc["CUM0"][:], sc["CUM0"][:], float(T), None, ALU.is_lt),
        after=[p0], tag="b5_ka0")
    ka1 = em.op("vector", lambda v: v.tensor_scalar(
        sc["CUM1"][:], sc["CUM1"][:], float(T), None, ALU.is_lt),
        after=[p1], tag="b5_ka1")
    e0 = em.op("vector", lambda v: v.tensor_tensor(
        sc["OH0T"][:], sc["CUM0"][:], sc["OH0T"][:], ALU.mult),
        after=[ka0], tag="b5_em0")
    e1 = em.op("vector", lambda v: v.tensor_tensor(
        sc["OH1T"][:], sc["CUM1"][:], sc["OH1T"][:], ALU.mult),
        after=[ka1], tag="b5_em1")
    pch = {}
    for tqc in range(NTQ):
        tq0 = TQ * tqc
        a0 = em.op("tensor", lambda t, tq0=tq0: t.matmul(
            bk(ctx, 3, p=1), ctx.t["IOTA8B"][:], sc["OH0T"][0:8, tq0:tq0 + TQ],
            start=True, stop=True), after=[e0, pch.get("fe")], tag="b5_as0")
        a1 = em.op("tensor", lambda t, tq0=tq0: t.matmul(
            bk(ctx, 4, p=1), ctx.t["IOTA8B"][:], sc["OH1T"][0:8, tq0:tq0 + TQ],
            start=True, stop=True), after=[e1, pch.get("fe")], tag="b5_as1")
        t0m = em.op("tensor", lambda t, tq0=tq0: t.matmul(
            bk(ctx, 5, p=1), ctx.t["ONES8B"][:], sc["OH0T"][0:8, tq0:tq0 + TQ],
            start=True, stop=True), after=[e0, pch.get("wy1")], tag="b5_tm0")
        t1m = em.op("tensor", lambda t, tq0=tq0: t.matmul(
            bk(ctx, 6, p=1), ctx.t["ONES8B"][:], sc["OH1T"][0:8, tq0:tq0 + TQ],
            start=True, stop=True), after=[e1, pch.get("wy2")], tag="b5_tm1")
        cpa = em.op("scalar", lambda s: s.copy(sc["RTA"][0:1, :],
                                               bk(ctx, 3, p=1)),
                    after=[a0, pch.get("cpb")], tag="b5_cpa")
        mrow = em.op("vector", lambda v: v.tensor_tensor(
            sc["RTB"][0:1, :], bk(ctx, 4, p=1), sc["RTA"][0:1, :], ALU.max),
            after=[a1, cpa], tag="b5_m")
        mrb = em.op("scalar", lambda s: s.activation(
            sc["MSLB"][0:1, :], sc["RTB"][0:1, :], AF.Identity),
            after=[mrow], tag="b5_mb")
        dch = em.dma("gpsimd", sc["DROW"][0:1, 0:TQ],
                     bass.AP(ctx.dram["rowscratch"].tensor, tq0,
                             [[TQ, 1], [1, TQ]]),
                     after=[d_in, pch.get("sg")], tag="b5_dch")
        sg = em.op("scalar", lambda s: s.activation(
            sc["RTC"][0:1, :], sc["DROW"][0:1, 0:TQ], AF.Sigmoid),
            after=[dch, pch.get("wy1")], tag="b5_sg")
        sg0 = em.op("vector", lambda v: v.tensor_scalar(
            sc["RTD"][0:1, :], sc["RTC"][0:1, :], -1.0, 1.0, ALU.mult,
            ALU.add), after=[sg, pch.get("wy1")], tag="b5_sg0")
        cpb = em.op("scalar", lambda s: s.copy(sc["RTA"][0:1, :],
                                               bk(ctx, 5, p=1)),
                    after=[t0m, mrow], tag="b5_cpb")
        wy1 = em.op("vector", lambda v: v.tensor_tensor(
            sc["RTD"][0:1, :], sc["RTD"][0:1, :], sc["RTA"][0:1, :],
            ALU.mult), after=[cpb, sg0], tag="b5_wy1")
        wy2 = em.op("vector", lambda v: v.tensor_tensor(
            sc["RTC"][0:1, :], sc["RTC"][0:1, :], bk(ctx, 6, p=1), ALU.mult),
            after=[t1m, sg, wy1], tag="b5_wy2")
        wyr = em.op("vector", lambda v: v.tensor_tensor(
            sc["RTD"][0:1, :], sc["RTD"][0:1, :], sc["RTC"][0:1, :], ALU.add),
            after=[wy1, wy2], tag="b5_wy")
        wyb = em.op("scalar", lambda s: s.activation(
            sc["WYB"][0:1, :], sc["RTD"][0:1, :], AF.Identity),
            after=[wyr], tag="b5_wyb")
        wxr = em.op("vector", lambda v, tq0=tq0: v.tensor_scalar(
            sc["WXB"][0:1, tq0:tq0 + TQ], sc["RTD"][0:1, :], -1.0, 1.0,
            ALU.mult, ALU.add), after=[wyr], tag="b5_wx")
        mb8 = em.op("tensor", lambda t: t.matmul(
            bk(ctx, 3, p=8), ctx.t["ONESR"][0:1, 0:8], sc["MSLB"][0:1, :],
            start=True, stop=True), after=[mrb, cpa, a0], tag="b5_mb8")
        ge8 = em.op("vector", lambda v: v.tensor_scalar(
            sc["G8S"][:], bk(ctx, 3, p=8), ctx.t["IOTA8"][:], None,
            ALU.is_equal), after=[mb8], tag="b5_ge8")
        wb8 = em.op("tensor", lambda t: t.matmul(
            bk(ctx, 4, p=8), ctx.t["ONESR"][0:1, 0:8], sc["WYB"][0:1, :],
            start=True, stop=True), after=[wyb, mrow, a1], tag="b5_wb8")
        fe = em.op("vector", lambda v, tq0=tq0: v.scalar_tensor_tensor(
            sc["FE8"][0:8, tq0:tq0 + TQ], sc["G8S"][:], 1.0,
            bk(ctx, 4, p=8), ALU.mult, ALU.mult), after=[ge8, wb8],
            tag="b5_fe")
        pch = dict(mrow=mrow, wy1=wy1, wy2=wy2, fe=fe, cpb=cpb, sg=sg)
        if tqc == NTQ - 1:
            emit_taps(ctx, em, "FE8", sc["FE8"][0:8, :], [fe])
            emit_taps(ctx, em, "WXB", sc["WXB"][0:1, :], [wxr])
            emit_taps(ctx, em, "EM0", sc["OH0T"][0:8, :], [fe])
            emit_taps(ctx, em, "EM1", sc["OH1T"][0:8, :], [fe])
    em.emit()
    st4.close()
    if B_PHASES == "route":
        stR.close()
        st.close()
        return

    st5 = ExitStack()
    ctx.sb("HM", [128, 16 * 1024], BF16, st5)
    ctx.sb("ASB", [128, 1024], BF16, st5)
    ctx.sb("FEB", [128, 1024], BF16, st5)
    ctx.sb("W1C", [128, 1024], BF16, st5)
    ctx.sb("W2C", [128, 1024], BF16, st5)
    ctx.sb("W3CO", [128, 1024], BF16, st5)
    ctx.t["RT1"] = ctx.t["APPA"]
    NQTR = 4
    TPQ = NTQ // NQTR
    w1t = ctx.dram["L0_ffn_w1"].tensor
    w2t = ctx.dram["L0_ffn_w2"].tensor
    w3t = ctx.dram["L0_ffn_w3"].tensor
    for qtr in range(NQTR):
        em = Em(ctx, f"B6_{qtr}")
        tqcs = [qtr * TPQ + i for i in range(TPQ)]
        conv = []
        for tqc in tqcs:
            tq0 = TQ * tqc
            rstd = emit_rms_chunkstats(
                ctx, em, sc,
                lambda c, tq0=tq0: sc["SCR8"][:, T * c + tq0:T * c + tq0 + TQ],
                1e-6, conv[-1:])
            bcm = em.op("tensor", lambda t: t.matmul(
                bk(ctx, 2), ctx.t["ONESR"][:], sc["RSTDB"][0:1, 0:TQ],
                start=True, stop=True), after=[rstd] + conv[-1:], tag="b6_bc")
            wxm = em.op("tensor", lambda t, tq0=tq0: t.matmul(
                bk(ctx, 3), ctx.t["ONESR"][:], sc["WXB"][0:1, tq0:tq0 + TQ],
                start=True, stop=True), after=conv[-2:-1], tag="b6_wxm")
            for c in range(C):
                cv = em.op("vector", lambda v, c=c, tq0=tq0: v.scalar_tensor_tensor(
                    sc["SCR8"][:, T * c + tq0:T * c + tq0 + TQ],
                    sc["SCR8"][:, T * c + tq0:T * c + tq0 + TQ],
                    sc["G2C"][:, c:c + 1], bk(ctx, 2), ALU.mult, ALU.mult),
                    after=[bcm] + conv[-1:], tag="b6_conv")
                t1 = em.op("vector", lambda v, c=c, tq0=tq0: v.tensor_tensor(
                    sc["RT1"][:, 0:TQ],
                    sc["SCR8"][:, T * c + tq0:T * c + tq0 + TQ],
                    bk(ctx, 3), ALU.mult), after=[cv, wxm], tag="b6_t1")
                ac = em.op("vector", lambda v, c=c, tq0=tq0: v.tensor_tensor(
                    xsl(ctx, c, tq0 // TQ), xsl(ctx, c, tq0 // TQ),
                    sc["RT1"][:, 0:TQ], ALU.add), after=[t1], tag="b6_ac")
                conv.append(ac)
        prev_e = {}
        for e in range(8):
            feb_ops = []
            for i, tqc in enumerate(tqcs):
                tq0 = TQ * tqc
                fm = em.op("tensor", lambda t, e=e, tq0=tq0: t.matmul(
                    bk(ctx, 5), ctx.t["E8"][:, 128 * e:128 * (e + 1)],
                    sc["FE8"][0:8, tq0:tq0 + TQ], start=True, stop=True),
                    after=conv[-1:] + feb_ops[-1:] + [prev_e.get("fe")],
                    tag="b6_febmm")
                fe = em.op("scalar", lambda s, i=i: s.activation(
                    sc["FEB"][:, TQ * i:TQ * (i + 1)], bk(ctx, 5),
                    AF.Identity), after=[fm], tag="b6_febev")
                feb_ops.append(fe)
            prev_g = {}
            for hc in range(16):
                w1l = em.dma("gpsimd", sc["W1C"][:, 0:1024],
                             bass.AP(w1t, e * D * 2048 + 128 * hc,
                                     [[2048, 128], [128 * 2048, C], [1, 128]]),
                             after=[prev_g.get("am")], tag="b6_w1l")
                w2l = em.dma("gpsimd", sc["W2C"][:, 0:1024],
                             bass.AP(w2t, e * D * 2048 + 128 * hc,
                                     [[2048, 128], [128 * 2048, C], [1, 128]]),
                             after=[prev_g.get("bm")], tag="b6_w2l")
                ams = []
                bms = []
                for i, tqc in enumerate(tqcs):
                    tq0 = TQ * tqc
                    cdep = conv[-1:] if (hc == 0 and e == 0) else []
                    am = mm_chain(ctx, em, bk(ctx, 0 + i),
                                  [sc["W1C"][:, 128 * ci:128 * (ci + 1)]
                                   for ci in range(C)],
                                  [sc["SCR8"][:, T * ci + tq0:
                                              T * ci + tq0 + TQ]
                                   for ci in range(C)],
                                  after=[w1l, prev_g.get("aev")] + cdep,
                                  tag="b6_amm")
                    bm = mm_chain(ctx, em, bk(ctx, 2 + i),
                                  [sc["W2C"][:, 128 * ci:128 * (ci + 1)]
                                   for ci in range(C)],
                                  [sc["SCR8"][:, T * ci + tq0:
                                              T * ci + tq0 + TQ]
                                   for ci in range(C)],
                                  after=[w2l, prev_g.get("prd")] + cdep,
                                  tag="b6_bmm")
                    ams.append(am)
                    bms.append(bm)
                aev = em.op("scalar", lambda s: s.activation(
                    sc["ASB"][:, 0:1024], bkspan(ctx, 0, 2), AF.Identity),
                    after=ams + [prev_g.get("prd")], tag="b6_aev")
                prd = em.op("vector", lambda v, hc=hc: v.tensor_tensor(
                    sc["HM"][:, 1024 * hc:1024 * (hc + 1)], bkspan(ctx, 2, 2),
                    sc["ASB"][:, 0:1024], ALU.mult),
                    after=bms + [aev, prev_g.get("gel"), prev_g.get("gate")],
                    tag="b6_prd")
                gel = em.op("scalar", lambda s, hc=hc: s.activation(
                    sc["HM"][:, 1024 * hc:1024 * (hc + 1)],
                    sc["HM"][:, 1024 * hc:1024 * (hc + 1)], AF.Gelu),
                    after=[prd, prev_g.get("gate")], tag="b6_gelu")
                gate = em.op("gpsimd", lambda g, hc=hc: g.tensor_mul(
                    sc["HM"][:, 1024 * hc:1024 * (hc + 1)],
                    sc["HM"][:, 1024 * hc:1024 * (hc + 1)],
                    sc["FEB"][:, 0:1024]), after=[gel] + feb_ops,
                    tag="b6_gate")
                prev_g = dict(am=ams[-1], bm=bms[-1], aev=aev, prd=prd,
                              gel=gel, gate=gate)
            prev_y = {}
            for co in range(C):
                for hh in range(2):
                    w3l = em.dma("gpsimd", sc["W3CO"][:, 0:1024],
                                 bass.AP(w3t, e * 2048 * D + 1024 * hh * D +
                                         128 * co,
                                         [[D, 128], [128 * D, 8], [1, 128]]),
                                 after=[prev_y.get("ym_last")], tag="b6_w3l")
                    for i, tqc in enumerate(tqcs):
                        tq0 = TQ * tqc
                        dep = [w3l, prev_g["gate"], prev_y.get(f"ya{i}")]
                        if co == 0 and hh == 0 and i == 0:
                            dep.append(prev_e.get("ya_last"))
                        ym = mm_chain(
                            ctx, em, bk(ctx, 4 + 2 * i),
                            [sc["W3CO"][:, 128 * hc:128 * (hc + 1)]
                             for hc in range(8)],
                            [sc["HM"][:, 1024 * (8 * hh + hc) + TQ * i:
                                      1024 * (8 * hh + hc) + TQ * (i + 1)]
                             for hc in range(8)],
                            after=dep, tag="b6_ymm")
                        ya = em.op("vector", lambda v, co=co, tqc=tqc, i=i:
                                   v.tensor_tensor(
                                       xsl(ctx, co, tqc), xsl(ctx, co, tqc),
                                       bk(ctx, 4 + 2 * i), ALU.add),
                                   after=[ym], tag="b6_yacc")
                        prev_y[f"ya{i}"] = ya
                        prev_y["ym_last"] = ym
            prev_e = dict(fe=feb_ops[-1], ya_last=prev_y.get("ya1") or
                          prev_y.get("ya0"))
        em.emit()
    st5.close()

    for tqc in range(NTQ):
        em = Em(ctx, f"B7_{tqc}")
        emit_ln_update(ctx, em, tqc, lambda c: xsl(ctx, c, tqc), [], sc,
                       src_bf16=True)
        em.emit()
    stR.close()
    st.close()


def emit_block_c(ctx):
    nc = ctx.nc
    st = ExitStack()
    sc = ctx.t
    ctx.sb("PREACC", [128, C * T], BF16, st)
    ctx.sb("BQC", [128, C], F32, st)
    ctx.sb("BKC", [128, C], F32, st)
    ctx.sb("BVC", [128, C], F32, st)
    ctx.sb("BPC", [128, C], F32, st)

    st0 = ExitStack()
    ctx.sb("GROW", [1, D], F32, st0)
    em = Em(ctx, "C0")
    gprev = None
    for gname, gsb, scl in [("L1_attn_bq", "BQC", 0.125), ("L1_attn_bk", "BKC", 1.0),
                            ("L1_attn_bv", "BVC", 1.0), ("L1_attn_bp", "BPC", 1.0)]:
        gld = em.dma("sync", sc["GROW"][:],
                     bass.AP(ctx.dram[gname].tensor, 0, [[D, 1], [1, D]]),
                     after=[gprev], tag=f"g:{gname}")
        gtr = []
        for c in range(C):
            def tfn(t, c=c):
                return t.transpose(ctx.t["PSUM"][:, c:c + 1],
                                   sc["GROW"][0:1, 128 * c:128 * (c + 1)],
                                   ctx.t["EYEF"][0:1, 0:1])
            gtr.append(em.op("tensor", tfn,
                             after=[gld] + (gtr[-1:] or [gprev]), tag="gtr"))
        gprev = em.op("vector", lambda v, gsb=gsb, scl=scl: v.tensor_scalar(
            sc[gsb][:, 0:C], ctx.t["PSUM"][:, 0:C], scl, None, ALU.mult),
            after=[gtr[-1]], tag="gev")
    em.op("gpsimd", lambda g: g.memset(sc["PREACC"][:], 0.0), tag="pre0")
    em.emit()
    st0.close()

    stA = ExitStack()
    ctx.sb("COS1S", [128, TQ], BF16, stA)
    ctx.sb("SIN1S", [128, TQ], BF16, stA)
    ctx.sb("CMASK", [128, 2048], BF16, stA)
    ctx.sb("K2A", [65, T], BF16, stA)
    ctx.sb("K2B", [65, T], BF16, stA)
    ctx.sb("VPA", [128, 65 * NT], BF16, stA)
    ctx.sb("VPB", [128, 65 * NT], BF16, stA)
    ctx.sb("WQP", [128, C * 128], BF16, stA)
    ctx.sb("WKP", [128, C * 128], BF16, stA)
    ctx.sb("WVP", [128, C * 128], BF16, stA)
    ctx.sb("WPP", [128, C * 128], BF16, stA)
    ctx.sb("QA", [128, TQ], BF16, stA)
    ctx.sb("QAS", [128, TQ], BF16, stA)
    ctx.sb("RT1", [128, TQ], F32, stA)
    ctx.sb("Q2P", [65, 2 * TQ], BF16, stA)
    ctx.sb("SQS", [128, TQ], BF16, stA)
    ctx.t["SNQ0"] = ctx.t["ROWA"]
    ctx.t["SNQ1"] = ctx.t["ROWB"]
    ctx.t["SROW"] = ctx.t["ROWB"]
    ctx.t["SROW2"] = ctx.t["ROWA"]
    ctx.sb("KM2", [1, 2], F32, stA)
    ctx.sb("KC2", [1, 2], F32, stA)
    ctx.sb("PT", [128, 2048], BF16, stA)
    ctx.sb("RROW", [1, TQ], BF16, stA)
    ctx.sb("RBC", [64, TQ], BF16, stA)
    ctx.sb("AONP", [128, TQ], BF16, stA)

    em = Em(ctx, "Cpre")
    em.dma("sync", sc["CMASK"][:], ctx.dram["CMASK"], tag="cmask")
    em.op("gpsimd", lambda g: g.memset(sc["K2A"][64:65, :], 1.0), tag="k2a1")
    em.op("gpsimd", lambda g: g.memset(sc["K2B"][64:65, :], 1.0), tag="k2b1")
    em.op("gpsimd", lambda g: g.memset(sc["VPA"][:], 1.0), tag="vpa1")
    em.op("gpsimd", lambda g: g.memset(sc["VPB"][:], 1.0), tag="vpb1")
    em.emit()

    def rope1(em, psum_bank, dst0, dst1, deps, scale, bias_col, prev,
              extra_ev=()):
        ev = em.op("scalar", lambda s: s.activation(
            sc["QA"][:, 0:TQ], bk(ctx, psum_bank), AF.Identity, scale=scale,
            bias=bias_col),
            after=deps + [prev.get("t1"), prev.get("rot")] + list(extra_ev),
            tag="c_ev")
        rot = em.op("tensor", lambda t: t.matmul(
            bk(ctx, 3), ctx.t["PJ"][:], sc["QA"][:, 0:TQ],
            start=True, stop=True), after=[ev, prev.get("t2")], tag="c_rot")
        t1 = em.op("vector", lambda v: v.tensor_tensor(
            sc["RT1"][:, 0:TQ], sc["QA"][:, 0:TQ], sc["COS1S"][:, 0:TQ],
            ALU.mult),
            after=[ev, prev.get("o0"), prev.get("o1"), prev.get("tbl")],
            tag="c_t1")
        t2 = em.op("vector", lambda v: v.tensor_tensor(
            sc["QAS"][:, 0:TQ], bk(ctx, 3), sc["SIN1S"][:, 0:TQ], ALU.mult),
            after=[rot, prev.get("o0"), prev.get("o1"), prev.get("tbl")],
            tag="c_t2")
        o0 = em.op("vector", lambda v: v.tensor_tensor(
            dst0, sc["RT1"][0:64, 0:TQ], sc["QAS"][0:64, 0:TQ], ALU.add),
            after=[t1, t2], tag="c_o0")
        o1 = em.op("vector", lambda v: v.tensor_tensor(
            dst1, sc["RT1"][64:128, 0:TQ], sc["QAS"][64:128, 0:TQ], ALU.add),
            after=[t1, t2], tag="c_o1")
        return dict(ev=ev, rot=rot, t1=t1, t2=t2, o0=o0, o1=o1)

    wq_t = ctx.dram["L1_attn_Wq"].tensor
    wk_t = ctx.dram["L1_attn_Wk"].tensor
    wv_t = ctx.dram["L1_attn_Wv"].tensor
    wp_t = ctx.dram["L1_attn_Wp"].tensor

    for ro in range(C):
        h0, h1 = 2 * ro, 2 * ro + 1
        em = Em(ctx, f"Cw_{ro}")
        for wt, sb in [(wq_t, "WQP"), (wk_t, "WKP"), (wv_t, "WVP")]:
            em.dma("gpsimd",
                   bass.AP(sc[sb], 0, [[C * 128, 128], [128, C], [1, 128]]),
                   bass.AP(wt, 128 * ro, [[D, 128], [128 * D, C], [1, 128]]),
                   tag=f"w:{sb}")
        em.dma("gpsimd", sc["WPP"][:, 0:C * 128],
               bass.AP(wp_t, 128 * ro * D, [[D, 128], [1, 1], [1, D]]),
               tag="w:WPP")
        em.emit()

        em = Em(ctx, f"Ckv_{ro}")
        prevb = {}
        rp = {}
        km0 = em.op("gpsimd", lambda g: g.memset(sc["KM2"][:], 0.0), tag="km20")
        kn_prev = {0: km0, 1: km0}
        for tqc in range(NTQ):
            tq0 = TQ * tqc
            tl0 = em.dma("sync", sc["COS1S"][:],
                         ctx.dram["COS1"][:, tq0:tq0 + TQ],
                         after=[rp.get("t1")], tag="tbl_c")
            tl1 = em.dma("sync", sc["SIN1S"][:],
                         ctx.dram["SIN1"][:, tq0:tq0 + TQ],
                         after=[rp.get("t2")], tag="tbl_s")
            tbl = em.op("sync", lambda e: None, after=[tl0, tl1], tag="tblw")
            rp["tbl"] = tbl
            kmm = mm_chain(ctx, em, bk(ctx, 0),
                           [sc["WKP"][:, 128 * ci:128 * (ci + 1)]
                            for ci in range(C)],
                           [xsl(ctx, ci, tqc) for ci in range(C)],
                           after=[rp.get("ev")], tag="kmm")
            rp = rope1(em, 0,
                       sc["K2A"][0:64, tq0:tq0 + TQ],
                       sc["K2B"][0:64, tq0:tq0 + TQ],
                       [kmm], 1.0, sc["BKC"][:, ro:ro + 1], rp,
                       extra_ev=[prevb.get("vpe")])
            for par, dep in ((0, rp["o0"]), (1, rp["o1"])):
                k2t = sc["K2A"] if par == 0 else sc["K2B"]
                qq = em.op("scalar", lambda s, k2t=k2t, tq0=tq0: s.activation(
                    sc["SQS"][0:64, 0:TQ], k2t[0:64, tq0:tq0 + TQ],
                    AF.Square), after=[dep, kn_prev[par], kn_prev[1 - par]],
                    tag="kn_sq")
                nm = em.op("tensor", lambda t: t.matmul(
                    bk(ctx, 5, p=1), ctx.t["ONESCB"][0:64, :],
                    sc["SQS"][0:64, 0:TQ], start=True, stop=True),
                    after=[qq, kn_prev[par], kn_prev[1 - par]], tag="kn_mm")
                rd = em.op("vector", lambda v: v.tensor_reduce(
                    sc["SNQ0"][0:1, 0:1], bk(ctx, 5, p=1), AX.X, ALU.max),
                    after=[nm, kn_prev[par], kn_prev[1 - par]], tag="kn_rd")
                ka = em.op("vector", lambda v, par=par: v.tensor_tensor(
                    sc["KM2"][0:1, par:par + 1], sc["KM2"][0:1, par:par + 1],
                    sc["SNQ0"][0:1, 0:1], ALU.max), after=[rd], tag="kn_acc")
                kn_prev[par] = ka
            vmm = mm_chain(ctx, em, bk(ctx, 1),
                           [sc["WVP"][:, 128 * ci:128 * (ci + 1)]
                            for ci in range(C)],
                           [xsl(ctx, ci, tqc) for ci in range(C)],
                           after=[prevb.get("vev")], tag="vmm")
            vev = em.op("scalar", lambda s, ro=ro: s.activation(
                sc["QAS"][:, 0:TQ], bk(ctx, 1), AF.Identity,
                bias=sc["BVC"][:, ro:ro + 1]),
                after=[vmm, rp.get("t2"), rp.get("o0"), rp.get("o1"),
                       prevb.get("vpe")], tag="vev")
            prevb["vev"] = vev
            vpe = prevb.get("vpe")
            for par in range(2):
                vpt = sc["VPA"] if par == 0 else sc["VPB"]
                for j in range(4):
                    def tfn(t, par=par, j=j):
                        return t.transpose(
                            ctx.t["PSB"][:, 64 * j:64 * (j + 1)],
                            sc["QAS"][64 * par:64 * par + 64,
                                      128 * j:128 * (j + 1)],
                            ctx.t["EYEB"][64 * par:64 * par + 64,
                                          64 * par:64 * par + 64])
                    vtr = em.op("tensor", tfn, after=[vev, vpe], tag="vtr")
                    def efn(v, vpt=vpt, j=j, tqc=tqc):
                        jj = 4 * tqc + j
                        return v.tensor_copy(
                            vpt[:, 65 * jj:65 * jj + 64],
                            ctx.t["PSB"][:, 64 * j:64 * (j + 1)])
                    vpe = em.op("vector", efn, after=[vtr], tag="vpe")
            prevb["vpe"] = vpe
        for par in range(2):
            kq = em.op("scalar", lambda s, par=par: s.activation(
                sc["KC2"][0:1, par:par + 1], sc["KM2"][0:1, par:par + 1],
                AF.Sqrt), after=[kn_prev[0], kn_prev[1]], tag="kq")
            em.op("vector", lambda v, par=par: v.tensor_scalar(
                sc["KC2"][0:1, par:par + 1], sc["KC2"][0:1, par:par + 1],
                -1.0, None, ALU.mult), after=[kq], tag="kneg")
        em.emit()

        for tqc in range(NTQ):
            tq0 = TQ * tqc
            em = Em(ctx, f"Cq_{ro}_{tqc}")
            tl0 = em.dma("sync", sc["COS1S"][:],
                         ctx.dram["COS1"][:, tq0:tq0 + TQ], tag="tbl_c")
            tl1 = em.dma("sync", sc["SIN1S"][:],
                         ctx.dram["SIN1"][:, tq0:tq0 + TQ], tag="tbl_s")
            tbl = em.op("sync", lambda e: None, after=[tl0, tl1], tag="tblw")
            qmm = mm_chain(ctx, em, bk(ctx, 0),
                           [sc["WQP"][:, 128 * ci:128 * (ci + 1)]
                            for ci in range(C)],
                           [xsl(ctx, ci, tqc) for ci in range(C)],
                           after=[], tag="qmm")
            rp2 = rope1(em, 0,
                        sc["Q2P"][0:64, 0:TQ],
                        sc["Q2P"][0:64, TQ:2 * TQ],
                        [qmm], 0.125, sc["BQC"][:, ro:ro + 1], {"tbl": tbl})
            c0, c1 = emit_cnorm(
                ctx, em, sc,
                sc["Q2P"][0:64, 0:TQ], sc["Q2P"][0:64, TQ:2 * TQ],
                sc["KC2"][0:1, 0:1], sc["KC2"][0:1, 1:2],
                sc["Q2P"][64:65, 0:TQ], sc["Q2P"][64:65, TQ:2 * TQ],
                [rp2["o0"], rp2["o1"]], [None, None])
            qdeps = [rp2["o0"], rp2["o1"], c0, c1]
            jn = 4 * tqc + 4
            pn1, pn6, lexp = emit_attn_sweep(
                ctx, em, sc, 2, lambda h: jn,
                lambda h, j: (sc["K2A"] if h == 0 else sc["K2B"])[:, 128 * j:128 * (j + 1)],
                lambda h, j: (sc["VPA"] if h == 0 else sc["VPB"])[:, 65 * j:65 * (j + 1)],
                lambda h: sc["Q2P"][:, TQ * h:TQ * (h + 1)],
                lambda h: sc["AONP"][64 * h:64 * h + 64, 0:TQ],
                True, qdeps)
            pre_prev = []
            for co in range(C):
                b = co % 2
                dep = [pn6, lexp] + pre_prev[-2:-1]
                omm = em.op("tensor", lambda t, co=co, b=b: t.matmul(
                    bk(ctx, b), sc["WPP"][:, 128 * co:128 * (co + 1)],
                    sc["AONP"][:, 0:TQ], start=True, stop=True),
                    after=dep, tag=f"c_omm{co}")
                r1 = em.op("vector", lambda v, co=co, b=b, tq0=tq0: v.tensor_tensor(
                    sc["PREACC"][:, T * co + tq0:T * co + tq0 + TQ],
                    sc["PREACC"][:, T * co + tq0:T * co + tq0 + TQ],
                    bk(ctx, b), ALU.add), after=[omm], tag="c_acc")
                pre_prev.append(r1)
            em.emit()
    stA.close()

    st4 = ExitStack()
    ctx.sb("PREF", [128, C * TQ], F32, st4)
    for tqc in range(NTQ):
        em = Em(ctx, f"C4_{tqc}")
        deps = []
        for co in range(C):
            tq0 = TQ * tqc
            d1 = em.op("vector", lambda v, co=co, tq0=tq0: v.tensor_tensor(
                sc["PREF"][:, TQ * co:TQ * (co + 1)],
                sc["PREACC"][:, T * co + tq0:T * co + tq0 + TQ],
                xsl(ctx, co, tqc), ALU.add), after=[], tag="c4_add")
            d2 = em.op("vector", lambda v, co=co: v.tensor_scalar(
                sc["PREF"][:, TQ * co:TQ * (co + 1)],
                sc["PREF"][:, TQ * co:TQ * (co + 1)],
                sc["BPC"][:, co:co + 1], None, ALU.add), after=[d1],
                tag="c4_bp")
            deps.append(d2)
        emit_ln_update(ctx, em, tqc,
                       lambda c: sc["PREF"][:, TQ * c:TQ * (c + 1)],
                       deps, sc)
        em.emit()
    st4.close()
    st.close()


def emit_block_d(ctx):
    nc = ctx.nc
    st = ExitStack()
    sc = ctx.t
    NH = 4
    TPH = NTQ // NH
    HW = TPH * TQ
    ctx.sb("BMC", [128, C], F32, st)
    ctx.sb("BOC", [128, C], F32, st)
    ctx.sb("EMB", [128, 16], F32, st)
    ctx.sb("B2T", [16, 128], BF16, st)
    ctx.sb("B1C", [128, 64], F32, st)
    ctx.sb("W1S", [128, 16 * 512], BF16, st)
    ctx.sb("W2S", [128, 16 * 512], BF16, st)
    ctx.sb("H2TH", [128, C * HW], BF16, st)
    ctx.sb("GTH", [16, C * HW], BF16, st)

    st0 = ExitStack()
    ctx.sb("GROW", [1, D], F32, st0)
    ctx.sb("B1R", [16, 512], F32, st0)
    em = Em(ctx, "D0")
    em.dma("sync", sc["EMB"][:], ctx.dram["L1_ffn_expert_emb"], tag="emb")
    em.dma("gpsimd", sc["B2T"][:], ctx.dram["L1_ffn_b2"], tag="b2")
    w1_t = ctx.dram["L1_ffn_W1"].tensor
    em.dma("gpsimd", sc["W1S"][:, 0:16 * 512],
           bass.AP(w1_t, 0, [[512, 128], [128 * 512, 16], [1, 512]]),
           tag="w1")
    w2_t = ctx.dram["L1_ffn_W2"].tensor
    for ic in range(4):
        em.dma("gpsimd",
               bass.AP(sc["W2S"], 128 * ic, [[16 * 512, 128], [512, 16], [1, 128]]),
               bass.AP(w2_t, 128 * 128 * ic,
                       [[128, 128], [512 * 128, 16], [1, 128]]),
               tag=f"w2_{ic}")
    gprev = None
    for gname, gsb in [("L1_ffn_bm", "BMC"), ("L1_ffn_bo", "BOC")]:
        gld = em.dma("sync", sc["GROW"][:],
                     bass.AP(ctx.dram[gname].tensor, 0, [[D, 1], [1, D]]),
                     after=[gprev], tag=f"g:{gname}")
        gtr = []
        for c in range(C):
            def tfn(t, c=c):
                return t.transpose(ctx.t["PSUM"][:, c:c + 1],
                                   sc["GROW"][0:1, 128 * c:128 * (c + 1)],
                                   ctx.t["EYEF"][0:1, 0:1])
            gtr.append(em.op("tensor", tfn,
                             after=[gld] + (gtr[-1:] or [gprev]), tag="gtr"))
        gprev = em.op("vector", lambda v, gsb=gsb: v.tensor_copy(
            sc[gsb][:, 0:C], ctx.t["PSUM"][:, 0:C]), after=[gtr[-1]], tag="gev")
    b1ld = em.dma("sync", sc["B1R"][:], ctx.dram["L1_ffn_b1"],
                  after=[gprev], tag="b1")
    btr = []
    for ic in range(4):
        def tfn(t, ic=ic):
            return t.transpose(ctx.t["PSUM"][:, 16 * ic:16 * (ic + 1)],
                               sc["B1R"][0:16, 128 * ic:128 * (ic + 1)],
                               ctx.t["EYEF"][0:16, 0:16])
        btr.append(em.op("tensor", tfn,
                         after=[b1ld] + (btr[-1:] or [gprev]), tag="btr"))
    em.op("vector", lambda v: v.tensor_copy(sc["B1C"][:, 0:64],
                                            ctx.t["PSUM"][:, 0:64]),
          after=[btr[-1]], tag="b1ev")
    em.emit()
    st0.close()

    wm_t = ctx.dram["L1_ffn_Wm"].tensor
    wo_t = ctx.dram["L1_ffn_Wo"].tensor

    for hf in range(NH):
        tqcs = [hf * TPH + i for i in range(TPH)]
        st1 = ExitStack()
        ctx.sb("WM", [128, C * D], BF16, st1)
        ctx.sb("RT1", [128, TQ], F32, st1)
        ctx.sb("RT2", [128, TQ], F32, st1)
        ctx.sb("LTS", [16, TQ], F32, st1)
        ctx.sb("LTT", [128, 16], F32, st1)
        ctx.sb("GTM", [128, 16], F32, st1)
        ctx.sb("E16T", [128, 16], F32, st1)
        ctx.sb("MXC", [128, 1], F32, st1)
        ctx.sb("SSC", [128, 1], F32, st1)
        ctx.sb("V8D", [128, 8], F32, st1)
        em = Em(ctx, f"D1_{hf}")
        wmld = em.dma("gpsimd", sc["WM"][:, 0:C * D],
                      bass.AP(wm_t, 0, [[D, 128], [128 * D, C], [1, D]]),
                      tag="wm")
        pv = {}
        for ti, tqc in enumerate(tqcs):
            th0 = TQ * ti
            for j in range(C):
                par = j % 2
                rt = "RT1" if par == 0 else "RT2"
                hmm = mm_chain(ctx, em, bk(ctx, par),
                               [sc["WM"][:, D * ci + 128 * j:
                                         D * ci + 128 * (j + 1)]
                                for ci in range(C)],
                               [xsl(ctx, ci, tqc) for ci in range(C)],
                               after=[pv.get(f"ev{par}"), wmld],
                               tag="d1_hmm")
                evb = em.op("scalar", lambda s, j=j, th0=th0, par=par:
                            s.activation(
                                sc["H2TH"][:, HW * j + th0:HW * j + th0 + TQ],
                                bk(ctx, par), AF.Identity,
                                bias=sc["BMC"][:, j:j + 1]),
                            after=[hmm], tag="d1_evb")
                evf = em.op("vector", lambda v, par=par, j=j, rt=rt:
                            v.tensor_scalar(
                                sc[rt][:, 0:TQ], bk(ctx, par),
                                sc["BMC"][:, j:j + 1], None, ALU.add),
                            after=[hmm, evb, pv.get(f"lgw{par}")],
                            tag="d1_evf")
                pv[f"ev{par}"] = evf
                lgm = em.op("tensor", lambda t, rt=rt: t.matmul(
                    bk(ctx, 3, p=16), sc["EMB"][:], sc[rt][:, 0:TQ],
                    start=True, stop=True),
                    after=[evf, pv.get("lge")], tag="d1_lgm")
                pv[f"lgw{par}"] = lgm
                lte0 = em.op("vector", lambda v: v.tensor_copy(
                    sc["LTS"][0:16, 0:TQ], bk(ctx, 3, p=16)),
                    after=[lgm, pv.get("ltr_last")], tag="d1_lts")
                pv["lge"] = lte0
                for tt in range(4):
                    ltr = em.op("tensor", lambda t, tt=tt: t.transpose(
                        bk(ctx, 4, w=16),
                        sc["LTS"][0:16, 128 * tt:128 * (tt + 1)],
                        ctx.t["EYEF"][0:16, 0:16]),
                        after=[lte0, pv.get("lte")], tag="d1_ltr")
                    lte = em.op("vector", lambda v: v.tensor_copy(
                        sc["LTT"][:], bk(ctx, 4, w=16)),
                        after=[ltr, pv.get("gtm")], tag="d1_lte")
                    pv["lte"] = lte
                    if tt == 3:
                        pv["ltr_last"] = lte
                    mx = em.op("vector", lambda v: v.tensor_reduce(
                        sc["MXC"][:], sc["LTT"][:], AX.X, ALU.max,
                        negate=True), after=[lte, pv.get("e16d")], tag="d1_mx")
                    e16d = em.op("scalar", lambda s: s.activation(
                        sc["E16T"][:], sc["LTT"][:], AF.Exp,
                        bias=sc["MXC"][:]), after=[mx, pv.get("gtm")],
                        tag="d1_e16")
                    pv["e16d"] = e16d
                    ss = em.op("vector", lambda v: v.tensor_reduce(
                        sc["SSC"][:], sc["E16T"][:], AX.X, ALU.add),
                        after=[e16d], tag="d1_ss")
                    rs = em.op("vector", lambda v: v.reciprocal(
                        sc["SSC"][:], sc["SSC"][:]), after=[ss], tag="d1_rs")
                    m8 = em.op("vector", lambda v: v.max(
                        out=sc["V8D"][:], in_=sc["LTT"][:]),
                        after=[lte, rs], tag="d1_m8")
                    gtm = em.op("vector", lambda v: v.tensor_scalar(
                        sc["GTM"][:], sc["LTT"][:], sc["V8D"][:, 1:2], None,
                        ALU.is_ge), after=[m8, pv.get("gtr2")], tag="d1_ge")
                    gtm2 = em.op("vector", lambda v: v.tensor_tensor(
                        sc["GTM"][:], sc["GTM"][:], sc["E16T"][:], ALU.mult),
                        after=[gtm], tag="d1_gmul")
                    gtm3 = em.op("vector", lambda v: v.tensor_scalar(
                        sc["GTM"][:], sc["GTM"][:], sc["SSC"][:], None,
                        ALU.mult), after=[gtm2], tag="d1_gnorm")
                    pv["gtm"] = gtm3
                    gtr2 = em.op("tensor", lambda t: t.transpose(
                        ctx.t["PSUM"][0:16, 2560:2560 + 128], sc["GTM"][:],
                        ctx.t["EYEF"][:]), after=[gtm3, pv.get("gte")],
                        tag="d1_gtr")
                    pv["gtr2"] = gtr2
                    gte = em.op("vector", lambda v, j=j, th0=th0, tt=tt:
                                v.tensor_copy(
                                    sc["GTH"][0:16, HW * j + th0 + 128 * tt:
                                              HW * j + th0 + 128 * (tt + 1)],
                                    ctx.t["PSUM"][0:16, 2560:2560 + 128]),
                                after=[gtr2], tag="d1_gte")
                    pv["gte"] = gte
        em.emit()
        st1.close()

        st2 = ExitStack()
        ctx.sb("HS", [128, 4 * TQ], BF16, st2)
        ctx.sb("FEB", [128, TQ], BF16, st2)
        ctx.sb("MOH", [128, C * TQ], BF16, st2)
        ctx.sb("WO", [128, C * D], BF16, st2)
        ctx.sb("PREF", [128, C * TQ], BF16, st2)
        emw = Em(ctx, f"D2w_{hf}")
        emw.dma("gpsimd", sc["WO"][:, 0:C * D],
                bass.AP(wo_t, 0, [[D, 128], [128 * D, C], [1, D]]), tag="wo")
        emw.emit()
        for ti, tqc in enumerate(tqcs):
            th0 = TQ * ti
            em = Em(ctx, f"D2_{hf}_{tqc}")
            mo_deps = []
            prev_e = {}
            for j in range(C):
                b2m = em.op("tensor", lambda t, j=j, th0=th0: t.matmul(
                    bk(ctx, 4), sc["B2T"][:],
                    sc["GTH"][0:16, HW * j + th0:HW * j + th0 + TQ],
                    start=True, stop=False),
                    after=mo_deps[-1:], tag="d2_b2")
                last_w2 = None
                for e in range(16):
                    fm = em.op("tensor", lambda t, e=e, j=j, th0=th0: t.matmul(
                        bk(ctx, 5), ctx.t["E16"][:, 128 * e:128 * (e + 1)],
                        sc["GTH"][0:16, HW * j + th0:HW * j + th0 + TQ],
                        start=True, stop=True),
                        after=[prev_e.get("fe")], tag="d2_fm")
                    fe = em.op("scalar", lambda s: s.activation(
                        sc["FEB"][:, 0:TQ], bk(ctx, 5), AF.Identity),
                        after=[fm, prev_e.get("gate_3")], tag="d2_fe")
                    for ic in range(4):
                        par = ic % 2
                        w1m = em.op("tensor",
                                    lambda t, e=e, ic=ic, j=j, th0=th0, par=par:
                                    t.matmul(
                                        bk(ctx, par),
                                        sc["W1S"][:, 512 * e + 128 * ic:
                                                  512 * e + 128 * (ic + 1)],
                                        sc["H2TH"][:, HW * j + th0:
                                                   HW * j + th0 + TQ],
                                        start=True, stop=True),
                                    after=[prev_e.get(f"gel{par}")],
                                    tag="d2_w1")
                        gel = em.op("scalar", lambda s, e=e, ic=ic, par=par:
                                    s.activation(
                                        sc["HS"][:, TQ * ic:TQ * (ic + 1)],
                                        bk(ctx, par), AF.Gelu,
                                        bias=sc["B1C"][:, 16 * ic + e:
                                                       16 * ic + e + 1]),
                                    after=[w1m, prev_e.get("w2l")],
                                    tag="d2_gelu")
                        gate = em.op("gpsimd", lambda g, ic=ic: g.tensor_mul(
                            sc["HS"][:, TQ * ic:TQ * (ic + 1)],
                            sc["HS"][:, TQ * ic:TQ * (ic + 1)],
                            sc["FEB"][:, 0:TQ]), after=[gel, fe], tag="d2_gate")
                        prev_e[f"gel{par}"] = gate
                        prev_e[f"gate_{ic}"] = gate
                    def w2fn(t, e=e):
                        inst = None
                        for ic in range(4):
                            inst = t.matmul(
                                bk(ctx, 4),
                                sc["W2S"][:, 512 * e + 128 * ic:
                                          512 * e + 128 * (ic + 1)],
                                sc["HS"][:, TQ * ic:TQ * (ic + 1)],
                                start=False, stop=(e == 15 and ic == 3))
                        return inst
                    last_w2 = em.op("tensor", w2fn,
                                    after=[prev_e[f"gate_{ic2}"]
                                           for ic2 in range(4)] + [b2m],
                                    tag="d2_w2")
                    prev_e["fe"] = fe
                    prev_e["w2l"] = last_w2
                moev = em.op("scalar", lambda s, j=j: s.activation(
                    sc["MOH"][:, TQ * j:TQ * (j + 1)], bk(ctx, 4),
                    AF.Identity), after=[last_w2], tag="d2_moev")
                mo_deps.append(moev)
            pre_deps = []
            for co in range(C):
                b = co % 2
                omm = mm_chain(ctx, em, bk(ctx, b),
                               [sc["WO"][:, D * ci + 128 * co:
                                         D * ci + 128 * (co + 1)]
                                for ci in range(C)],
                               [sc["MOH"][:, TQ * ci:TQ * (ci + 1)]
                                for ci in range(C)],
                               after=mo_deps + pre_deps[-2:-1], tag="d3_omm")
                r1 = em.op("vector", lambda v, co=co, b=b, tqc=tqc:
                           v.tensor_tensor(
                               sc["PREF"][:, TQ * co:TQ * (co + 1)], bk(ctx, b),
                               xsl(ctx, co, tqc), ALU.add), after=[omm],
                           tag="d3_add")
                r2 = em.op("vector", lambda v, co=co: v.tensor_scalar(
                    sc["PREF"][:, TQ * co:TQ * (co + 1)],
                    sc["PREF"][:, TQ * co:TQ * (co + 1)],
                    sc["BOC"][:, co:co + 1], None, ALU.add), after=[r1],
                    tag="d3_bo")
                pre_deps.append(r2)
            emit_ln_update(ctx, em, tqc,
                           lambda c: sc["PREF"][:, TQ * c:TQ * (c + 1)],
                           pre_deps, sc, src_bf16=True)
            em.emit()
        st2.close()
    st.close()

def _in_map(core, inputs, names, x_state):
    cn = host_constants(core)
    m = {}
    for n in names:
        if n == "xin":
            m[n] = x_state[core]
        elif n in cn:
            m[n] = cn[n]
        else:
            m[n] = np.ascontiguousarray(inputs[n], dtype=np.float32)
    return m


def kernel(**inputs):
    x = np.ascontiguousarray(inputs["L0_attn_x"], dtype=np.float32)
    x_state = [x[i] for i in range(8)]
    launches = [("A",), ("B",), ("C",), ("D",)]
    for blocks in launches:
        nc, names = build_program(list(blocks), 0)
        in_maps = []
        for core in range(8):
            if any(b for b in blocks if b == "B"):
                pass
            in_maps.append(_in_map(core, inputs, names, x_state))
        res = run_bass_kernel_spmd(nc, in_maps, core_ids=list(range(8)))
        x_state = [np.asarray(res.results[i]["xout"], dtype=np.float32)
                   for i in range(8)]
    return np.stack(x_state, axis=0)


# version 02
# speedup vs baseline: 1.1968x; 1.1968x over previous
import math
from contextlib import ExitStack

import numpy as np
import ml_dtypes

import concourse.bass as bass
import concourse.mybir as mybir
from concourse.bass_utils import run_bass_kernel_spmd

F32 = mybir.dt.float32
I16 = mybir.dt.int16
BF16 = mybir.dt.bfloat16
AF = mybir.ActivationFunctionType
ALU = mybir.AluOpType
AX = mybir.AxisListType

T = 4096
D = 1024
C = 8
TQ = 512
NTQ = T // TQ
NT = T // 128
BF = ml_dtypes.bfloat16
SHIFT = 15.0

DEBUG_TAPS = []
NO_CC = False



ENGINES = ("tensor", "vector", "scalar", "gpsimd", "sync")


class Op:
    __slots__ = ("idx", "engine", "fn", "after", "dma", "sig", "sigval", "tag",
                 "semkey", "cc")

    def __init__(self, idx, engine, fn, after, dma, tag):
        self.idx = idx
        self.engine = engine
        self.fn = fn
        self.after = after
        self.dma = dma
        self.cc = False
        self.sig = False
        self.sigval = None
        self.semkey = None
        self.tag = tag


DMA_CHANNELS = {"sync": 6, "gpsimd": 6, "scalar": 2, "vector": 2, "tensor": 1}


class Em:

    def __init__(self, ctx, name):
        self.ctx = ctx
        self.name = name
        self.ops = []
        self.opset = set()

    def op(self, engine, fn, after=(), dma=False, tag=""):
        kept = []
        for a in after:
            if a is None:
                continue
            if id(a) in self.opset or a.sigval is not None:
                kept.append(a)
        o = Op(len(self.ops), engine, fn, tuple(kept), dma, tag)
        self.ops.append(o)
        self.opset.add(id(o))
        return o

    def dma(self, engine, out, in_, after=(), tag=""):
        return self.op(engine, lambda e: e.dma_start(out=out, in_=in_),
                       after=after, dma=True, tag=tag)

    def emit(self, drain_dmas=True):
        if drain_dmas:
            for e in ENGINES:
                dops = [o for o in self.ops if o.engine == e and o.dma]
                if dops:
                    self.op(e, lambda _e: None, after=dops, tag="dma_drain")
        sems = self.ctx.sems
        for o in self.ops:
            if o.dma:
                o.sig = True
            for a in o.after:
                if a.dma or a.engine != o.engine or o.engine != "tensor":
                    a.sig = True
        rr = self.ctx.dma_rr
        for o in self.ops:
            if not o.sig:
                continue
            if o.cc:
                key = "Dcc"
            elif o.dma:
                ch = rr[o.engine] % DMA_CHANNELS[o.engine]
                rr[o.engine] += 1
                key = f"D{o.engine}{ch}"
            else:
                key = "E" + o.engine
            o.semkey = key
            amt = 16 if (o.dma and not o.cc) else 1
            o.sigval = sems.v[key] + amt
            sems.v[key] = o.sigval
        nc = self.ctx.nc
        with nc.Block(self.name) as blk:
            for eng_name in ENGINES:
                eng_ops = [o for o in self.ops if o.engine == eng_name]
                if not eng_ops:
                    continue

                def body(eng, eng_ops=eng_ops, eng_name=eng_name):
                    for o in eng_ops:
                        waited = {}
                        deps = list(o.after)
                        for a in deps:
                            if not a.sig:
                                continue
                            if a.dma or a.engine != o.engine or o.engine != "tensor":
                                key = a.semkey
                                if waited.get(key, -1) >= a.sigval:
                                    continue
                                waited[key] = a.sigval
                                eng.wait_ge(sems.h[key], a.sigval)
                        if o.dma and not o.cc and o.sigval > 16:
                            if waited.get(o.semkey, -1) < o.sigval - 16:
                                eng.wait_ge(sems.h[o.semkey], o.sigval - 16)
                        inst = o.fn(eng)
                        if o.sig:
                            if inst is None:
                                inst = eng.engine_nop() if hasattr(eng, "engine_nop") \
                                    else eng.nop()
                            if o.cc:
                                inst.then_inc(sems.h[o.semkey])
                            else:
                                inst.then_inc(sems.h[o.semkey], 16 if o.dma else 1)

                getattr(blk, eng_name)(body)


class Sems:
    def __init__(self, nc, stack, names):
        self.h = {n: stack.enter_context(nc.semaphore(n)) for n in names}
        self.v = {n: 0 for n in names}


class Ctx:
    def __init__(self, nc, stack):
        self.nc = nc
        self.stack = stack
        self.t = {}

    def sb(self, name, shape, dtype, stack=None):
        self._uniq = getattr(self, "_uniq", 0) + 1
        h = (stack or self.stack).enter_context(
            self.nc.sbuf_tensor(f"sb{self._uniq}_{name}", shape, dtype))
        self.t[name] = h
        return h



def host_constants(core):
    cn = {}
    eye = np.eye(128, dtype=np.float32)
    cn["EYEB"] = eye.astype(BF)
    cn["EYEF"] = eye.copy()
    M = np.zeros((64, 64), np.float32)
    M[0:32, 32:64] = -np.eye(32)
    M[32:64, 0:32] = np.eye(32)
    PN = np.zeros((128, 128), np.float32)
    PN[0:64, 0:64] = M.T
    PN[64:128, 64:128] = M.T
    cn["PN"] = PN.astype(BF)
    PJ = np.zeros((128, 128), np.float32)
    for i in range(64):
        PJ[2 * i, 2 * i + 1] = 1.0
        PJ[2 * i + 1, 2 * i] = 1.0
    cn["PJ"] = PJ.astype(BF)
    thetas = 10000.0 ** (-np.arange(0, 64, 2, dtype=np.float32) / 64.0)
    ticks = np.outer(np.arange(T, dtype=np.float32), thetas)
    ticks = np.tile(ticks, (1, 2))
    cosL0 = np.cos(ticks).T.astype(np.float32)
    sinL0 = np.sin(ticks).T.astype(np.float32)
    cn["COS0"] = np.concatenate([cosL0, cosL0], 0).astype(BF)
    cn["SIN0"] = np.concatenate([sinL0, sinL0], 0).astype(BF)
    half = 32
    inv_freq = 1.0 / 10000.0 ** (np.arange(half, dtype=np.float32) * 2.0 / 64.0)
    ang = np.arange(T, dtype=np.float32)[:, None] * inv_freq[None, :]
    cosj = np.cos(ang)
    sinj = np.sin(ang)
    CJ = np.zeros((64, T), np.float32)
    SJ = np.zeros((64, T), np.float32)
    for i in range(half):
        CJ[2 * i] = cosj[:, i]
        CJ[2 * i + 1] = cosj[:, i]
        SJ[2 * i] = -sinj[:, i]
        SJ[2 * i + 1] = sinj[:, i]
    cn["COS1"] = np.concatenate([CJ, CJ], 0).astype(BF)
    cn["SIN1"] = np.concatenate([SJ, SJ], 0).astype(BF)
    p = np.arange(128)[:, None]
    f = np.arange(TQ)[None, :]
    masks = [(f >= p + 128 * d).astype(np.float32) for d in range(4)]
    cn["CMASK"] = np.concatenate(masks, 1).astype(BF)
    PAT = np.zeros((128, 128), np.float32)
    PAT[0:64, 0] = 1.0
    PAT[64:128, 64] = 1.0
    cn["PAT2F"] = PAT.copy()
    cn["PAT2B"] = PAT.astype(BF)
    cn["ONESCB"] = np.ones((128, 1), np.float32).astype(BF)
    cn["ONESC"] = np.ones((128, 1), np.float32)
    cn["MEANCB"] = np.full((128, 1), 1.0 / D, np.float32).astype(BF)
    cn["IOTA8"] = np.arange(8, dtype=np.float32)[:, None]
    cn["IOTA8B"] = np.arange(8, dtype=np.float32)[:, None].astype(BF)
    cn["ONES8B"] = np.ones((8, 1), np.float32).astype(BF)
    cn["ONES8"] = np.ones((8, 1), np.float32)
    pmv = np.zeros((64, 8), np.float32)
    for r in range(core):
        for e in range(8):
            pmv[8 * r + e, e] = 1.0
    cn["PM64"] = pmv
    E = np.zeros((16, 16 * 128), np.float32)
    for e in range(16):
        E[e, 128 * e:128 * (e + 1)] = 1.0
    cn["E16"] = E.astype(BF)
    cn["ONESR"] = np.ones((1, 128), np.float32).astype(BF)
    E8 = np.zeros((8, 8 * 128), np.float32)
    for e in range(8):
        E8[e, 128 * e:128 * (e + 1)] = 1.0
    cn["E8"] = E8.astype(BF)
    cn["ONESRF"] = np.ones((1, 128), np.float32)
    return cn


TABLE_SHAPES = {
    "COS0": ([128, T], BF16), "SIN0": ([128, T], BF16),
    "COS1": ([128, T], BF16), "SIN1": ([128, T], BF16),
    "CMASK": ([128, 2048], BF16),
}

CONST_SHAPES = {
    "EYEB": ([128, 128], BF16), "EYEF": ([128, 128], F32),
    "PN": ([128, 128], BF16), "PJ": ([128, 128], BF16),
    "PAT2F": ([128, 128], F32),
    "PAT2B": ([128, 128], BF16), "ONESCB": ([128, 1], BF16),
    "ONESC": ([128, 1], F32), "MEANCB": ([128, 1], BF16),
    "IOTA8": ([8, 1], F32), "ONES8": ([8, 1], F32),
    "IOTA8B": ([8, 1], BF16), "ONES8B": ([8, 1], BF16),
    "PM64": ([64, 8], F32), "E16": ([16, 16 * 128], BF16),
    "ONESR": ([1, 128], BF16), "ONESRF": ([1, 128], F32),
    "E8": ([8, 8 * 128], BF16),
}

WEIGHTS = {
    "L0_attn_w_q": [D, D], "L0_attn_w_k": [64, D], "L0_attn_w_v": [64, D],
    "L0_attn_w_out": [D, D], "L0_attn_softmax_scale": [1],
    "L0_ffn_g1": [D], "L0_ffn_g2": [D],
    "L0_ffn_wq": [D, D], "L0_ffn_wk": [D, 256], "L0_ffn_wv": [D, 256],
    "L0_ffn_wo": [D, D], "L0_ffn_router_w": [D, 8], "L0_ffn_router_b": [8],
    "L0_ffn_w1": [8, D, 2048], "L0_ffn_w2": [8, D, 2048], "L0_ffn_w3": [8, 2048, D],
    "L1_attn_Wq": [D, D], "L1_attn_bq": [D], "L1_attn_Wk": [D, D], "L1_attn_bk": [D],
    "L1_attn_Wv": [D, D], "L1_attn_bv": [D], "L1_attn_Wp": [D, D], "L1_attn_bp": [D],
    "L1_ffn_Wm": [D, D], "L1_ffn_bm": [D], "L1_ffn_expert_emb": [128, 16],
    "L1_ffn_W1": [16, 128, 512], "L1_ffn_b1": [16, 512],
    "L1_ffn_W2": [16, 512, 128], "L1_ffn_b2": [16, 128],
    "L1_ffn_Wo": [D, D], "L1_ffn_bo": [D],
}

BLOCK_WEIGHTS = {
    "A": ["L0_attn_w_q", "L0_attn_w_k", "L0_attn_w_v", "L0_attn_w_out",
          "L0_attn_softmax_scale"],
    "B": ["L0_ffn_g1", "L0_ffn_g2", "L0_ffn_wq", "L0_ffn_wk", "L0_ffn_wv",
          "L0_ffn_wo", "L0_ffn_router_w", "L0_ffn_router_b", "L0_ffn_w1",
          "L0_ffn_w2", "L0_ffn_w3"],
    "C": ["L1_attn_Wq", "L1_attn_bq", "L1_attn_Wk", "L1_attn_bk", "L1_attn_Wv",
          "L1_attn_bv", "L1_attn_Wp", "L1_attn_bp"],
    "D": ["L1_ffn_Wm", "L1_ffn_bm", "L1_ffn_expert_emb", "L1_ffn_W1",
          "L1_ffn_b1", "L1_ffn_W2", "L1_ffn_b2", "L1_ffn_Wo", "L1_ffn_bo"],
}



def bk(ctx, i, w=TQ, p0=0, p=128):
    return ctx.t["PSUM"][p0:p0 + p, 512 * i:512 * i + w]


def bkspan(ctx, i, nb):
    return ctx.t["PSUM"][:, 512 * i:512 * (i + nb)]


def xsl(ctx, c, tqc, w=TQ, start=0):
    return ctx.t["X"][:, T * c + TQ * tqc + start: T * c + TQ * tqc + start + w]


def mm_chain(ctx, em, psum_view, lhsT_tiles, rhs_views, after=(), tag="mm",
             extra_first=None):
    def fn(t):
        inst = None
        if extra_first is not None:
            inst = extra_first(t)
        n = len(lhsT_tiles)
        for i, (lt, rv) in enumerate(zip(lhsT_tiles, rhs_views)):
            inst = t.matmul(psum_view, lt, rv,
                            start=(i == 0 and extra_first is None),
                            stop=(i == n - 1))
        return inst
    return em.op("tensor", fn, after=after, tag=tag)


def emit_weight_load_natural(ctx, em, wname, sbname, after=(), sub=None):
    src = ctx.dram[wname] if sub is None else sub
    rows, cols = src.shape
    dst = ctx.t[sbname]
    rp = min(128, rows)
    nro = max(rows // 128, 1)
    ap = bass.AP(src.tensor, src.offset, [[cols, rp], [rp * cols, nro], [1, cols]])
    return em.dma("gpsimd", dst[0:rp, 0:nro * cols],
                  ap, after=after, tag=f"wload:{wname}")


def emit_weight_load_transposed(ctx, em, wname, sbname, rows, cols, after=(),
                                psb_after=None):
    raw = ctx.t[sbname + "_raw"]
    dst = ctx.t[sbname]
    nro = max(rows // 128, 1)
    rp = min(rows, 128)
    nci = cols // 128
    src_ap = ctx.dram[wname]
    halves = 2 if nci > 4 else 1
    cph = cols // halves
    ncih = nci // halves
    evs = []
    for hf in range(halves):
        ap = bass.AP(src_ap.tensor, cph * hf,
                     [[cols, rp], [rp * cols, nro], [1, cph]])
        dep = list(after) + (evs[-1:] if evs else [])
        ld = em.dma("gpsimd", raw[0:rp, 0:nro * cph], ap, after=dep,
                    tag=f"wload:{wname}:{hf}")
        for cl in range(ncih):
            ci = hf * ncih + cl
            trs = []
            for ro in range(nro):
                def tfn(t, cl=cl, ro=ro):
                    return t.transpose(
                        ctx.t["PSB"][:, rp * ro:rp * ro + rp],
                        raw[0:rp, cph * ro + 128 * cl: cph * ro + 128 * cl + 128],
                        ctx.t["EYEB"][0:rp, 0:rp])
                dep = [ld] + (evs[-1:] if evs else ([psb_after] if psb_after else []))
                trs.append(em.op("tensor", tfn, after=dep, tag="wtr"))
            def efn(v, ci=ci, nro=nro, rp=rp):
                return v.tensor_copy(dst[:, rp * nro * ci: rp * nro * (ci + 1)],
                                     ctx.t["PSB"][:, 0:rp * nro])
            evs.append(em.op("vector", efn, after=[trs[-1]], tag="wtr_ev"))
    return evs[-1]


def emit_ln_update(ctx, em, tqc, pre, deps, sc, src_bf16=False):
    stat_ops = []
    mean_last = None
    msq_last = None
    for c in range(C):
        psrc = pre(c)
        if src_bf16:
            bsrc = psrc
            cp = None
        else:
            cp = em.op("scalar", lambda s, psrc=psrc: s.copy(
                sc["PREBC"][:, 0:TQ], psrc), after=deps, tag="ln_cp")
            bsrc = sc["PREBC"][:, 0:TQ]
        sqo = em.op("scalar", lambda s, c=c, bsrc=bsrc: s.activation(
            sc["SQBC"][:, 0:TQ], bsrc, AF.Square), after=[cp] if cp else deps,
            tag="ln_sq")
        mean_last = em.op("tensor", lambda t, c=c, bsrc=bsrc: t.matmul(
            bk(ctx, 4, p=1), ctx.t["MEANCB"][:], bsrc,
            start=(c == 0), stop=(c == C - 1)),
            after=[cp if cp else None, sqo] + (deps if c == 0 else []),
            tag="ln_mean")
        msq_last = em.op("tensor", lambda t, c=c: t.matmul(
            bk(ctx, 5, p=1), ctx.t["MEANCB"][:], sc["SQBC"][:, 0:TQ],
            start=(c == 0), stop=(c == C - 1)), after=[sqo], tag="ln_msq")
        deps = deps + [msq_last] if src_bf16 else [mean_last, msq_last]
    r_m2 = em.op("scalar", lambda s: s.activation(
        sc["ROWA"][0:1, 0:TQ], bk(ctx, 4, p=1), AF.Square),
        after=[mean_last], tag="ln_m2")
    r_var = em.op("vector", lambda v: v.scalar_tensor_tensor(
        sc["ROWB"][0:1, 0:TQ], bk(ctx, 5, p=1), 1e-5, sc["ROWA"][0:1, 0:TQ],
        ALU.add, ALU.subtract), after=[msq_last, r_m2], tag="ln_var")
    r_lv = em.op("scalar", lambda s: s.activation(
        sc["ROWA"][0:1, 0:TQ], sc["ROWB"][0:1, 0:TQ], AF.Ln),
        after=[r_var], tag="ln_lv")
    r_rstd = em.op("scalar", lambda s: s.activation(
        sc["RSTDB"][0:1, 0:TQ], sc["ROWA"][0:1, 0:TQ], AF.Exp, scale=-0.5),
        after=[r_lv], tag="ln_rstd")
    r_mr = em.op("vector", lambda v: v.tensor_tensor(
        sc["MRB"][0:1, 0:TQ], bk(ctx, 4, p=1), sc["RSTDB"][0:1, 0:TQ], ALU.mult),
        after=[mean_last, r_rstd], tag="ln_mr")
    b_r = em.op("tensor", lambda t: t.matmul(
        bk(ctx, 2), ctx.t["ONESR"][:], sc["RSTDB"][0:1, 0:TQ], start=True, stop=True),
        after=[r_rstd], tag="ln_bcr")
    b_mr = em.op("tensor", lambda t: t.matmul(
        bk(ctx, 3), ctx.t["ONESR"][:], sc["MRB"][0:1, 0:TQ], start=True, stop=True),
        after=[r_mr], tag="ln_bcmr")
    outs = []
    for c in range(C):
        psrc = pre(c)
        o1 = em.op("vector", lambda v, psrc=psrc: v.tensor_tensor(
            sc["APPA"][:, 0:TQ], psrc, bk(ctx, 2), ALU.mult),
            after=[b_r] + (outs[-1:] if outs else []), tag="ln_ap1")
        o2 = em.op("vector", lambda v, c=c: v.tensor_tensor(
            xsl(ctx, c, tqc), sc["APPA"][:, 0:TQ], bk(ctx, 3), ALU.subtract),
            after=[o1, b_mr], tag="ln_ap2")
        outs.append(o2)
    return outs



def emit_load_constants_and_x(ctx):
    nc = ctx.nc
    lst = ExitStack()
    for i in range(2):
        ctx.sb(f"XT{i}", [128, D], F32, lst)
        ctx.sb(f"XB{i}", [128, D], BF16, lst)
    em = Em(ctx, "init")
    lds = []
    for cname in CONST_SHAPES:
        lds.append(em.dma("sync", ctx.t[cname][:], ctx.dram[cname],
                          tag=f"c:{cname}"))
    xt = [ctx.t[f"XT{i}"] for i in range(2)]
    xb = [ctx.t[f"XB{i}"] for i in range(2)]
    prev_tr = [None, None]
    prev_ev = None
    eyeld = lds[0]
    for tt in range(NT):
        b = tt % 2
        ld = em.dma("sync", xt[b][:], ctx.dram["xin"][128 * tt:128 * (tt + 1), :],
                    after=[prev_tr[b]], tag=f"xld{tt}")
        cast = em.op("vector", lambda v, b=b: v.tensor_copy(xb[b][:], xt[b][:]),
                     after=[ld, prev_tr[b]], tag="xcast")
        trs = []
        for c in range(C):
            def tfn(t, b=b, c=c):
                return t.transpose(ctx.t["PSB"][:, 128 * c:128 * (c + 1)],
                                   xb[b][:, 128 * c:128 * (c + 1)],
                                   ctx.t["EYEB"][:])
            trs.append(em.op("tensor", tfn,
                             after=[cast, eyeld, prev_ev], tag="xtr"))
        def efn(v, tt=tt):
            ap = bass.AP(ctx.t["X"], 128 * tt, [[C * T, 128], [T, 8], [1, 128]])
            return v.tensor_copy(ap, ctx.t["PSB"][:, 0:1024])
        ev = em.op("vector", efn, after=[trs[-1]], tag="xev")
        prev_ev = ev
        prev_tr[b] = trs[-1]
    em.emit()
    lst.close()


def emit_store_x(ctx):
    nc = ctx.nc
    lst = ExitStack()
    for i in range(2):
        ctx.sb(f"STG{i}", [128, D], F32, lst)
    em = Em(ctx, "store")
    stg = [ctx.t["STG0"], ctx.t["STG1"]]
    prev_dma = [None, None]
    prev_ev = None
    last = None
    for tt in range(NT):
        b = tt % 2
        trs = []
        for c in range(C):
            def tfn(t, tt=tt, c=c):
                return t.transpose(
                    ctx.t["PSB"][:, 128 * c:128 * (c + 1)],
                    ctx.t["X"][:, T * c + 128 * tt: T * c + 128 * (tt + 1)],
                    ctx.t["EYEB"][:])
            trs.append(em.op("tensor", tfn, after=[prev_ev], tag="str"))
        ev = em.op("vector",
                   lambda v, b=b: v.tensor_copy(stg[b][:], ctx.t["PSB"][:, 0:1024]),
                   after=[trs[-1], prev_dma[b]], tag="sev")
        dma = em.dma("sync", ctx.dram["xout"][128 * tt:128 * (tt + 1), :],
                     stg[b][:], after=[ev], tag="sdma")
        prev_dma[b] = dma
        prev_ev = ev
        last = dma
    em.op("gpsimd", lambda g: None, after=[last], tag="final_wait")
    em.emit()
    lst.close()


def emit_taps(ctx, em, tname, src_ap, after):
    if tname in ctx.taps:
        return em.dma("sync", ctx.taps[tname], src_ap, after=after, tag=f"tap:{tname}")
    return None




def emit_attn_sweep(ctx, em, sc, nheads, jn_fn, k2_fn, vp_fn, q2_fn, aon_fn,
                    masked, first_deps, tap_pt=False):
    pv_op = None
    last_exp = None
    prev_n1 = prev_n6 = None
    for h in range(nheads):
        jn = jn_fn(h)
        ngrp = jn // 4
        for jb in range(ngrp):
            mm_ops = []
            for j4 in range(4):
                j = 4 * jb + j4
                def sfn(t, h=h, j=j, j4=j4):
                    return t.matmul(bk(ctx, j4), k2_fn(h, j), q2_fn(h),
                                    start=True, stop=True)
                dep = [last_exp]
                if h == 0 and jb == 0:
                    dep = dep + list(first_deps)
                mm_ops.append(em.op("tensor", sfn, after=dep, tag="smm"))
            def efn(s):
                return s.activation(sc["PT"][:, 0:2048], bkspan(ctx, 0, 4), AF.Exp)
            eop = em.op("scalar", efn, after=mm_ops + [pv_op], tag="exp")
            last_exp = eop
            consume = eop
            if masked and jb == ngrp - 1:
                consume = em.op("vector", lambda v: v.tensor_tensor(
                    sc["PT"][:, 0:2048], sc["PT"][:, 0:2048],
                    ctx.t["CMASK"][:, 0:2048], ALU.mult), after=[eop], tag="cmask")
                if tap_pt and h == 0 and "PT0" in ctx.taps:
                    tp = emit_taps(ctx, em, "PT0", sc["PT"][:, 0:2048],
                                   [consume])
                    consume = em.op("vector", lambda v: None, after=[tp, consume],
                                    tag="pt_tap_gate")
            for j4 in range(4):
                j = 4 * jb + j4
                def pvfn(t, h=h, j=j, j4=j4, jn=jn):
                    return t.matmul(bk(ctx, 6, p=65), vp_fn(h, j),
                                    sc["PT"][:, TQ * j4:TQ * (j4 + 1)],
                                    start=(j == 0), stop=(j == jn - 1))
                dep = [consume]
                if j == 0:
                    dep += [prev_n1, prev_n6]
                pv_op = em.op("tensor", pvfn, after=dep, tag="pv")
        n1 = em.op("vector", lambda v: v.tensor_scalar(
            sc["SROW"][0:1, :], bk(ctx, 6, p0=64, p=1), 1e-30, None,
            ALU.max), after=[pv_op], tag="n1")
        n2 = em.op("scalar", lambda s: s.activation(sc["SROW2"][0:1, :],
                                                    sc["SROW"][0:1, :], AF.Ln),
                   after=[n1], tag="n2")
        n3 = em.op("scalar", lambda s: s.activation(sc["RROW"][0:1, :],
                                                    sc["SROW2"][0:1, :],
                                                    AF.Exp, scale=-1.0),
                   after=[n2], tag="n3")
        n4 = em.op("tensor", lambda t: t.matmul(
            bk(ctx, 5, p=64), ctx.t["ONESR"][0:1, 0:64], sc["RROW"][0:1, :],
            start=True, stop=True), after=[n3], tag="n4")
        n5 = em.op("scalar", lambda s: s.copy(sc["RBC"][:], bk(ctx, 5, p=64)),
                   after=[n4], tag="n5")
        def n6fn(v, h=h):
            return v.tensor_tensor(aon_fn(h), bk(ctx, 6, p0=0, p=64),
                                   sc["RBC"][:], ALU.mult)
        n6 = em.op("vector", n6fn, after=[n5, pv_op, n1], tag="n6")
        prev_n1, prev_n6 = n1, n6
    return prev_n1, prev_n6, last_exp


def emit_rms_chunkstats(ctx, em, sc, src_fn, eps, deps):
    msq_last = None
    dd = list(deps)
    for c in range(C):
        src_ap = src_fn(c)
        sqo = em.op("scalar", lambda s, src_ap=src_ap: s.activation(
            sc["SQBC"][:, 0:TQ], src_ap, AF.Square), after=dd, tag="rms_sq")
        msq_last = em.op("tensor", lambda t, c=c: t.matmul(
            bk(ctx, 4, p=1), ctx.t["MEANCB"][:], sc["SQBC"][:, 0:TQ],
            start=(c == 0), stop=(c == C - 1)),
            after=[sqo] + (list(deps) if c == 0 else []), tag="rms_mm")
        dd = [msq_last]
    r_eps = em.op("vector", lambda v: v.tensor_scalar(
        sc["ROWB"][0:1, 0:TQ], bk(ctx, 4, p=1), eps, None, ALU.add),
        after=[msq_last], tag="rms_eps")
    r_ln = em.op("scalar", lambda s: s.activation(
        sc["ROWA"][0:1, 0:TQ], sc["ROWB"][0:1, 0:TQ], AF.Ln),
        after=[r_eps], tag="rms_ln")
    r_rstd = em.op("scalar", lambda s: s.activation(
        sc["RSTDB"][0:1, 0:TQ], sc["ROWA"][0:1, 0:TQ], AF.Exp, scale=-0.5),
        after=[r_ln], tag="rms_rstd")
    return r_rstd


def emit_cnorm(ctx, em, sc, q2r0, q2r1, kc0, kc1, cdst0, cdst1, deps,
               prev_crows):
    sq0 = em.op("scalar", lambda s: s.activation(
        sc["SQS"][0:64, 0:TQ], q2r0, AF.Square), after=deps, tag="cn_sq0")
    sq1 = em.op("scalar", lambda s: s.activation(
        sc["SQS"][64:128, 0:TQ], q2r1, AF.Square), after=deps, tag="cn_sq1")
    nmm = em.op("tensor", lambda t: t.matmul(
        bk(ctx, 4), ctx.t["PAT2B"][:], sc["SQS"][:, 0:TQ],
        start=True, stop=True), after=[sq0, sq1] + list(prev_crows),
        tag="cn_mm")
    s0r = em.op("scalar", lambda s: s.activation(
        sc["SNQ0"][0:1, :], bk(ctx, 4, p=1), AF.Sqrt),
        after=[nmm, prev_crows[0]], tag="cn_s0")
    s1r = em.op("scalar", lambda s: s.activation(
        sc["SNQ1"][0:1, :], bk(ctx, 4, p0=64, p=1), AF.Sqrt),
        after=[nmm, prev_crows[1]], tag="cn_s1")
    c0 = em.op("vector", lambda v: v.tensor_scalar(
        cdst0, sc["SNQ0"][0:1, :], kc0, SHIFT, ALU.mult, ALU.add),
        after=[s0r], tag="cn_c0")
    c1 = em.op("vector", lambda v: v.tensor_scalar(
        cdst1, sc["SNQ1"][0:1, :], kc1, SHIFT, ALU.mult, ALU.add),
        after=[s1r], tag="cn_c1")
    return c0, c1



A_PHASES = "012"


def emit_block_a(ctx):
    nc = ctx.nc
    st = ExitStack()
    ctx.sb("WRAW", [128, C * D // 2], BF16, st)
    ctx.t["WQT_raw"] = ctx.t["WRAW"]
    ctx.t["WKT_raw"] = ctx.t["WRAW"]
    ctx.t["WVT_raw"] = ctx.t["WRAW"]
    ctx.t["WOT_raw"] = ctx.t["WRAW"]
    ctx.sb("WQT", [128, C * D], BF16, st)
    ctx.sb("WKT", [128, C * 64], BF16, st)
    ctx.sb("WVT", [128, C * 64], BF16, st)
    ctx.sb("WOT", [128, C * D], BF16, st)
    ctx.sb("K2", [65, T], BF16, st)
    ctx.sb("VP", [128, NT * 65], BF16, st)
    ctx.sb("VTMP", [64, TQ], BF16, st)
    ctx.sb("KMAX", [1, 1], F32, st)
    ctx.sb("KROW", [1, TQ], F32, st)
    ctx.sb("S0SB", [1, 1], F32, st)
    ctx.sb("S0COL", [128, 1], F32, st)
    ctx.sb("KCSN", [1, 1], F32, st)
    ctx.sb("Q2", [65, 16 * TQ], BF16, st)
    ctx.sb("SQS", [128, TQ], F32, st)
    ctx.sb("SNQ0", [1, TQ], F32, st)
    ctx.sb("SNQ1", [1, TQ], F32, st)
    ctx.sb("PT", [128, 2048], BF16, st)
    ctx.sb("SROW", [1, TQ], F32, st)
    ctx.sb("SROW2", [1, TQ], F32, st)
    ctx.sb("RROW", [1, TQ], BF16, st)
    ctx.sb("RBC", [64, TQ], F32, st)
    ctx.sb("AON", [128, C * TQ], BF16, st)
    ctx.sb("PREF", [128, C * TQ], F32, st)

    sc = ctx.t

    em = Em(ctx, "A0")
    wq = emit_weight_load_transposed(ctx, em, "L0_attn_w_q", "WQT", D, D)
    wk = emit_weight_load_transposed(ctx, em, "L0_attn_w_k", "WKT", 64, D,
                                     after=[wq], psb_after=wq)
    wv = emit_weight_load_transposed(ctx, em, "L0_attn_w_v", "WVT", 64, D,
                                     after=[wk], psb_after=wk)
    wo = emit_weight_load_transposed(ctx, em, "L0_attn_w_out", "WOT", D, D,
                                     after=[wv], psb_after=wv)
    s0ld = em.dma("sync", sc["S0SB"][:],
                  bass.AP(ctx.dram["L0_attn_softmax_scale"].tensor, 0,
                          [[1, 1], [1, 1]]), tag="s0")
    s0mm = em.op("tensor", lambda t: t.matmul(bk(ctx, 6, w=1), ctx.t["ONESRF"][:],
                                              sc["S0SB"][:], start=True, stop=True),
                 after=[s0ld], tag="s0mm")
    s0ev = em.op("vector", lambda v: v.tensor_copy(sc["S0COL"][:], bk(ctx, 6, w=1)),
                 after=[s0mm], tag="s0ev")
    ones_vp = em.op("gpsimd", lambda g: g.memset(sc["VP"][:], 1.0), tag="vp1")
    ones_k2 = em.op("gpsimd", lambda g: g.memset(sc["K2"][64:65, :], 1.0), tag="k21")
    em.emit()

    if "1" not in A_PHASES:
        st.close()
        return
    em = Em(ctx, "A1")
    kmax_ops = []
    km0 = em.op("gpsimd", lambda g: g.memset(sc["KMAX"][:], 0.0), tag="km0")
    prev = {}
    for tqc in range(NTQ):
        kmm = mm_chain(ctx, em, bk(ctx, 0, p=64),
                       [sc["WKT"][:, 64 * ci:64 * (ci + 1)] for ci in range(C)],
                       [xsl(ctx, ci, tqc) for ci in range(C)],
                       after=[prev.get("kev"), prev.get("ksq")], tag="kmm")
        kev = em.op("scalar", lambda s, tqc=tqc: s.copy(
            sc["K2"][0:64, TQ * tqc:TQ * (tqc + 1)], bk(ctx, 0, p=64)),
            after=[kmm], tag="kev")
        ksq = em.op("scalar", lambda s: s.activation(
            sc["SQS"][0:64, 0:TQ], bk(ctx, 0, p=64), AF.Square), after=[kmm], tag="ksq")
        knm = em.op("tensor", lambda t: t.matmul(
            bk(ctx, 1, p=1), ctx.t["ONESC"][0:64, :], sc["SQS"][0:64, 0:TQ],
            start=True, stop=True), after=[ksq, prev.get("kred")], tag="knm")
        kred = em.op("vector", lambda v: v.tensor_reduce(
            sc["KROW"][0:1, 0:1], bk(ctx, 1, p=1), AX.X, ALU.max),
            after=[knm, prev.get("kacc")], tag="kred")
        kacc = em.op("vector", lambda v: v.tensor_tensor(
            sc["KMAX"][:], sc["KMAX"][:], sc["KROW"][0:1, 0:1], ALU.max),
            after=[kred, km0], tag="kacc")
        vmm = mm_chain(ctx, em, bk(ctx, 2, p=64),
                       [sc["WVT"][:, 64 * ci:64 * (ci + 1)] for ci in range(C)],
                       [xsl(ctx, ci, tqc) for ci in range(C)],
                       after=[prev.get("vev")], tag="vmm")
        vev = em.op("scalar", lambda s: s.copy(sc["VTMP"][:], bk(ctx, 2, p=64)),
                    after=[vmm], tag="vev")
        vpe = prev.get("vpe")
        for j in range(4):
            def tfn(t, j=j, tqc=tqc):
                return t.transpose(ctx.t["PSB"][:, 64 * j:64 * (j + 1)],
                                   sc["VTMP"][0:64, 128 * j:128 * (j + 1)],
                                   ctx.t["EYEB"][0:64, 0:64])
            vtr = em.op("tensor", tfn, after=[vev, vpe], tag="vtr")
            def efn(v, j=j, tqc=tqc):
                jj = 4 * tqc + j
                return v.tensor_copy(sc["VP"][:, 65 * jj:65 * jj + 64],
                                     ctx.t["PSB"][:, 64 * j:64 * (j + 1)])
            vpe = em.op("vector", efn, after=[vtr, ones_vp], tag="vpe")
        prev = dict(kev=kev, ksq=ksq, kred=kred, vev=vev, vpe=vpe, kacc=kacc)
        kmax_ops.append(kacc)
    kroot = em.op("scalar", lambda s: s.activation(sc["KROW"][0:1, 0:1],
                                                   sc["KMAX"][:], AF.Sqrt),
                  after=[kmax_ops[-1]], tag="kroot")
    kcsn = em.op("vector", lambda v: v.tensor_scalar(
        sc["KCSN"][:], sc["KROW"][0:1, 0:1], sc["S0SB"][:], -1.0,
        ALU.mult, ALU.mult), after=[kroot, s0ev], tag="kcsn")
    em.emit()

    if "2" not in A_PHASES and "q" not in A_PHASES and "a" not in A_PHASES \
            and "f" not in A_PHASES:
        st.close()
        return
    for tqc in range(NTQ):
        em = Em(ctx, f"A2_{tqc}")
        qdeps = []
        bank_readers = {0: [], 1: [], 4: [], 5: []}
        prev_c0 = prev_c1 = None
        for ro in range(C):
            b = ro % 2
            qmm = mm_chain(
                ctx, em, bk(ctx, b),
                [sc["WQT"][:, D * ci + 128 * ro: D * ci + 128 * (ro + 1)]
                 for ci in range(C)],
                [xsl(ctx, ci, tqc) for ci in range(C)],
                after=bank_readers[b], tag=f"qmm{ro}")
            bank_readers[b] = []
            ev0 = em.op("vector", lambda v, ro=ro, b=b: v.tensor_scalar(
                sc["Q2"][0:64, TQ * 2 * ro:TQ * (2 * ro + 1)],
                bk(ctx, b, p0=0, p=64), sc["S0COL"][0:64, :], None, ALU.mult),
                after=[qmm], tag="qev0")
            ev1 = em.op("vector", lambda v, ro=ro, b=b: v.tensor_scalar(
                sc["Q2"][0:64, TQ * (2 * ro + 1):TQ * (2 * ro + 2)],
                bk(ctx, b, p0=64, p=64), sc["S0COL"][64:128, :], None, ALU.mult),
                after=[qmm], tag="qev1")
            if "n" in A_PHASES:
                bank_readers[b] = [ev0, ev1]
                qdeps += [ev0, ev1]
                continue
            sq = em.op("scalar", lambda s, b=b: s.activation(
                sc["SQS"][:], bk(ctx, b), AF.Square), after=[qmm, ev0, ev1],
                tag="qsq")
            bank_readers[b] = [ev0, ev1, sq]
            nb = 4 + b
            if "x" in A_PHASES:
                bank_readers[nb] = []
                qdeps += [ev0, ev1, sq]
                continue
            nmm = em.op("tensor", lambda t, b=b, nb=nb: t.matmul(
                bk(ctx, nb), ctx.t["PAT2F"][:], sc["SQS"][:],
                start=True, stop=True), after=[sq] + bank_readers[nb],
                tag="qnmm")
            if "m" in A_PHASES:
                bank_readers[nb] = []
                qdeps += [ev0, ev1]
                ctx._nmmlast = nmm
                continue
            s0r = em.op("scalar", lambda s, nb=nb: s.activation(
                sc["SNQ0"][0:1, :], bk(ctx, nb, p=1), AF.Sqrt),
                after=[nmm, prev_c0], tag="qsn0")
            s1r = em.op("scalar", lambda s, nb=nb: s.activation(
                sc["SNQ1"][0:1, :], bk(ctx, nb, p0=64, p=1), AF.Sqrt),
                after=[nmm, prev_c1], tag="qsn1")
            bank_readers[nb] = [s0r, s1r]
            if "s" in A_PHASES:
                bank_readers[nb] = [s0r, s1r]
                qdeps += [ev0, ev1, s0r, s1r]
                continue
            c0 = em.op("vector", lambda v, ro=ro: v.tensor_scalar(
                sc["Q2"][64:65, TQ * 2 * ro:TQ * (2 * ro + 1)],
                sc["SNQ0"][0:1, :], sc["KCSN"][:], SHIFT, ALU.mult, ALU.add),
                after=[s0r], tag="qc0")
            c1 = em.op("vector", lambda v, ro=ro: v.tensor_scalar(
                sc["Q2"][64:65, TQ * (2 * ro + 1):TQ * (2 * ro + 2)],
                sc["SNQ1"][0:1, :], sc["KCSN"][:], SHIFT, ALU.mult, ALU.add),
                after=[s1r], tag="qc1")
            qdeps += [ev0, ev1, c0, c1]
            prev_c0, prev_c1 = c0, c1
        qdeps += bank_readers[0] + bank_readers[1] + bank_readers[4] + bank_readers[5]

        if "q" in A_PHASES and "a" not in A_PHASES and "f" not in A_PHASES:
            em.emit()
            continue
        pv_op = None
        last_exp = None
        prev_n1 = None
        prev_n6 = None
        for h in range(16):
            first_mm_dep = qdeps + [prev_n1, prev_n6]
            for jb in range(8):
                mm_ops = []
                for j4 in range(4):
                    j = 4 * jb + j4
                    def sfn(t, j=j, j4=j4, h=h):
                        return t.matmul(
                            bk(ctx, j4),
                            sc["K2"][:, 128 * j:128 * (j + 1)],
                            sc["Q2"][:, TQ * h:TQ * (h + 1)],
                            start=True, stop=True)
                    dep = [last_exp]
                    if h == 0 and jb == 0:
                        dep = dep + first_mm_dep
                    mm_ops.append(em.op("tensor", sfn, after=dep, tag="smm"))
                pb = jb % 2
                def efn(s, pb=pb):
                    return s.activation(sc["PT"][:, 0:2048],
                                        bkspan(ctx, 0, 4), AF.Exp)
                eop = em.op("scalar", efn, after=mm_ops + [pv_op], tag="exp")
                last_exp = eop
                for j4 in range(4):
                    j = 4 * jb + j4
                    def pvfn(t, j=j, j4=j4, pb=pb):
                        return t.matmul(
                            bk(ctx, 6, p=65),
                            sc["VP"][:, 65 * j:65 * (j + 1)],
                            sc["PT"][:, TQ * j4:TQ * (j4 + 1)],
                            start=(j == 0), stop=(j == NT - 1))
                    dep = [eop]
                    if j == 0:
                        dep += [prev_n1, prev_n6]
                    pv_op = em.op("tensor", pvfn, after=dep, tag="pv")
            n1 = em.op("vector", lambda v: v.tensor_scalar(
                sc["SROW"][0:1, :], bk(ctx, 6, p0=64, p=1), 1e-30, None,
                ALU.max), after=[pv_op], tag="n1")
            n2 = em.op("scalar", lambda s: s.activation(sc["SROW2"][0:1, :],
                                                        sc["SROW"][0:1, :], AF.Ln),
                       after=[n1], tag="n2")
            n3 = em.op("scalar", lambda s: s.activation(sc["RROW"][0:1, :],
                                                        sc["SROW2"][0:1, :],
                                                        AF.Exp, scale=-1.0),
                       after=[n2], tag="n3")
            n4 = em.op("tensor", lambda t: t.matmul(
                bk(ctx, 5, p=64), ctx.t["ONESR"][0:1, 0:64], sc["RROW"][0:1, :],
                start=True, stop=True), after=[n3], tag="n4")
            n5 = em.op("scalar", lambda s: s.copy(sc["RBC"][:], bk(ctx, 5, p=64)),
                       after=[n4], tag="n5")
            def n6fn(v, h=h):
                ro, half = h // 2, h % 2
                return v.tensor_tensor(
                    sc["AON"][64 * half:64 * half + 64, TQ * ro:TQ * (ro + 1)],
                    bk(ctx, 6, p0=0, p=64), sc["RBC"][:], ALU.mult)
            n6 = em.op("vector", n6fn, after=[n5, pv_op], tag="n6")
            prev_n1, prev_n6 = n1, n6

        if "a" in A_PHASES and "f" not in A_PHASES:
            em.emit()
            continue
        pre_deps = []
        for co in range(C):
            b = co % 2
            dep = [prev_n6, last_exp]
            if len(pre_deps) >= 2:
                dep.append(pre_deps[-2])
            omm = mm_chain(
                ctx, em, bk(ctx, b),
                [sc["WOT"][:, D * ci + 128 * co: D * ci + 128 * (co + 1)]
                 for ci in range(C)],
                [sc["AON"][:, TQ * ci:TQ * (ci + 1)] for ci in range(C)],
                after=dep, tag=f"omm{co}")
            r1 = em.op("vector", lambda v, co=co, b=b: v.tensor_tensor(
                sc["PREF"][:, TQ * co:TQ * (co + 1)], bk(ctx, b),
                xsl(ctx, co, tqc), ALU.add), after=[omm], tag="resadd")
            pre_deps.append(r1)
        emit_ln_update(ctx, em, tqc,
                       lambda c: sc["PREF"][:, TQ * c:TQ * (c + 1)],
                       pre_deps + [prev_n1], sc)
        em.emit()
    st.close()



def build_program(blocks, core):
    nc = bass.Bass("TRN2", target_bir_lowering=False)
    stack = ExitStack()
    ctx = Ctx(nc, stack)
    ctx.core = core
    ctx.blocks = blocks
    ctx.dram = {}

    ctx.dram["xin"] = nc.dram_tensor("xin", [T, D], F32, kind="ExternalInput").ap()
    ctx.dram["xout"] = nc.dram_tensor("xout", [T, D], F32, kind="ExternalOutput").ap()
    input_names = ["xin"]
    for cname, (shape, dt) in CONST_SHAPES.items():
        ctx.dram[cname] = nc.dram_tensor(cname, shape, dt, kind="ExternalInput").ap()
        input_names.append(cname)
    tabs = []
    if "B" in blocks:
        tabs += ["COS0", "SIN0"]
    if "C" in blocks:
        tabs += ["COS1", "SIN1"]
    if "B" in blocks or "C" in blocks:
        tabs += ["CMASK"]
    for cname in tabs:
        shape, dt = TABLE_SHAPES[cname]
        ctx.dram[cname] = nc.dram_tensor(cname, shape, dt, kind="ExternalInput").ap()
        input_names.append(cname)
    needed = []
    for b in blocks:
        needed += BLOCK_WEIGHTS[b]
    for wname in needed:
        ctx.dram[wname] = nc.dram_tensor(wname, WEIGHTS[wname], F32,
                                         kind="ExternalInput").ap()
        input_names.append(wname)
    ctx.taps = {}
    for tname, shape, dt in DEBUG_TAPS:
        ctx.taps[tname] = nc.dram_tensor(tname, shape, dt, kind="ExternalOutput").ap()
    ctx.dram["rowscratch"] = nc.dram_tensor("rowscratch", [32, 128], F32).ap()
    if "B" in blocks:
        ctx.dram["cc_in"] = nc.dram_tensor("cc_in", [8, 2], F32).ap()
        ctx.dram["cc_out"] = nc.dram_tensor("cc_out", [64, 2], F32).ap()

    ctx.sb("X", [128, C * T], BF16)
    ctx.sb("PREBC", [128, TQ], BF16)
    ctx.sb("SQBC", [128, TQ], BF16)
    ctx.sb("ROWA", [1, TQ], F32)
    ctx.sb("ROWB", [1, TQ], F32)
    ctx.sb("RSTDB", [1, TQ], BF16)
    ctx.sb("MRB", [1, TQ], BF16)
    ctx.sb("APPA", [128, TQ], F32)
    ctx.t["PSUM"] = stack.enter_context(nc.psum_tensor("PSUM", [128, 3584], F32))
    ctx.t["PSB"] = stack.enter_context(nc.psum_tensor("PSB", [128, 1024], BF16))
    for cname, (shape, dt) in CONST_SHAPES.items():
        ctx.sb(cname, shape, dt)
    names = ["Dcc"]
    for e in ENGINES:
        names.append("E" + e)
        for ch in range(DMA_CHANNELS[e]):
            names.append(f"D{e}{ch}")
    ctx.sems = Sems(nc, stack, names)
    ctx.dma_rr = {e: 0 for e in ENGINES}

    emit_load_constants_and_x(ctx)
    for b in blocks:
        if b == "A":
            emit_block_a(ctx)
        elif b == "B":
            emit_block_b(ctx)
        elif b == "C":
            emit_block_c(ctx)
        elif b == "D":
            emit_block_d(ctx)
    emit_store_x(ctx)
    stack.close()
    return nc, input_names


B_PHASES = "all"


def emit_block_b(ctx):
    nc = ctx.nc
    st = ExitStack()
    sc = ctx.t
    ctx.sb("G1C", [128, C], F32, st)
    ctx.sb("G2C", [128, C], F32, st)
    ctx.sb("RW", [128, C * 8], F32, st)
    ctx.sb("RB8", [8, 1], F32, st)
    ctx.sb("SCR8", [128, C * T], BF16, st)

    st0 = ExitStack()
    ctx.sb("GROW", [1, D], F32, st0)
    em = Em(ctx, "B0")
    emit_weight_load_natural(ctx, em, "L0_ffn_router_w", "RW")
    em.dma("sync", sc["RB8"][:], bass.AP(ctx.dram["L0_ffn_router_b"].tensor, 0,
                                         [[1, 8], [1, 1]]), tag="rb")
    gprev = None
    for gname, gsb in [("L0_ffn_g1", "G1C"), ("L0_ffn_g2", "G2C")]:
        gld = em.dma("sync", sc["GROW"][:],
                     bass.AP(ctx.dram[gname].tensor, 0, [[D, 1], [1, D]]),
                     after=[gprev], tag=f"g:{gname}")
        gtr = []
        for c in range(C):
            def tfn(t, c=c):
                return t.transpose(ctx.t["PSUM"][:, c:c + 1],
                                   sc["GROW"][0:1, 128 * c:128 * (c + 1)],
                                   ctx.t["EYEF"][0:1, 0:1])
            gtr.append(em.op("tensor", tfn,
                             after=[gld] + (gtr[-1:] or [gprev]), tag="gtr"))
        gprev = em.op("vector", lambda v, gsb=gsb: v.tensor_copy(
            sc[gsb][:, 0:C], ctx.t["PSUM"][:, 0:C]), after=[gtr[-1]], tag="gev")
    em.op("gpsimd", lambda g: g.memset(sc["SCR8"][:], 0.0), tag="scr0")
    em.emit()
    st0.close()

    stA = ExitStack()
    ctx.sb("COS0S", [128, TQ], BF16, stA)
    ctx.sb("SIN0S", [128, TQ], BF16, stA)
    ctx.sb("CMASK", [128, 2048], BF16, stA)
    ctx.sb("K2G", [65, T], BF16, stA)
    ctx.sb("VPG", [128, 65 * NT], BF16, stA)
    ctx.sb("WQG", [128, C * 256], BF16, stA)
    ctx.sb("WOG", [128, 2 * D], BF16, stA)
    ctx.sb("WKG", [128, C * 64], BF16, stA)
    ctx.sb("WVG", [128, C * 64], BF16, stA)
    ctx.sb("XR1C", [128, C * TQ], BF16, stA)
    ctx.sb("QA", [128, TQ], BF16, stA)
    ctx.sb("QAS", [128, TQ], BF16, stA)
    ctx.sb("RT1", [128, TQ], F32, stA)
    ctx.sb("Q2G", [65, 4 * TQ], BF16, stA)
    ctx.sb("SQS", [128, TQ], BF16, stA)
    ctx.t["SNQ0"] = ctx.t["ROWA"]
    ctx.t["SNQ1"] = ctx.t["ROWB"]
    ctx.t["SROW"] = ctx.t["ROWB"]
    ctx.t["SROW2"] = ctx.t["ROWA"]
    ctx.sb("KM4", [1, 4], F32, stA)
    ctx.sb("KC4", [1, 4], F32, stA)
    ctx.sb("PT", [128, 2048], BF16, stA)
    ctx.sb("RROW", [1, TQ], BF16, stA)
    ctx.sb("RBC", [64, TQ], BF16, stA)
    ctx.sb("AONG", [128, 2 * TQ], BF16, stA)

    em = Em(ctx, "Bpre")
    em.dma("sync", sc["CMASK"][:], ctx.dram["CMASK"], tag="cmask")
    em.op("gpsimd", lambda g: g.memset(sc["K2G"][64:65, :], 1.0), tag="k2g1")
    em.op("gpsimd", lambda g: g.memset(sc["VPG"][:], 1.0), tag="vpg1")
    em.op("gpsimd", lambda g: g.memset(sc["KM4"][:], 0.0), tag="km4")
    em.emit()

    def rope_pair(em, psum_bank, tq0, dst0, dst1, deps, scale, prev, p=128,
                  extra_ev=()):
        ev = em.op("scalar", lambda s: s.activation(
            sc["QA"][0:p, 0:TQ], bk(ctx, psum_bank, p=p), AF.Copy, scale=scale),
            after=deps + [prev.get("t1"), prev.get("rot")] + list(extra_ev),
            tag="rp_ev")
        rot = em.op("tensor", lambda t: t.matmul(
            bk(ctx, 3, p=p), ctx.t["PN"][0:p, 0:p], sc["QA"][0:p, 0:TQ],
            start=True, stop=True), after=[ev, prev.get("t2")], tag="rp_rot")
        t1 = em.op("vector", lambda v: v.tensor_tensor(
            sc["RT1"][0:p, 0:TQ], sc["QA"][0:p, 0:TQ],
            sc["COS0S"][0:p, 0:TQ], ALU.mult),
            after=[ev, prev.get("o0"), prev.get("o1"), prev.get("tbl")],
            tag="rp_t1")
        t2 = em.op("vector", lambda v: v.tensor_tensor(
            sc["QAS"][0:p, 0:TQ], bk(ctx, 3, p=p),
            sc["SIN0S"][0:p, 0:TQ], ALU.mult),
            after=[rot, prev.get("o0"), prev.get("o1"), prev.get("tbl")],
            tag="rp_t2")
        o0 = em.op("vector", lambda v: v.tensor_tensor(
            dst0, sc["RT1"][0:64, 0:TQ], sc["QAS"][0:64, 0:TQ], ALU.add),
            after=[t1, t2], tag="rp_o0")
        o1 = o0
        if p == 128:
            o1 = em.op("vector", lambda v: v.tensor_tensor(
                dst1, sc["RT1"][64:128, 0:TQ], sc["QAS"][64:128, 0:TQ],
                ALU.add), after=[t1, t2], tag="rp_o1")
        return dict(ev=ev, rot=rot, t1=t1, t2=t2, o0=o0, o1=o1)

    wq_t = ctx.dram["L0_ffn_wq"].tensor
    wo_t = ctx.dram["L0_ffn_wo"].tensor
    wk_t = ctx.dram["L0_ffn_wk"].tensor
    wv_t = ctx.dram["L0_ffn_wv"].tensor

    for gp in range(4):
        em = Em(ctx, f"Bw_{gp}")
        for hl in range(4):
            em.dma("gpsimd",
                   bass.AP(sc["WQG"], 64 * hl, [[C * 256, 128], [256, C], [1, 64]]),
                   bass.AP(wq_t, 64 * (gp + 4 * hl),
                           [[D, 128], [128 * D, C], [1, 64]]),
                   tag=f"wqg{hl}")
        for hl in range(4):
            em.dma("gpsimd",
                   sc["WOG"][64 * (hl % 2):64 * (hl % 2) + 64,
                             D * (hl // 2):D * (hl // 2) + D],
                   bass.AP(wo_t, 64 * (gp + 4 * hl) * D,
                           [[D, 64], [1, 1], [1, D]]),
                   tag=f"wog{hl}")
        em.dma("gpsimd", sc["WKG"][:, 0:C * 64],
               bass.AP(wk_t, 64 * gp, [[256, 128], [128 * 256, C], [1, 64]]),
               tag="wkg")
        em.dma("gpsimd", sc["WVG"][:, 0:C * 64],
               bass.AP(wv_t, 64 * gp, [[256, 128], [128 * 256, C], [1, 64]]),
               tag="wvg")
        em.emit()

        em = Em(ctx, f"Bkv_{gp}")
        prevb = {}
        rp = {}
        kn_prev = None
        for tqc in range(NTQ):
            tq0 = TQ * tqc
            tl0 = em.dma("sync", sc["COS0S"][:],
                         ctx.dram["COS0"][:, tq0:tq0 + TQ],
                         after=[rp.get("t1")], tag="tbl_c")
            tl1 = em.dma("sync", sc["SIN0S"][:],
                         ctx.dram["SIN0"][:, tq0:tq0 + TQ],
                         after=[rp.get("t2")], tag="tbl_s")
            tbl = em.op("sync", lambda e: None, after=[tl0, tl1], tag="tblw")
            rp["tbl"] = tbl
            rstd = emit_rms_chunkstats(
                ctx, em, sc, lambda c: xsl(ctx, c, tqc), 1e-6,
                [prevb.get("xr_last")])
            bcm = em.op("tensor", lambda t: t.matmul(
                bk(ctx, 2), ctx.t["ONESR"][:], sc["RSTDB"][0:1, 0:TQ],
                start=True, stop=True), after=[rstd, prevb.get("xr_last")],
                tag="b1_bc")
            xrs = []
            for c in range(C):
                xin = xsl(ctx, c, tqc)
                xrs.append(em.op("vector", lambda v, c=c, xin=xin: v.scalar_tensor_tensor(
                    sc["XR1C"][:, TQ * c:TQ * (c + 1)], xin,
                    sc["G1C"][:, c:c + 1], bk(ctx, 2), ALU.mult, ALU.mult),
                    after=[bcm] + xrs[-1:], tag="b1_xr"))
            if gp == 0 and tqc == 0 and "XR1C" in ctx.taps:
                tp1 = emit_taps(ctx, em, "XR1C", sc["XR1C"][:, 0:C * TQ], xrs)
                tp2 = emit_taps(ctx, em, "G1CT", sc["G1C"][:, 0:C], xrs)
                tp3 = emit_taps(ctx, em, "WKGT", sc["WKG"][:, 0:C * 64], xrs)
                tp4 = emit_taps(ctx, em, "RSTDT", sc["RSTDB"][0:1, 0:TQ], xrs)
                tp5 = emit_taps(ctx, em, "XTAP", ctx.t["X"][:, 0:TQ], xrs)
                xrs = xrs + [em.op("vector", lambda v: None,
                                   after=[tp1, tp2, tp3, tp4, tp5, xrs[-1]],
                                   tag="tapgate")]
            kmm = mm_chain(ctx, em, bk(ctx, 0, p=64),
                           [sc["WKG"][:, 64 * ci:64 * (ci + 1)]
                            for ci in range(C)],
                           [sc["XR1C"][:, TQ * ci:TQ * (ci + 1)]
                            for ci in range(C)],
                           after=xrs + [rp.get("ev")], tag="kmm")
            rp["tbl"] = tbl
            rp = rope_pair(em, 0, tq0,
                           sc["K2G"][0:64, tq0:tq0 + TQ], None,
                           [kmm], 1.0, rp, p=64,
                           extra_ev=[prevb.get("vpe")])
            qq = em.op("scalar", lambda s, tq0=tq0: s.activation(
                sc["SQS"][0:64, 0:TQ], sc["K2G"][0:64, tq0:tq0 + TQ],
                AF.Square), after=[rp["o0"], kn_prev], tag="kn_sq")
            nm = em.op("tensor", lambda t: t.matmul(
                bk(ctx, 5, p=1), ctx.t["ONESCB"][0:64, :],
                sc["SQS"][0:64, 0:TQ], start=True, stop=True),
                after=[qq, kn_prev], tag="kn_mm")
            rd = em.op("vector", lambda v: v.tensor_reduce(
                sc["SNQ0"][0:1, 0:1], bk(ctx, 5, p=1), AX.X, ALU.max),
                after=[nm, kn_prev], tag="kn_rd")
            ka = em.op("vector", lambda v, gp=gp: v.tensor_tensor(
                sc["KM4"][0:1, gp:gp + 1], sc["KM4"][0:1, gp:gp + 1],
                sc["SNQ0"][0:1, 0:1], ALU.max), after=[rd], tag="kn_acc")
            kn_prev = ka
            vmm = mm_chain(ctx, em, bk(ctx, 1, p=64),
                           [sc["WVG"][:, 64 * ci:64 * (ci + 1)]
                            for ci in range(C)],
                           [sc["XR1C"][:, TQ * ci:TQ * (ci + 1)]
                            for ci in range(C)],
                           after=xrs + [prevb.get("vev")], tag="vmm")
            vev = em.op("scalar", lambda s: s.copy(sc["QA"][0:64, 0:TQ],
                                                   bk(ctx, 1, p=64)),
                        after=[vmm, rp.get("t1"), rp.get("rot"),
                               prevb.get("vpe")], tag="vev")
            prevb["vev"] = vev
            vpe = prevb.get("vpe")
            for j in range(4):
                def tfn(t, j=j):
                    return t.transpose(
                        ctx.t["PSB"][:, 64 * j:64 * (j + 1)],
                        sc["QA"][0:64, 128 * j:128 * (j + 1)],
                        ctx.t["EYEB"][0:64, 0:64])
                vtr = em.op("tensor", tfn, after=[vev, vpe], tag="vtr")
                def efn(v, j=j, tqc=tqc):
                    jj = 4 * tqc + j
                    return v.tensor_copy(
                        sc["VPG"][:, 65 * jj:65 * jj + 64],
                        ctx.t["PSB"][:, 64 * j:64 * (j + 1)])
                vpe = em.op("vector", efn, after=[vtr], tag="vpe")
            prevb["vpe"] = vpe
            prevb["xr_last"] = em.op("vector", lambda v: None,
                                     after=[kmm, vmm, xrs[-1]], tag="xr_gate")
        kq = em.op("scalar", lambda s, gp=gp: s.activation(
            sc["KC4"][0:1, gp:gp + 1], sc["KM4"][0:1, gp:gp + 1], AF.Sqrt),
            after=[kn_prev], tag="kq")
        em.op("vector", lambda v, gp=gp: v.tensor_scalar(
            sc["KC4"][0:1, gp:gp + 1], sc["KC4"][0:1, gp:gp + 1], -1.0, None,
            ALU.mult), after=[kq], tag="kneg")
        if B_PHASES == "kv" and "K2GD" in ctx.taps:
            emit_taps(ctx, em, "K2GD", sc["K2G"][0:65, 0:T], [kq, prevb["vpe"]])
            emit_taps(ctx, em, "VPGD", sc["VPG"][:, 0:65 * NT], [kq, prevb["vpe"]])
        em.emit()
        if B_PHASES == "kv":
            stA.close()
            st.close()
            return

        for tqc in range(NTQ):
            tq0 = TQ * tqc
            em = Em(ctx, f"Bq_{gp}_{tqc}")
            tl0 = em.dma("sync", sc["COS0S"][:],
                         ctx.dram["COS0"][:, tq0:tq0 + TQ], tag="tbl_c")
            tl1 = em.dma("sync", sc["SIN0S"][:],
                         ctx.dram["SIN0"][:, tq0:tq0 + TQ], tag="tbl_s")
            tbl = em.op("sync", lambda e: None, after=[tl0, tl1], tag="tblw")
            rstd = emit_rms_chunkstats(ctx, em, sc,
                                       lambda c: xsl(ctx, c, tqc), 1e-6, [])
            bcm = em.op("tensor", lambda t: t.matmul(
                bk(ctx, 2), ctx.t["ONESR"][:], sc["RSTDB"][0:1, 0:TQ],
                start=True, stop=True), after=[rstd], tag="b2_bc")
            xrs = []
            for c in range(C):
                xrs.append(em.op("vector", lambda v, c=c: v.scalar_tensor_tensor(
                    sc["XR1C"][:, TQ * c:TQ * (c + 1)], xsl(ctx, c, tqc),
                    sc["G1C"][:, c:c + 1], bk(ctx, 2), ALU.mult, ALU.mult),
                    after=[bcm] + xrs[-1:], tag="b2_xr"))
            rp2 = {"tbl": tbl}
            crows = [None, None]
            qdeps = []
            prev_rd = []
            for ro2 in range(2):
                qmm = mm_chain(ctx, em, bk(ctx, 0),
                               [sc["WQG"][:, 256 * ci + 128 * ro2:
                                         256 * ci + 128 * (ro2 + 1)]
                                for ci in range(C)],
                               [sc["XR1C"][:, TQ * ci:TQ * (ci + 1)]
                                for ci in range(C)],
                               after=xrs + prev_rd + [rp2.get("ev")],
                               tag=f"b2_qmm{ro2}")
                hl0, hl1 = 2 * ro2, 2 * ro2 + 1
                rp2 = rope_pair(em, 0, tq0,
                                sc["Q2G"][0:64, TQ * hl0:TQ * hl0 + TQ],
                                sc["Q2G"][0:64, TQ * hl1:TQ * hl1 + TQ],
                                [qmm], 0.125, rp2)
                prev_rd = [rp2["ev"]]
                c0, c1 = emit_cnorm(
                    ctx, em, sc,
                    sc["Q2G"][0:64, TQ * hl0:TQ * hl0 + TQ],
                    sc["Q2G"][0:64, TQ * hl1:TQ * hl1 + TQ],
                    sc["KC4"][0:1, gp:gp + 1], sc["KC4"][0:1, gp:gp + 1],
                    sc["Q2G"][64:65, TQ * hl0:TQ * hl0 + TQ],
                    sc["Q2G"][64:65, TQ * hl1:TQ * hl1 + TQ],
                    [rp2["o0"], rp2["o1"]], crows)
                crows = [c0, c1]
                qdeps += [rp2["o0"], rp2["o1"], c0, c1]
            jn = 4 * tqc + 4
            pn1, pn6, lexp = emit_attn_sweep(
                ctx, em, sc, 4, lambda h: jn,
                lambda h, j: sc["K2G"][:, 128 * j:128 * (j + 1)],
                lambda h, j: sc["VPG"][:, 65 * j:65 * (j + 1)],
                lambda h: sc["Q2G"][:, TQ * h:TQ * (h + 1)],
                lambda h: sc["AONG"][64 * (h % 2):64 * (h % 2) + 64,
                                     TQ * (h // 2):TQ * (h // 2) + TQ],
                True, qdeps, tap_pt=(gp == 0 and tqc == 0))
            if gp == 0 and tqc == 0 and "AONG0" in ctx.taps:
                emit_taps(ctx, em, "AONG0", sc["AONG"][:, 0:2 * TQ], [pn6])
                if "VPG0" in ctx.taps:
                    emit_taps(ctx, em, "VPG0", sc["VPG"][:, 0:130], [pn6])
                if "K2G0" in ctx.taps:
                    emit_taps(ctx, em, "K2G0", sc["K2G"][0:65, 0:512], [pn6])
            pre_prev = []
            for co in range(C):
                b = co % 2
                dep = [pn6, lexp] + pre_prev[-2:-1]
                omm = mm_chain(ctx, em, bk(ctx, b),
                               [sc["WOG"][:, D * r + 128 * co:
                                         D * r + 128 * (co + 1)]
                                for r in range(2)],
                               [sc["AONG"][:, TQ * r:TQ * (r + 1)]
                                for r in range(2)],
                               after=dep, tag=f"b2_omm{co}")
                r1 = em.op("vector", lambda v, co=co, b=b, tq0=tq0: v.tensor_tensor(
                    sc["SCR8"][:, T * co + tq0:T * co + tq0 + TQ],
                    sc["SCR8"][:, T * co + tq0:T * co + tq0 + TQ],
                    bk(ctx, b), ALU.add), after=[omm], tag="b2_acc")
                pre_prev.append(r1)
            em.emit()
    stA.close()

    em = Em(ctx, "B3")
    pr = []
    for tqc in range(NTQ):
        for co in range(C):
            tq0 = TQ * tqc
            r1 = em.op("vector", lambda v, co=co, tq0=tq0: v.tensor_tensor(
                sc["SCR8"][:, T * co + tq0:T * co + tq0 + TQ],
                sc["SCR8"][:, T * co + tq0:T * co + tq0 + TQ],
                xsl(ctx, co, tq0 // TQ), ALU.add), after=pr[-1:], tag="b3_h")
            r2 = em.op("vector", lambda v, co=co, tq0=tq0: v.tensor_tensor(
                xsl(ctx, co, tq0 // TQ), xsl(ctx, co, tq0 // TQ),
                sc["SCR8"][:, T * co + tq0:T * co + tq0 + TQ], ALU.add),
                after=[r1], tag="b3_s")
            pr.append(r2)
    em.emit()

    if B_PHASES == "att":
        stR = ExitStack()
        stR.close()
        st.close()
        return
    stR = ExitStack()
    ctx.sb("FE8", [8, T], BF16, stR)
    ctx.sb("WXB", [1, T], BF16, stR)
    st3 = ExitStack()
    ctx.sb("RT1", [128, TQ], F32, st3)
    ctx.sb("RT2", [128, TQ], F32, st3)
    ctx.sb("LT2", [8, T], F32, st3)
    em = Em(ctx, "B4")
    prevc = {}
    for tqc in range(NTQ):
        tq0 = TQ * tqc
        rstd = emit_rms_chunkstats(
            ctx, em, sc,
            lambda c: sc["SCR8"][:, T * c + tq0:T * c + tq0 + TQ], 1e-6,
            [prevc.get("x2_last"), prevc.get("lgev")])
        bcm = em.op("tensor", lambda t: t.matmul(
            bk(ctx, 2), ctx.t["ONESR"][:], sc["RSTDB"][0:1, 0:TQ],
            start=True, stop=True), after=[rstd, prevc.get("x2_last")],
            tag="b4_bc")
        lmms = []
        x2s = []
        for c in range(C):
            par = c % 2
            x2 = em.op("vector", lambda v, c=c, par=par, tq0=tq0: v.scalar_tensor_tensor(
                sc["RT1" if par == 0 else "RT2"][:, 0:TQ],
                sc["SCR8"][:, T * c + tq0:T * c + tq0 + TQ],
                sc["G2C"][:, c:c + 1], bk(ctx, 2), ALU.mult, ALU.mult),
                after=[bcm] + lmms[-2:-1] + ([prevc.get("lgev")] if c == 0 else []),
                tag="b4_x2")
            lmm = em.op("tensor", lambda t, c=c, par=par: t.matmul(
                bk(ctx, 3, p=8), sc["RW"][:, 8 * c:8 * (c + 1)],
                sc["RT1" if par == 0 else "RT2"][:, 0:TQ],
                start=(c == 0), stop=(c == C - 1)),
                after=[x2], tag="b4_lmm")
            lmms.append(lmm)
            x2s.append(x2)
        lgev = em.op("scalar", lambda s, tq0=tq0: s.activation(
            sc["LT2"][0:8, tq0:tq0 + TQ], bk(ctx, 3, p=8), AF.Identity,
            bias=sc["RB8"][:]), after=[lmms[-1]], tag="b4_lgev")
        prevc = dict(lgev=lgev, x2_last=x2s[-1])
        emit_taps(ctx, em, "LT2", sc["LT2"][0:8, :], [lgev])
    em.emit()
    if B_PHASES == "logits":
        st3.close()
        stR.close()
        st.close()
        return
    st3.close()

    st4 = ExitStack()
    st4a = ExitStack()
    ctx.sb("OH0T", [8, T], BF16, st4)
    ctx.sb("OH1T", [8, T], BF16, st4)
    ctx.sb("CUM0", [8, T], I16, st4)
    ctx.sb("CUM1", [8, T], I16, st4)
    ctx.sb("ONES8T", [8, TQ], F32, st4)
    ctx.sb("TOT", [8, 2], F32, st4)
    ctx.sb("GT64", [64, 2], F32, st4)
    ctx.sb("OFF", [8, 2], F32, st4)
    ctx.sb("OFFS", [8, 1], F32, st4)
    ctx.sb("DROW", [1, TQ], BF16, st4)
    ctx.t["RTA"] = ctx.t["ROWA"]
    ctx.t["RTB"] = ctx.t["ROWB"]
    ctx.t["MSLB"] = ctx.t["RSTDB"]
    ctx.t["WYB"] = ctx.t["MRB"]
    ctx.sb("G8S", [8, TQ], BF16, st4)
    ctx.sb("LTT", [128, NT * 8], F32, st4a)
    ctx.sb("V8", [128, 8], F32, st4a)
    ctx.sb("OHS", [128, 8], F32, st4a)
    ctx.sb("DD", [128, NT], F32, st4a)
    ctx.sb("TRS", [32, 128], F32, st4a)

    em = Em(ctx, "B5")
    evs = []
    trp = None
    for tqc in range(NTQ):
        trs = []
        for j in range(4):
            tt = 4 * tqc + j
            def tfn(t, tt=tt, j=j):
                return t.transpose(ctx.t["PSUM"][:, 8 * j:8 * (j + 1)],
                                   sc["LT2"][0:8, 128 * tt:128 * (tt + 1)],
                                   ctx.t["EYEF"][0:8, 0:8])
            trs.append(em.op("tensor", tfn, after=[trp], tag="b5_ltr"))
        ev = em.op("vector", lambda v, tqc=tqc: v.tensor_copy(
            sc["LTT"][:, 32 * tqc:32 * (tqc + 1)], ctx.t["PSUM"][:, 0:32]),
            after=[trs[-1]], tag="b5_lev")
        trp = ev
        evs.append(ev)
    ohdeps = []
    p_oh = {}
    for tt in range(NT):
        lt = sc["LTT"][:, 8 * tt:8 * (tt + 1)]
        mx8 = em.op("vector", lambda v, lt=lt: v.max(out=sc["V8"][:], in_=lt),
                    after=[evs[tt // 4], p_oh.get("dd")] + ohdeps[-1:],
                    tag="b5_max8")
        dd = em.op("vector", lambda v, tt=tt: v.tensor_tensor(
            sc["DD"][:, tt:tt + 1], sc["V8"][:, 1:2], sc["V8"][:, 0:1],
            ALU.subtract), after=[mx8], tag="b5_dd")
        p_oh["dd"] = dd
        for k, dst in ((0, "OH0T"), (1, "OH1T")):
            oh = em.op("vector", lambda v, lt=lt, k=k: v.tensor_scalar(
                sc["OHS"][:], lt, sc["V8"][:, k:k + 1], None, ALU.is_equal),
                after=[mx8, p_oh.get(("ev", k))], tag="b5_oh")
            tr = em.op("tensor", lambda t, k=k: t.transpose(
                ctx.t["PSUM"][0:8, 64 * k:64 * k + 128], sc["OHS"][:],
                ctx.t["EYEF"][:]), after=[oh, p_oh.get("ev_any")],
                tag="b5_ohtr")
            ev = em.op("vector", lambda v, tt=tt, k=k, dst=dst: v.tensor_copy(
                sc[dst][0:8, 128 * tt:128 * (tt + 1)],
                ctx.t["PSUM"][0:8, 64 * k:64 * k + 128]),
                after=[tr], tag="b5_ohev")
            p_oh[("ev", k)] = ev
            p_oh["ev_any"] = ev
            ohdeps.append(ev)
    dtr = em.op("tensor", lambda t: t.transpose(
        ctx.t["PSUM"][0:32, 512:512 + 128], sc["DD"][:], ctx.t["EYEF"][:]),
        after=[p_oh["dd"]], tag="b5_dtr")
    dte = em.op("vector", lambda v: v.tensor_copy(
        sc["TRS"][:], ctx.t["PSUM"][0:32, 512:512 + 128]), after=[dtr],
        tag="b5_dte")
    d_out = em.dma("gpsimd", ctx.dram["rowscratch"], sc["TRS"][:],
                   after=[dte], tag="b5_dout")
    d_in = d_out
    st4a.close()
    ctx.sb("RTC", [1, TQ], F32, st4)
    ctx.sb("RTD", [1, TQ], F32, st4)
    o8 = em.op("gpsimd", lambda g: g.memset(sc["ONES8T"][:], 1.0), tag="b5_o8")
    sc0p = sc1p = None
    for cname in ("CUM0", "CUM1"):
        oht = "OH0T" if cname == "CUM0" else "OH1T"
        prev_scan = sc0p if cname == "CUM1" else None
        for ch in range(NTQ):
            t0c = TQ * ch
            def scfn(v, cname=cname, oht=oht, ch=ch, t0c=t0c):
                init = 0.0 if ch == 0 else sc[cname][0:8, t0c - 1:t0c]
                return v.tensor_tensor_scan(
                    sc[cname][0:8, t0c:t0c + TQ], sc["ONES8T"][:],
                    sc[oht][0:8, t0c:t0c + TQ], init, ALU.mult, ALU.add)
            prev_scan = em.op("vector", scfn,
                             after=[o8, prev_scan] + (ohdeps if ch == 0 else []),
                             tag=f"b5_scan_{cname}_{ch}")
        if cname == "CUM0":
            sc0p = prev_scan
        else:
            sc1p = prev_scan
    tt0 = em.op("vector", lambda v: v.tensor_copy(
        sc["TOT"][:, 0:1], sc["CUM0"][:, T - 1:T]), after=[sc0p], tag="b5_t0")
    tt1 = em.op("vector", lambda v: v.tensor_copy(
        sc["TOT"][:, 1:2], sc["CUM1"][:, T - 1:T]), after=[sc1p], tag="b5_t1")
    if NO_CC:
        offe = em.op("gpsimd", lambda g: g.memset(sc["OFF"][:], 0.0),
                     after=[tt0, tt1], tag="b5_offz")
    else:
        ccout = em.dma("gpsimd", ctx.dram["cc_in"], sc["TOT"][:],
                       after=[tt0, tt1], tag="b5_ccout")
        def ccfn(g):
            return g.collective_compute(
                "AllGather", ALU.bypass, replica_groups=[list(range(8))],
                ins=[ctx.dram["cc_in"].opt()], outs=[ctx.dram["cc_out"].opt()])
        cc = em.op("gpsimd", ccfn, after=[ccout], dma=True, tag="b5_cc")
        cc.cc = True
        ccin = em.dma("gpsimd", sc["GT64"][:], ctx.dram["cc_out"], after=[cc],
                      tag="b5_ccin")
        pmld = em.op("tensor", lambda t: t.matmul(
            ctx.t["PSUM"][0:8, 0:2], ctx.t["PM64"][:], sc["GT64"][:],
            start=True, stop=True), after=[ccin, evs[-1]], tag="b5_offmm")
        offe = em.op("vector", lambda v: v.tensor_copy(sc["OFF"][:],
                                                       ctx.t["PSUM"][0:8, 0:2]),
                     after=[pmld], tag="b5_offev")
    offs = em.op("vector", lambda v: v.tensor_tensor(
        sc["OFFS"][:], sc["OFF"][:, 0:1], sc["OFF"][:, 1:2], ALU.add),
        after=[offe], tag="b5_offs")
    ss = em.op("vector", lambda v: v.tensor_tensor(
        sc["CUM1"][:], sc["CUM0"][:], sc["CUM1"][:], ALU.add),
        after=[sc1p, tt0, tt1], tag="b5_ss")
    p0 = em.op("vector", lambda v: v.tensor_scalar(
        sc["CUM0"][:], sc["CUM0"][:], sc["OFF"][:, 0:1], None, ALU.add),
        after=[ss, offe], tag="b5_p0")
    p1 = em.op("vector", lambda v: v.tensor_scalar(
        sc["CUM1"][:], sc["CUM1"][:], sc["OFFS"][:], None, ALU.add),
        after=[ss, offs], tag="b5_p1")
    ka0 = em.op("vector", lambda v: v.tensor_scalar(
        sc["CUM0"][:], sc["CUM0"][:], float(T), None, ALU.is_lt),
        after=[p0], tag="b5_ka0")
    ka1 = em.op("vector", lambda v: v.tensor_scalar(
        sc["CUM1"][:], sc["CUM1"][:], float(T), None, ALU.is_lt),
        after=[p1], tag="b5_ka1")
    e0 = em.op("vector", lambda v: v.tensor_tensor(
        sc["OH0T"][:], sc["CUM0"][:], sc["OH0T"][:], ALU.mult),
        after=[ka0], tag="b5_em0")
    e1 = em.op("vector", lambda v: v.tensor_tensor(
        sc["OH1T"][:], sc["CUM1"][:], sc["OH1T"][:], ALU.mult),
        after=[ka1], tag="b5_em1")
    pch = {}
    for tqc in range(NTQ):
        tq0 = TQ * tqc
        a0 = em.op("tensor", lambda t, tq0=tq0: t.matmul(
            bk(ctx, 3, p=1), ctx.t["IOTA8B"][:], sc["OH0T"][0:8, tq0:tq0 + TQ],
            start=True, stop=True), after=[e0, pch.get("fe")], tag="b5_as0")
        a1 = em.op("tensor", lambda t, tq0=tq0: t.matmul(
            bk(ctx, 4, p=1), ctx.t["IOTA8B"][:], sc["OH1T"][0:8, tq0:tq0 + TQ],
            start=True, stop=True), after=[e1, pch.get("fe")], tag="b5_as1")
        t0m = em.op("tensor", lambda t, tq0=tq0: t.matmul(
            bk(ctx, 5, p=1), ctx.t["ONES8B"][:], sc["OH0T"][0:8, tq0:tq0 + TQ],
            start=True, stop=True), after=[e0, pch.get("wy1")], tag="b5_tm0")
        t1m = em.op("tensor", lambda t, tq0=tq0: t.matmul(
            bk(ctx, 6, p=1), ctx.t["ONES8B"][:], sc["OH1T"][0:8, tq0:tq0 + TQ],
            start=True, stop=True), after=[e1, pch.get("wy2")], tag="b5_tm1")
        cpa = em.op("scalar", lambda s: s.copy(sc["RTA"][0:1, :],
                                               bk(ctx, 3, p=1)),
                    after=[a0, pch.get("cpb")], tag="b5_cpa")
        mrow = em.op("vector", lambda v: v.tensor_tensor(
            sc["RTB"][0:1, :], bk(ctx, 4, p=1), sc["RTA"][0:1, :], ALU.max),
            after=[a1, cpa], tag="b5_m")
        mrb = em.op("scalar", lambda s: s.activation(
            sc["MSLB"][0:1, :], sc["RTB"][0:1, :], AF.Identity),
            after=[mrow], tag="b5_mb")
        dch = em.dma("gpsimd", sc["DROW"][0:1, 0:TQ],
                     bass.AP(ctx.dram["rowscratch"].tensor, tq0,
                             [[TQ, 1], [1, TQ]]),
                     after=[d_in, pch.get("sg")], tag="b5_dch")
        sg = em.op("scalar", lambda s: s.activation(
            sc["RTC"][0:1, :], sc["DROW"][0:1, 0:TQ], AF.Sigmoid),
            after=[dch, pch.get("wy1")], tag="b5_sg")
        sg0 = em.op("vector", lambda v: v.tensor_scalar(
            sc["RTD"][0:1, :], sc["RTC"][0:1, :], -1.0, 1.0, ALU.mult,
            ALU.add), after=[sg, pch.get("wy1")], tag="b5_sg0")
        cpb = em.op("scalar", lambda s: s.copy(sc["RTA"][0:1, :],
                                               bk(ctx, 5, p=1)),
                    after=[t0m, mrow], tag="b5_cpb")
        wy1 = em.op("vector", lambda v: v.tensor_tensor(
            sc["RTD"][0:1, :], sc["RTD"][0:1, :], sc["RTA"][0:1, :],
            ALU.mult), after=[cpb, sg0], tag="b5_wy1")
        wy2 = em.op("vector", lambda v: v.tensor_tensor(
            sc["RTC"][0:1, :], sc["RTC"][0:1, :], bk(ctx, 6, p=1), ALU.mult),
            after=[t1m, sg, wy1], tag="b5_wy2")
        wyr = em.op("vector", lambda v: v.tensor_tensor(
            sc["RTD"][0:1, :], sc["RTD"][0:1, :], sc["RTC"][0:1, :], ALU.add),
            after=[wy1, wy2], tag="b5_wy")
        wyb = em.op("scalar", lambda s: s.activation(
            sc["WYB"][0:1, :], sc["RTD"][0:1, :], AF.Identity),
            after=[wyr], tag="b5_wyb")
        wxr = em.op("vector", lambda v, tq0=tq0: v.tensor_scalar(
            sc["WXB"][0:1, tq0:tq0 + TQ], sc["RTD"][0:1, :], -1.0, 1.0,
            ALU.mult, ALU.add), after=[wyr], tag="b5_wx")
        mb8 = em.op("tensor", lambda t: t.matmul(
            bk(ctx, 3, p=8), ctx.t["ONESR"][0:1, 0:8], sc["MSLB"][0:1, :],
            start=True, stop=True), after=[mrb, cpa, a0], tag="b5_mb8")
        ge8 = em.op("vector", lambda v: v.tensor_scalar(
            sc["G8S"][:], bk(ctx, 3, p=8), ctx.t["IOTA8"][:], None,
            ALU.is_equal), after=[mb8], tag="b5_ge8")
        wb8 = em.op("tensor", lambda t: t.matmul(
            bk(ctx, 4, p=8), ctx.t["ONESR"][0:1, 0:8], sc["WYB"][0:1, :],
            start=True, stop=True), after=[wyb, mrow, a1], tag="b5_wb8")
        fe = em.op("vector", lambda v, tq0=tq0: v.scalar_tensor_tensor(
            sc["FE8"][0:8, tq0:tq0 + TQ], sc["G8S"][:], 1.0,
            bk(ctx, 4, p=8), ALU.mult, ALU.mult), after=[ge8, wb8],
            tag="b5_fe")
        pch = dict(mrow=mrow, wy1=wy1, wy2=wy2, fe=fe, cpb=cpb, sg=sg)
        if tqc == NTQ - 1:
            emit_taps(ctx, em, "FE8", sc["FE8"][0:8, :], [fe])
            emit_taps(ctx, em, "WXB", sc["WXB"][0:1, :], [wxr])
            emit_taps(ctx, em, "EM0", sc["OH0T"][0:8, :], [fe])
            emit_taps(ctx, em, "EM1", sc["OH1T"][0:8, :], [fe])
    em.emit()
    st4.close()
    if B_PHASES == "route":
        stR.close()
        st.close()
        return

    st5 = ExitStack()
    ctx.sb("HM", [128, 16 * 1024], BF16, st5)
    ctx.sb("ASB", [128, 1024], BF16, st5)
    ctx.sb("FEB", [128, 1024], BF16, st5)
    ctx.sb("W1C", [128, 1024], BF16, st5)
    ctx.sb("W2C", [128, 1024], BF16, st5)
    ctx.sb("W3CO", [128, 1024], BF16, st5)
    ctx.t["RT1"] = ctx.t["APPA"]
    NQTR = 4
    TPQ = NTQ // NQTR
    w1t = ctx.dram["L0_ffn_w1"].tensor
    w2t = ctx.dram["L0_ffn_w2"].tensor
    w3t = ctx.dram["L0_ffn_w3"].tensor
    for qtr in range(NQTR):
        em = Em(ctx, f"B6_{qtr}")
        tqcs = [qtr * TPQ + i for i in range(TPQ)]
        conv = []
        for tqc in tqcs:
            tq0 = TQ * tqc
            rstd = emit_rms_chunkstats(
                ctx, em, sc,
                lambda c, tq0=tq0: sc["SCR8"][:, T * c + tq0:T * c + tq0 + TQ],
                1e-6, conv[-1:])
            bcm = em.op("tensor", lambda t: t.matmul(
                bk(ctx, 2), ctx.t["ONESR"][:], sc["RSTDB"][0:1, 0:TQ],
                start=True, stop=True), after=[rstd] + conv[-1:], tag="b6_bc")
            wxm = em.op("tensor", lambda t, tq0=tq0: t.matmul(
                bk(ctx, 3), ctx.t["ONESR"][:], sc["WXB"][0:1, tq0:tq0 + TQ],
                start=True, stop=True), after=conv[-2:-1], tag="b6_wxm")
            for c in range(C):
                cv = em.op("vector", lambda v, c=c, tq0=tq0: v.scalar_tensor_tensor(
                    sc["SCR8"][:, T * c + tq0:T * c + tq0 + TQ],
                    sc["SCR8"][:, T * c + tq0:T * c + tq0 + TQ],
                    sc["G2C"][:, c:c + 1], bk(ctx, 2), ALU.mult, ALU.mult),
                    after=[bcm] + conv[-1:], tag="b6_conv")
                t1 = em.op("vector", lambda v, c=c, tq0=tq0: v.tensor_tensor(
                    sc["RT1"][:, 0:TQ],
                    sc["SCR8"][:, T * c + tq0:T * c + tq0 + TQ],
                    bk(ctx, 3), ALU.mult), after=[cv, wxm], tag="b6_t1")
                ac = em.op("vector", lambda v, c=c, tq0=tq0: v.tensor_tensor(
                    xsl(ctx, c, tq0 // TQ), xsl(ctx, c, tq0 // TQ),
                    sc["RT1"][:, 0:TQ], ALU.add), after=[t1], tag="b6_ac")
                conv.append(ac)
        prev_e = {}
        for e in range(8):
            feb_ops = []
            for i, tqc in enumerate(tqcs):
                tq0 = TQ * tqc
                fm = em.op("tensor", lambda t, e=e, tq0=tq0: t.matmul(
                    bk(ctx, 5), ctx.t["E8"][:, 128 * e:128 * (e + 1)],
                    sc["FE8"][0:8, tq0:tq0 + TQ], start=True, stop=True),
                    after=conv[-1:] + feb_ops[-1:] + [prev_e.get("fe")],
                    tag="b6_febmm")
                fe = em.op("scalar", lambda s, i=i: s.activation(
                    sc["FEB"][:, TQ * i:TQ * (i + 1)], bk(ctx, 5),
                    AF.Identity), after=[fm], tag="b6_febev")
                feb_ops.append(fe)
            prev_g = {}
            for hc in range(16):
                w1l = em.dma("gpsimd", sc["W1C"][:, 0:1024],
                             bass.AP(w1t, e * D * 2048 + 128 * hc,
                                     [[2048, 128], [128 * 2048, C], [1, 128]]),
                             after=[prev_g.get("am")], tag="b6_w1l")
                w2l = em.dma("gpsimd", sc["W2C"][:, 0:1024],
                             bass.AP(w2t, e * D * 2048 + 128 * hc,
                                     [[2048, 128], [128 * 2048, C], [1, 128]]),
                             after=[prev_g.get("bm")], tag="b6_w2l")
                ams = []
                bms = []
                for i, tqc in enumerate(tqcs):
                    tq0 = TQ * tqc
                    cdep = conv[-1:] if (hc == 0 and e == 0) else []
                    am = mm_chain(ctx, em, bk(ctx, 0 + i),
                                  [sc["W1C"][:, 128 * ci:128 * (ci + 1)]
                                   for ci in range(C)],
                                  [sc["SCR8"][:, T * ci + tq0:
                                              T * ci + tq0 + TQ]
                                   for ci in range(C)],
                                  after=[w1l, prev_g.get("aev")] + cdep,
                                  tag="b6_amm")
                    bm = mm_chain(ctx, em, bk(ctx, 2 + i),
                                  [sc["W2C"][:, 128 * ci:128 * (ci + 1)]
                                   for ci in range(C)],
                                  [sc["SCR8"][:, T * ci + tq0:
                                              T * ci + tq0 + TQ]
                                   for ci in range(C)],
                                  after=[w2l, prev_g.get("prd")] + cdep,
                                  tag="b6_bmm")
                    ams.append(am)
                    bms.append(bm)
                aev = em.op("scalar", lambda s: s.activation(
                    sc["ASB"][:, 0:1024], bkspan(ctx, 0, 2), AF.Identity),
                    after=ams + [prev_g.get("prd")], tag="b6_aev")
                prd = em.op("vector", lambda v, hc=hc: v.tensor_tensor(
                    sc["HM"][:, 1024 * hc:1024 * (hc + 1)], bkspan(ctx, 2, 2),
                    sc["ASB"][:, 0:1024], ALU.mult),
                    after=bms + [aev, prev_g.get("gel"), prev_g.get("gate")],
                    tag="b6_prd")
                gel = em.op("scalar", lambda s, hc=hc: s.activation(
                    sc["HM"][:, 1024 * hc:1024 * (hc + 1)],
                    sc["HM"][:, 1024 * hc:1024 * (hc + 1)], AF.Gelu),
                    after=[prd, prev_g.get("gate")], tag="b6_gelu")
                gate = em.op("gpsimd", lambda g, hc=hc: g.tensor_mul(
                    sc["HM"][:, 1024 * hc:1024 * (hc + 1)],
                    sc["HM"][:, 1024 * hc:1024 * (hc + 1)],
                    sc["FEB"][:, 0:1024]), after=[gel] + feb_ops,
                    tag="b6_gate")
                prev_g = dict(am=ams[-1], bm=bms[-1], aev=aev, prd=prd,
                              gel=gel, gate=gate)
            prev_y = {}
            for co in range(C):
                for hh in range(2):
                    w3l = em.dma("gpsimd", sc["W3CO"][:, 0:1024],
                                 bass.AP(w3t, e * 2048 * D + 1024 * hh * D +
                                         128 * co,
                                         [[D, 128], [128 * D, 8], [1, 128]]),
                                 after=[prev_y.get("ym_last")], tag="b6_w3l")
                    for i, tqc in enumerate(tqcs):
                        tq0 = TQ * tqc
                        dep = [w3l, prev_g["gate"], prev_y.get(f"ya{i}")]
                        if co == 0 and hh == 0 and i == 0:
                            dep.append(prev_e.get("ya_last"))
                        ym = mm_chain(
                            ctx, em, bk(ctx, 4 + 2 * i),
                            [sc["W3CO"][:, 128 * hc:128 * (hc + 1)]
                             for hc in range(8)],
                            [sc["HM"][:, 1024 * (8 * hh + hc) + TQ * i:
                                      1024 * (8 * hh + hc) + TQ * (i + 1)]
                             for hc in range(8)],
                            after=dep, tag="b6_ymm")
                        ya = em.op("vector", lambda v, co=co, tqc=tqc, i=i:
                                   v.tensor_tensor(
                                       xsl(ctx, co, tqc), xsl(ctx, co, tqc),
                                       bk(ctx, 4 + 2 * i), ALU.add),
                                   after=[ym], tag="b6_yacc")
                        prev_y[f"ya{i}"] = ya
                        prev_y["ym_last"] = ym
            prev_e = dict(fe=feb_ops[-1], ya_last=prev_y.get("ya1") or
                          prev_y.get("ya0"))
        em.emit()
    st5.close()

    for tqc in range(NTQ):
        em = Em(ctx, f"B7_{tqc}")
        emit_ln_update(ctx, em, tqc, lambda c: xsl(ctx, c, tqc), [], sc,
                       src_bf16=True)
        em.emit()
    stR.close()
    st.close()


def emit_block_c(ctx):
    nc = ctx.nc
    st = ExitStack()
    sc = ctx.t
    ctx.sb("PREACC", [128, C * T], BF16, st)
    ctx.sb("BQC", [128, C], F32, st)
    ctx.sb("BKC", [128, C], F32, st)
    ctx.sb("BVC", [128, C], F32, st)
    ctx.sb("BPC", [128, C], F32, st)

    st0 = ExitStack()
    ctx.sb("GROW", [1, D], F32, st0)
    em = Em(ctx, "C0")
    gprev = None
    for gname, gsb, scl in [("L1_attn_bq", "BQC", 0.125), ("L1_attn_bk", "BKC", 1.0),
                            ("L1_attn_bv", "BVC", 1.0), ("L1_attn_bp", "BPC", 1.0)]:
        gld = em.dma("sync", sc["GROW"][:],
                     bass.AP(ctx.dram[gname].tensor, 0, [[D, 1], [1, D]]),
                     after=[gprev], tag=f"g:{gname}")
        gtr = []
        for c in range(C):
            def tfn(t, c=c):
                return t.transpose(ctx.t["PSUM"][:, c:c + 1],
                                   sc["GROW"][0:1, 128 * c:128 * (c + 1)],
                                   ctx.t["EYEF"][0:1, 0:1])
            gtr.append(em.op("tensor", tfn,
                             after=[gld] + (gtr[-1:] or [gprev]), tag="gtr"))
        gprev = em.op("vector", lambda v, gsb=gsb, scl=scl: v.tensor_scalar(
            sc[gsb][:, 0:C], ctx.t["PSUM"][:, 0:C], scl, None, ALU.mult),
            after=[gtr[-1]], tag="gev")
    em.op("gpsimd", lambda g: g.memset(sc["PREACC"][:], 0.0), tag="pre0")
    em.emit()
    st0.close()

    stA = ExitStack()
    ctx.sb("COS1S", [128, TQ], BF16, stA)
    ctx.sb("SIN1S", [128, TQ], BF16, stA)
    ctx.sb("CMASK", [128, 2048], BF16, stA)
    ctx.sb("K2A", [65, T], BF16, stA)
    ctx.sb("K2B", [65, T], BF16, stA)
    ctx.sb("VPA", [128, 65 * NT], BF16, stA)
    ctx.sb("VPB", [128, 65 * NT], BF16, stA)
    ctx.sb("WQP", [128, C * 128], BF16, stA)
    ctx.sb("WKP", [128, C * 128], BF16, stA)
    ctx.sb("WVP", [128, C * 128], BF16, stA)
    ctx.sb("WPP", [128, C * 128], BF16, stA)
    ctx.sb("QA", [128, TQ], BF16, stA)
    ctx.sb("QAS", [128, TQ], BF16, stA)
    ctx.sb("RT1", [128, TQ], F32, stA)
    ctx.sb("Q2P", [65, 2 * TQ], BF16, stA)
    ctx.sb("SQS", [128, TQ], BF16, stA)
    ctx.t["SNQ0"] = ctx.t["ROWA"]
    ctx.t["SNQ1"] = ctx.t["ROWB"]
    ctx.t["SROW"] = ctx.t["ROWB"]
    ctx.t["SROW2"] = ctx.t["ROWA"]
    ctx.sb("KM2", [1, 2], F32, stA)
    ctx.sb("KC2", [1, 2], F32, stA)
    ctx.sb("PT", [128, 2048], BF16, stA)
    ctx.sb("RROW", [1, TQ], BF16, stA)
    ctx.sb("RBC", [64, TQ], BF16, stA)
    ctx.sb("AONP", [128, TQ], BF16, stA)

    em = Em(ctx, "Cpre")
    em.dma("sync", sc["CMASK"][:], ctx.dram["CMASK"], tag="cmask")
    em.op("gpsimd", lambda g: g.memset(sc["K2A"][64:65, :], 1.0), tag="k2a1")
    em.op("gpsimd", lambda g: g.memset(sc["K2B"][64:65, :], 1.0), tag="k2b1")
    em.op("gpsimd", lambda g: g.memset(sc["VPA"][:], 1.0), tag="vpa1")
    em.op("gpsimd", lambda g: g.memset(sc["VPB"][:], 1.0), tag="vpb1")
    em.emit()

    def rope1(em, psum_bank, dst0, dst1, deps, scale, bias_col, prev,
              extra_ev=()):
        ev = em.op("scalar", lambda s: s.activation(
            sc["QA"][:, 0:TQ], bk(ctx, psum_bank), AF.Identity, scale=scale,
            bias=bias_col),
            after=deps + [prev.get("t1"), prev.get("rot")] + list(extra_ev),
            tag="c_ev")
        rot = em.op("tensor", lambda t: t.matmul(
            bk(ctx, 3), ctx.t["PJ"][:], sc["QA"][:, 0:TQ],
            start=True, stop=True), after=[ev, prev.get("t2")], tag="c_rot")
        t1 = em.op("vector", lambda v: v.tensor_tensor(
            sc["RT1"][:, 0:TQ], sc["QA"][:, 0:TQ], sc["COS1S"][:, 0:TQ],
            ALU.mult),
            after=[ev, prev.get("o0"), prev.get("o1"), prev.get("tbl")],
            tag="c_t1")
        t2 = em.op("vector", lambda v: v.tensor_tensor(
            sc["QAS"][:, 0:TQ], bk(ctx, 3), sc["SIN1S"][:, 0:TQ], ALU.mult),
            after=[rot, prev.get("o0"), prev.get("o1"), prev.get("tbl")],
            tag="c_t2")
        o0 = em.op("vector", lambda v: v.tensor_tensor(
            dst0, sc["RT1"][0:64, 0:TQ], sc["QAS"][0:64, 0:TQ], ALU.add),
            after=[t1, t2], tag="c_o0")
        o1 = em.op("vector", lambda v: v.tensor_tensor(
            dst1, sc["RT1"][64:128, 0:TQ], sc["QAS"][64:128, 0:TQ], ALU.add),
            after=[t1, t2], tag="c_o1")
        return dict(ev=ev, rot=rot, t1=t1, t2=t2, o0=o0, o1=o1)

    wq_t = ctx.dram["L1_attn_Wq"].tensor
    wk_t = ctx.dram["L1_attn_Wk"].tensor
    wv_t = ctx.dram["L1_attn_Wv"].tensor
    wp_t = ctx.dram["L1_attn_Wp"].tensor

    for ro in range(C):
        h0, h1 = 2 * ro, 2 * ro + 1
        em = Em(ctx, f"Cw_{ro}")
        for wt, sb in [(wq_t, "WQP"), (wk_t, "WKP"), (wv_t, "WVP")]:
            em.dma("gpsimd",
                   bass.AP(sc[sb], 0, [[C * 128, 128], [128, C], [1, 128]]),
                   bass.AP(wt, 128 * ro, [[D, 128], [128 * D, C], [1, 128]]),
                   tag=f"w:{sb}")
        em.dma("gpsimd", sc["WPP"][:, 0:C * 128],
               bass.AP(wp_t, 128 * ro * D, [[D, 128], [1, 1], [1, D]]),
               tag="w:WPP")
        em.emit()

        em = Em(ctx, f"Ckv_{ro}")
        prevb = {}
        rp = {}
        km0 = em.op("gpsimd", lambda g: g.memset(sc["KM2"][:], 0.0), tag="km20")
        kn_prev = {0: km0, 1: km0}
        for tqc in range(NTQ):
            tq0 = TQ * tqc
            tl0 = em.dma("sync", sc["COS1S"][:],
                         ctx.dram["COS1"][:, tq0:tq0 + TQ],
                         after=[rp.get("t1")], tag="tbl_c")
            tl1 = em.dma("sync", sc["SIN1S"][:],
                         ctx.dram["SIN1"][:, tq0:tq0 + TQ],
                         after=[rp.get("t2")], tag="tbl_s")
            tbl = em.op("sync", lambda e: None, after=[tl0, tl1], tag="tblw")
            rp["tbl"] = tbl
            kmm = mm_chain(ctx, em, bk(ctx, 0),
                           [sc["WKP"][:, 128 * ci:128 * (ci + 1)]
                            for ci in range(C)],
                           [xsl(ctx, ci, tqc) for ci in range(C)],
                           after=[rp.get("ev")], tag="kmm")
            rp = rope1(em, 0,
                       sc["K2A"][0:64, tq0:tq0 + TQ],
                       sc["K2B"][0:64, tq0:tq0 + TQ],
                       [kmm], 1.0, sc["BKC"][:, ro:ro + 1], rp,
                       extra_ev=[prevb.get("vpe")])
            for par, dep in ((0, rp["o0"]), (1, rp["o1"])):
                k2t = sc["K2A"] if par == 0 else sc["K2B"]
                qq = em.op("scalar", lambda s, k2t=k2t, tq0=tq0: s.activation(
                    sc["SQS"][0:64, 0:TQ], k2t[0:64, tq0:tq0 + TQ],
                    AF.Square), after=[dep, kn_prev[par], kn_prev[1 - par]],
                    tag="kn_sq")
                nm = em.op("tensor", lambda t: t.matmul(
                    bk(ctx, 5, p=1), ctx.t["ONESCB"][0:64, :],
                    sc["SQS"][0:64, 0:TQ], start=True, stop=True),
                    after=[qq, kn_prev[par], kn_prev[1 - par]], tag="kn_mm")
                rd = em.op("vector", lambda v: v.tensor_reduce(
                    sc["SNQ0"][0:1, 0:1], bk(ctx, 5, p=1), AX.X, ALU.max),
                    after=[nm, kn_prev[par], kn_prev[1 - par]], tag="kn_rd")
                ka = em.op("vector", lambda v, par=par: v.tensor_tensor(
                    sc["KM2"][0:1, par:par + 1], sc["KM2"][0:1, par:par + 1],
                    sc["SNQ0"][0:1, 0:1], ALU.max), after=[rd], tag="kn_acc")
                kn_prev[par] = ka
            vmm = mm_chain(ctx, em, bk(ctx, 1),
                           [sc["WVP"][:, 128 * ci:128 * (ci + 1)]
                            for ci in range(C)],
                           [xsl(ctx, ci, tqc) for ci in range(C)],
                           after=[prevb.get("vev")], tag="vmm")
            vev = em.op("scalar", lambda s, ro=ro: s.activation(
                sc["QAS"][:, 0:TQ], bk(ctx, 1), AF.Identity,
                bias=sc["BVC"][:, ro:ro + 1]),
                after=[vmm, rp.get("t2"), rp.get("o0"), rp.get("o1"),
                       prevb.get("vpe")], tag="vev")
            prevb["vev"] = vev
            vpe = prevb.get("vpe")
            for par in range(2):
                vpt = sc["VPA"] if par == 0 else sc["VPB"]
                for j in range(4):
                    def tfn(t, par=par, j=j):
                        return t.transpose(
                            ctx.t["PSB"][:, 64 * j:64 * (j + 1)],
                            sc["QAS"][64 * par:64 * par + 64,
                                      128 * j:128 * (j + 1)],
                            ctx.t["EYEB"][64 * par:64 * par + 64,
                                          64 * par:64 * par + 64])
                    vtr = em.op("tensor", tfn, after=[vev, vpe], tag="vtr")
                    def efn(v, vpt=vpt, j=j, tqc=tqc):
                        jj = 4 * tqc + j
                        return v.tensor_copy(
                            vpt[:, 65 * jj:65 * jj + 64],
                            ctx.t["PSB"][:, 64 * j:64 * (j + 1)])
                    vpe = em.op("vector", efn, after=[vtr], tag="vpe")
            prevb["vpe"] = vpe
        for par in range(2):
            kq = em.op("scalar", lambda s, par=par: s.activation(
                sc["KC2"][0:1, par:par + 1], sc["KM2"][0:1, par:par + 1],
                AF.Sqrt), after=[kn_prev[0], kn_prev[1]], tag="kq")
            em.op("vector", lambda v, par=par: v.tensor_scalar(
                sc["KC2"][0:1, par:par + 1], sc["KC2"][0:1, par:par + 1],
                -1.0, None, ALU.mult), after=[kq], tag="kneg")
        em.emit()

        for tqc in range(NTQ):
            tq0 = TQ * tqc
            em = Em(ctx, f"Cq_{ro}_{tqc}")
            tl0 = em.dma("sync", sc["COS1S"][:],
                         ctx.dram["COS1"][:, tq0:tq0 + TQ], tag="tbl_c")
            tl1 = em.dma("sync", sc["SIN1S"][:],
                         ctx.dram["SIN1"][:, tq0:tq0 + TQ], tag="tbl_s")
            tbl = em.op("sync", lambda e: None, after=[tl0, tl1], tag="tblw")
            qmm = mm_chain(ctx, em, bk(ctx, 0),
                           [sc["WQP"][:, 128 * ci:128 * (ci + 1)]
                            for ci in range(C)],
                           [xsl(ctx, ci, tqc) for ci in range(C)],
                           after=[], tag="qmm")
            rp2 = rope1(em, 0,
                        sc["Q2P"][0:64, 0:TQ],
                        sc["Q2P"][0:64, TQ:2 * TQ],
                        [qmm], 0.125, sc["BQC"][:, ro:ro + 1], {"tbl": tbl})
            c0, c1 = emit_cnorm(
                ctx, em, sc,
                sc["Q2P"][0:64, 0:TQ], sc["Q2P"][0:64, TQ:2 * TQ],
                sc["KC2"][0:1, 0:1], sc["KC2"][0:1, 1:2],
                sc["Q2P"][64:65, 0:TQ], sc["Q2P"][64:65, TQ:2 * TQ],
                [rp2["o0"], rp2["o1"]], [None, None])
            qdeps = [rp2["o0"], rp2["o1"], c0, c1]
            jn = 4 * tqc + 4
            pn1, pn6, lexp = emit_attn_sweep(
                ctx, em, sc, 2, lambda h: jn,
                lambda h, j: (sc["K2A"] if h == 0 else sc["K2B"])[:, 128 * j:128 * (j + 1)],
                lambda h, j: (sc["VPA"] if h == 0 else sc["VPB"])[:, 65 * j:65 * (j + 1)],
                lambda h: sc["Q2P"][:, TQ * h:TQ * (h + 1)],
                lambda h: sc["AONP"][64 * h:64 * h + 64, 0:TQ],
                True, qdeps)
            pre_prev = []
            for co in range(C):
                b = co % 2
                dep = [pn6, lexp] + pre_prev[-2:-1]
                omm = em.op("tensor", lambda t, co=co, b=b: t.matmul(
                    bk(ctx, b), sc["WPP"][:, 128 * co:128 * (co + 1)],
                    sc["AONP"][:, 0:TQ], start=True, stop=True),
                    after=dep, tag=f"c_omm{co}")
                r1 = em.op("vector", lambda v, co=co, b=b, tq0=tq0: v.tensor_tensor(
                    sc["PREACC"][:, T * co + tq0:T * co + tq0 + TQ],
                    sc["PREACC"][:, T * co + tq0:T * co + tq0 + TQ],
                    bk(ctx, b), ALU.add), after=[omm], tag="c_acc")
                pre_prev.append(r1)
            em.emit()
    stA.close()

    st4 = ExitStack()
    ctx.sb("PREF", [128, C * TQ], F32, st4)
    for tqc in range(NTQ):
        em = Em(ctx, f"C4_{tqc}")
        deps = []
        for co in range(C):
            tq0 = TQ * tqc
            d1 = em.op("vector", lambda v, co=co, tq0=tq0: v.tensor_tensor(
                sc["PREF"][:, TQ * co:TQ * (co + 1)],
                sc["PREACC"][:, T * co + tq0:T * co + tq0 + TQ],
                xsl(ctx, co, tqc), ALU.add), after=[], tag="c4_add")
            d2 = em.op("vector", lambda v, co=co: v.tensor_scalar(
                sc["PREF"][:, TQ * co:TQ * (co + 1)],
                sc["PREF"][:, TQ * co:TQ * (co + 1)],
                sc["BPC"][:, co:co + 1], None, ALU.add), after=[d1],
                tag="c4_bp")
            deps.append(d2)
        emit_ln_update(ctx, em, tqc,
                       lambda c: sc["PREF"][:, TQ * c:TQ * (c + 1)],
                       deps, sc)
        em.emit()
    st4.close()
    st.close()


def emit_block_d(ctx):
    nc = ctx.nc
    st = ExitStack()
    sc = ctx.t
    NH = 4
    TPH = NTQ // NH
    HW = TPH * TQ
    ctx.sb("BMC", [128, C], F32, st)
    ctx.sb("BOC", [128, C], F32, st)
    ctx.sb("EMB", [128, 16], F32, st)
    ctx.sb("B2T", [16, 128], BF16, st)
    ctx.sb("B1C", [128, 64], F32, st)
    ctx.sb("W1S", [128, 16 * 512], BF16, st)
    ctx.sb("W2S", [128, 16 * 512], BF16, st)
    ctx.sb("H2TH", [128, C * HW], BF16, st)
    ctx.sb("GTH", [16, C * HW], BF16, st)

    st0 = ExitStack()
    ctx.sb("GROW", [1, D], F32, st0)
    ctx.sb("B1R", [16, 512], F32, st0)
    em = Em(ctx, "D0")
    em.dma("sync", sc["EMB"][:], ctx.dram["L1_ffn_expert_emb"], tag="emb")
    em.dma("gpsimd", sc["B2T"][:], ctx.dram["L1_ffn_b2"], tag="b2")
    w1_t = ctx.dram["L1_ffn_W1"].tensor
    em.dma("gpsimd", sc["W1S"][:, 0:16 * 512],
           bass.AP(w1_t, 0, [[512, 128], [128 * 512, 16], [1, 512]]),
           tag="w1")
    w2_t = ctx.dram["L1_ffn_W2"].tensor
    for ic in range(4):
        em.dma("gpsimd",
               bass.AP(sc["W2S"], 128 * ic, [[16 * 512, 128], [512, 16], [1, 128]]),
               bass.AP(w2_t, 128 * 128 * ic,
                       [[128, 128], [512 * 128, 16], [1, 128]]),
               tag=f"w2_{ic}")
    gprev = None
    for gname, gsb in [("L1_ffn_bm", "BMC"), ("L1_ffn_bo", "BOC")]:
        gld = em.dma("sync", sc["GROW"][:],
                     bass.AP(ctx.dram[gname].tensor, 0, [[D, 1], [1, D]]),
                     after=[gprev], tag=f"g:{gname}")
        gtr = []
        for c in range(C):
            def tfn(t, c=c):
                return t.transpose(ctx.t["PSUM"][:, c:c + 1],
                                   sc["GROW"][0:1, 128 * c:128 * (c + 1)],
                                   ctx.t["EYEF"][0:1, 0:1])
            gtr.append(em.op("tensor", tfn,
                             after=[gld] + (gtr[-1:] or [gprev]), tag="gtr"))
        gprev = em.op("vector", lambda v, gsb=gsb: v.tensor_copy(
            sc[gsb][:, 0:C], ctx.t["PSUM"][:, 0:C]), after=[gtr[-1]], tag="gev")
    b1ld = em.dma("sync", sc["B1R"][:], ctx.dram["L1_ffn_b1"],
                  after=[gprev], tag="b1")
    btr = []
    for ic in range(4):
        def tfn(t, ic=ic):
            return t.transpose(ctx.t["PSUM"][:, 16 * ic:16 * (ic + 1)],
                               sc["B1R"][0:16, 128 * ic:128 * (ic + 1)],
                               ctx.t["EYEF"][0:16, 0:16])
        btr.append(em.op("tensor", tfn,
                         after=[b1ld] + (btr[-1:] or [gprev]), tag="btr"))
    em.op("vector", lambda v: v.tensor_copy(sc["B1C"][:, 0:64],
                                            ctx.t["PSUM"][:, 0:64]),
          after=[btr[-1]], tag="b1ev")
    em.emit()
    st0.close()

    wm_t = ctx.dram["L1_ffn_Wm"].tensor
    wo_t = ctx.dram["L1_ffn_Wo"].tensor

    for hf in range(NH):
        tqcs = [hf * TPH + i for i in range(TPH)]
        st1 = ExitStack()
        ctx.sb("WM", [128, C * D], BF16, st1)
        ctx.sb("RT1", [128, TQ], F32, st1)
        ctx.sb("RT2", [128, TQ], F32, st1)
        ctx.sb("LTS", [16, TQ], F32, st1)
        ctx.sb("LTT", [128, 16], F32, st1)
        ctx.sb("GTM", [128, 16], F32, st1)
        ctx.sb("E16T", [128, 16], F32, st1)
        ctx.sb("MXC", [128, 1], F32, st1)
        ctx.sb("SSC", [128, 1], F32, st1)
        ctx.sb("V8D", [128, 8], F32, st1)
        em = Em(ctx, f"D1_{hf}")
        wmld = em.dma("gpsimd", sc["WM"][:, 0:C * D],
                      bass.AP(wm_t, 0, [[D, 128], [128 * D, C], [1, D]]),
                      tag="wm")
        pv = {}
        for ti, tqc in enumerate(tqcs):
            th0 = TQ * ti
            for j in range(C):
                par = j % 2
                rt = "RT1" if par == 0 else "RT2"
                hmm = mm_chain(ctx, em, bk(ctx, par),
                               [sc["WM"][:, D * ci + 128 * j:
                                         D * ci + 128 * (j + 1)]
                                for ci in range(C)],
                               [xsl(ctx, ci, tqc) for ci in range(C)],
                               after=[pv.get(f"ev{par}"), wmld],
                               tag="d1_hmm")
                evb = em.op("scalar", lambda s, j=j, th0=th0, par=par:
                            s.activation(
                                sc["H2TH"][:, HW * j + th0:HW * j + th0 + TQ],
                                bk(ctx, par), AF.Identity,
                                bias=sc["BMC"][:, j:j + 1]),
                            after=[hmm], tag="d1_evb")
                evf = em.op("vector", lambda v, par=par, j=j, rt=rt:
                            v.tensor_scalar(
                                sc[rt][:, 0:TQ], bk(ctx, par),
                                sc["BMC"][:, j:j + 1], None, ALU.add),
                            after=[hmm, evb, pv.get(f"lgw{par}")],
                            tag="d1_evf")
                pv[f"ev{par}"] = evf
                lgm = em.op("tensor", lambda t, rt=rt: t.matmul(
                    bk(ctx, 3, p=16), sc["EMB"][:], sc[rt][:, 0:TQ],
                    start=True, stop=True),
                    after=[evf, pv.get("lge")], tag="d1_lgm")
                pv[f"lgw{par}"] = lgm
                lte0 = em.op("vector", lambda v: v.tensor_copy(
                    sc["LTS"][0:16, 0:TQ], bk(ctx, 3, p=16)),
                    after=[lgm, pv.get("ltr_last")], tag="d1_lts")
                pv["lge"] = lte0
                for tt in range(4):
                    ltr = em.op("tensor", lambda t, tt=tt: t.transpose(
                        bk(ctx, 4, w=16),
                        sc["LTS"][0:16, 128 * tt:128 * (tt + 1)],
                        ctx.t["EYEF"][0:16, 0:16]),
                        after=[lte0, pv.get("lte")], tag="d1_ltr")
                    lte = em.op("vector", lambda v: v.tensor_copy(
                        sc["LTT"][:], bk(ctx, 4, w=16)),
                        after=[ltr, pv.get("gtm")], tag="d1_lte")
                    pv["lte"] = lte
                    if tt == 3:
                        pv["ltr_last"] = lte
                    mx = em.op("vector", lambda v: v.tensor_reduce(
                        sc["MXC"][:], sc["LTT"][:], AX.X, ALU.max,
                        negate=True), after=[lte, pv.get("e16d")], tag="d1_mx")
                    e16d = em.op("scalar", lambda s: s.activation(
                        sc["E16T"][:], sc["LTT"][:], AF.Exp,
                        bias=sc["MXC"][:]), after=[mx, pv.get("gtm")],
                        tag="d1_e16")
                    pv["e16d"] = e16d
                    ss = em.op("vector", lambda v: v.tensor_reduce(
                        sc["SSC"][:], sc["E16T"][:], AX.X, ALU.add),
                        after=[e16d], tag="d1_ss")
                    rs = em.op("vector", lambda v: v.reciprocal(
                        sc["SSC"][:], sc["SSC"][:]), after=[ss], tag="d1_rs")
                    m8 = em.op("vector", lambda v: v.max(
                        out=sc["V8D"][:], in_=sc["LTT"][:]),
                        after=[lte, rs], tag="d1_m8")
                    gtm = em.op("vector", lambda v: v.tensor_scalar(
                        sc["GTM"][:], sc["LTT"][:], sc["V8D"][:, 1:2], None,
                        ALU.is_ge), after=[m8, pv.get("gtr2")], tag="d1_ge")
                    gtm2 = em.op("vector", lambda v: v.tensor_tensor(
                        sc["GTM"][:], sc["GTM"][:], sc["E16T"][:], ALU.mult),
                        after=[gtm], tag="d1_gmul")
                    gtm3 = em.op("vector", lambda v: v.tensor_scalar(
                        sc["GTM"][:], sc["GTM"][:], sc["SSC"][:], None,
                        ALU.mult), after=[gtm2], tag="d1_gnorm")
                    pv["gtm"] = gtm3
                    gtr2 = em.op("tensor", lambda t: t.transpose(
                        ctx.t["PSUM"][0:16, 2560:2560 + 128], sc["GTM"][:],
                        ctx.t["EYEF"][:]), after=[gtm3, pv.get("gte")],
                        tag="d1_gtr")
                    pv["gtr2"] = gtr2
                    gte = em.op("vector", lambda v, j=j, th0=th0, tt=tt:
                                v.tensor_copy(
                                    sc["GTH"][0:16, HW * j + th0 + 128 * tt:
                                              HW * j + th0 + 128 * (tt + 1)],
                                    ctx.t["PSUM"][0:16, 2560:2560 + 128]),
                                after=[gtr2], tag="d1_gte")
                    pv["gte"] = gte
        em.emit()
        st1.close()

        st2 = ExitStack()
        ctx.sb("HS", [128, 4 * TQ], BF16, st2)
        ctx.sb("FEB", [128, TQ], BF16, st2)
        ctx.sb("MOH", [128, C * TQ], BF16, st2)
        ctx.sb("WO", [128, C * D], BF16, st2)
        ctx.sb("PREF", [128, C * TQ], BF16, st2)
        emw = Em(ctx, f"D2w_{hf}")
        emw.dma("gpsimd", sc["WO"][:, 0:C * D],
                bass.AP(wo_t, 0, [[D, 128], [128 * D, C], [1, D]]), tag="wo")
        emw.emit()
        for ti, tqc in enumerate(tqcs):
            th0 = TQ * ti
            em = Em(ctx, f"D2_{hf}_{tqc}")
            mo_deps = []
            prev_e = {}
            for j in range(C):
                b2m = em.op("tensor", lambda t, j=j, th0=th0: t.matmul(
                    bk(ctx, 4), sc["B2T"][:],
                    sc["GTH"][0:16, HW * j + th0:HW * j + th0 + TQ],
                    start=True, stop=False),
                    after=mo_deps[-1:], tag="d2_b2")
                last_w2 = None
                for e in range(16):
                    fm = em.op("tensor", lambda t, e=e, j=j, th0=th0: t.matmul(
                        bk(ctx, 5), ctx.t["E16"][:, 128 * e:128 * (e + 1)],
                        sc["GTH"][0:16, HW * j + th0:HW * j + th0 + TQ],
                        start=True, stop=True),
                        after=[prev_e.get("fe")], tag="d2_fm")
                    fe = em.op("scalar", lambda s: s.activation(
                        sc["FEB"][:, 0:TQ], bk(ctx, 5), AF.Identity),
                        after=[fm, prev_e.get("gate_3")], tag="d2_fe")
                    for ic in range(4):
                        par = ic % 2
                        w1m = em.op("tensor",
                                    lambda t, e=e, ic=ic, j=j, th0=th0, par=par:
                                    t.matmul(
                                        bk(ctx, par),
                                        sc["W1S"][:, 512 * e + 128 * ic:
                                                  512 * e + 128 * (ic + 1)],
                                        sc["H2TH"][:, HW * j + th0:
                                                   HW * j + th0 + TQ],
                                        start=True, stop=True),
                                    after=[prev_e.get(f"gel{par}")],
                                    tag="d2_w1")
                        gel = em.op("scalar", lambda s, e=e, ic=ic, par=par:
                                    s.activation(
                                        sc["HS"][:, TQ * ic:TQ * (ic + 1)],
                                        bk(ctx, par), AF.Gelu,
                                        bias=sc["B1C"][:, 16 * ic + e:
                                                       16 * ic + e + 1]),
                                    after=[w1m, prev_e.get("w2l")],
                                    tag="d2_gelu")
                        gate = em.op("gpsimd", lambda g, ic=ic: g.tensor_mul(
                            sc["HS"][:, TQ * ic:TQ * (ic + 1)],
                            sc["HS"][:, TQ * ic:TQ * (ic + 1)],
                            sc["FEB"][:, 0:TQ]), after=[gel, fe], tag="d2_gate")
                        prev_e[f"gel{par}"] = gate
                        prev_e[f"gate_{ic}"] = gate
                    def w2fn(t, e=e):
                        inst = None
                        for ic in range(4):
                            inst = t.matmul(
                                bk(ctx, 4),
                                sc["W2S"][:, 512 * e + 128 * ic:
                                          512 * e + 128 * (ic + 1)],
                                sc["HS"][:, TQ * ic:TQ * (ic + 1)],
                                start=False, stop=(e == 15 and ic == 3))
                        return inst
                    last_w2 = em.op("tensor", w2fn,
                                    after=[prev_e[f"gate_{ic2}"]
                                           for ic2 in range(4)] + [b2m],
                                    tag="d2_w2")
                    prev_e["fe"] = fe
                    prev_e["w2l"] = last_w2
                moev = em.op("scalar", lambda s, j=j: s.activation(
                    sc["MOH"][:, TQ * j:TQ * (j + 1)], bk(ctx, 4),
                    AF.Identity), after=[last_w2], tag="d2_moev")
                mo_deps.append(moev)
            pre_deps = []
            for co in range(C):
                b = co % 2
                omm = mm_chain(ctx, em, bk(ctx, b),
                               [sc["WO"][:, D * ci + 128 * co:
                                         D * ci + 128 * (co + 1)]
                                for ci in range(C)],
                               [sc["MOH"][:, TQ * ci:TQ * (ci + 1)]
                                for ci in range(C)],
                               after=mo_deps + pre_deps[-2:-1], tag="d3_omm")
                r1 = em.op("vector", lambda v, co=co, b=b, tqc=tqc:
                           v.tensor_tensor(
                               sc["PREF"][:, TQ * co:TQ * (co + 1)], bk(ctx, b),
                               xsl(ctx, co, tqc), ALU.add), after=[omm],
                           tag="d3_add")
                r2 = em.op("vector", lambda v, co=co: v.tensor_scalar(
                    sc["PREF"][:, TQ * co:TQ * (co + 1)],
                    sc["PREF"][:, TQ * co:TQ * (co + 1)],
                    sc["BOC"][:, co:co + 1], None, ALU.add), after=[r1],
                    tag="d3_bo")
                pre_deps.append(r2)
            emit_ln_update(ctx, em, tqc,
                           lambda c: sc["PREF"][:, TQ * c:TQ * (c + 1)],
                           pre_deps, sc, src_bf16=True)
            em.emit()
        st2.close()
    st.close()

def _in_map(core, inputs, names, x_state):
    cn = host_constants(core)
    m = {}
    for n in names:
        if n == "xin":
            m[n] = x_state[core]
        elif n in cn:
            m[n] = cn[n]
        else:
            m[n] = np.ascontiguousarray(inputs[n], dtype=np.float32)
    return m


def kernel(**inputs):
    x = np.ascontiguousarray(inputs["L0_attn_x"], dtype=np.float32)
    x_state = [x[i] for i in range(8)]
    launches = [("A", "B", "C", "D")]
    for blocks in launches:
        nc, names = build_program(list(blocks), 0)
        in_maps = []
        for core in range(8):
            if any(b for b in blocks if b == "B"):
                pass
            in_maps.append(_in_map(core, inputs, names, x_state))
        res = run_bass_kernel_spmd(nc, in_maps, core_ids=list(range(8)))
        x_state = [np.asarray(res.results[i]["xout"], dtype=np.float32)
                   for i in range(8)]
    return np.stack(x_state, axis=0)


# version 03 (final)
# speedup vs baseline: 1.3334x; 1.1141x over previous
import math
from contextlib import ExitStack

import numpy as np
import ml_dtypes

import concourse.bass as bass
import concourse.mybir as mybir
from concourse.bass_utils import run_bass_kernel_spmd

F32 = mybir.dt.float32
I16 = mybir.dt.int16
BF16 = mybir.dt.bfloat16
AF = mybir.ActivationFunctionType
ALU = mybir.AluOpType
AX = mybir.AxisListType

T = 4096
D = 1024
C = 8
TQ = 512
NTQ = T // TQ
NT = T // 128
BF = ml_dtypes.bfloat16
SHIFT = 15.0

DEBUG_TAPS = []
NO_CC = False



ENGINES = ("tensor", "vector", "scalar", "gpsimd", "sync")


class Op:
    __slots__ = ("idx", "engine", "fn", "after", "dma", "sig", "sigval", "tag",
                 "semkey", "cc")

    def __init__(self, idx, engine, fn, after, dma, tag):
        self.idx = idx
        self.engine = engine
        self.fn = fn
        self.after = after
        self.dma = dma
        self.cc = False
        self.sig = False
        self.sigval = None
        self.semkey = None
        self.tag = tag


DMA_CHANNELS = {"sync": 6, "gpsimd": 6, "scalar": 2, "vector": 2, "tensor": 1}


class Em:

    def __init__(self, ctx, name):
        self.ctx = ctx
        self.name = name
        self.ops = []
        self.opset = set()

    def op(self, engine, fn, after=(), dma=False, tag=""):
        kept = []
        for a in after:
            if a is None:
                continue
            if id(a) in self.opset or a.sigval is not None:
                kept.append(a)
        o = Op(len(self.ops), engine, fn, tuple(kept), dma, tag)
        self.ops.append(o)
        self.opset.add(id(o))
        return o

    def dma(self, engine, out, in_, after=(), tag=""):
        return self.op(engine, lambda e: e.dma_start(out=out, in_=in_),
                       after=after, dma=True, tag=tag)

    def emit(self, drain_dmas=True):
        if drain_dmas:
            for e in ENGINES:
                dops = [o for o in self.ops if o.engine == e and o.dma]
                if dops:
                    self.op(e, lambda _e: None, after=dops, tag="dma_drain")
        sems = self.ctx.sems
        for o in self.ops:
            if o.dma:
                o.sig = True
            for a in o.after:
                if a.dma or a.engine != o.engine or o.engine != "tensor":
                    a.sig = True
        rr = self.ctx.dma_rr
        for o in self.ops:
            if not o.sig:
                continue
            if o.cc:
                key = "Dcc"
            elif o.dma:
                ch = rr[o.engine] % DMA_CHANNELS[o.engine]
                rr[o.engine] += 1
                key = f"D{o.engine}{ch}"
            else:
                key = "E" + o.engine
            o.semkey = key
            amt = 16 if (o.dma and not o.cc) else 1
            o.sigval = sems.v[key] + amt
            sems.v[key] = o.sigval
        nc = self.ctx.nc
        with nc.Block(self.name) as blk:
            for eng_name in ENGINES:
                eng_ops = [o for o in self.ops if o.engine == eng_name]
                if not eng_ops:
                    continue

                def body(eng, eng_ops=eng_ops, eng_name=eng_name):
                    for o in eng_ops:
                        waited = {}
                        deps = list(o.after)
                        for a in deps:
                            if not a.sig:
                                continue
                            if a.dma or a.engine != o.engine or o.engine != "tensor":
                                key = a.semkey
                                if waited.get(key, -1) >= a.sigval:
                                    continue
                                waited[key] = a.sigval
                                eng.wait_ge(sems.h[key], a.sigval)
                        if o.dma and not o.cc and o.sigval > 16:
                            if waited.get(o.semkey, -1) < o.sigval - 16:
                                eng.wait_ge(sems.h[o.semkey], o.sigval - 16)
                        inst = o.fn(eng)
                        if o.sig:
                            if inst is None:
                                inst = eng.engine_nop() if hasattr(eng, "engine_nop") \
                                    else eng.nop()
                            if o.cc:
                                inst.then_inc(sems.h[o.semkey])
                            else:
                                inst.then_inc(sems.h[o.semkey], 16 if o.dma else 1)

                getattr(blk, eng_name)(body)


class Sems:
    def __init__(self, nc, stack, names):
        self.h = {n: stack.enter_context(nc.semaphore(n)) for n in names}
        self.v = {n: 0 for n in names}


class Ctx:
    def __init__(self, nc, stack):
        self.nc = nc
        self.stack = stack
        self.t = {}

    def sb(self, name, shape, dtype, stack=None):
        self._uniq = getattr(self, "_uniq", 0) + 1
        h = (stack or self.stack).enter_context(
            self.nc.sbuf_tensor(f"sb{self._uniq}_{name}", shape, dtype))
        self.t[name] = h
        return h



def host_constants(core):
    cn = {}
    eye = np.eye(128, dtype=np.float32)
    cn["EYEB"] = eye.astype(BF)
    cn["EYEF"] = eye.copy()
    M = np.zeros((64, 64), np.float32)
    M[0:32, 32:64] = -np.eye(32)
    M[32:64, 0:32] = np.eye(32)
    PN = np.zeros((128, 128), np.float32)
    PN[0:64, 0:64] = M.T
    PN[64:128, 64:128] = M.T
    cn["PN"] = PN.astype(BF)
    PJ = np.zeros((128, 128), np.float32)
    for i in range(64):
        PJ[2 * i, 2 * i + 1] = 1.0
        PJ[2 * i + 1, 2 * i] = 1.0
    cn["PJ"] = PJ.astype(BF)
    thetas = 10000.0 ** (-np.arange(0, 64, 2, dtype=np.float32) / 64.0)
    ticks = np.outer(np.arange(T, dtype=np.float32), thetas)
    ticks = np.tile(ticks, (1, 2))
    cosL0 = np.cos(ticks).T.astype(np.float32)
    sinL0 = np.sin(ticks).T.astype(np.float32)
    cn["COS0"] = np.concatenate([cosL0, cosL0], 0).astype(BF)
    cn["SIN0"] = np.concatenate([sinL0, sinL0], 0).astype(BF)
    half = 32
    inv_freq = 1.0 / 10000.0 ** (np.arange(half, dtype=np.float32) * 2.0 / 64.0)
    ang = np.arange(T, dtype=np.float32)[:, None] * inv_freq[None, :]
    cosj = np.cos(ang)
    sinj = np.sin(ang)
    CJ = np.zeros((64, T), np.float32)
    SJ = np.zeros((64, T), np.float32)
    for i in range(half):
        CJ[2 * i] = cosj[:, i]
        CJ[2 * i + 1] = cosj[:, i]
        SJ[2 * i] = -sinj[:, i]
        SJ[2 * i + 1] = sinj[:, i]
    cn["COS1"] = np.concatenate([CJ, CJ], 0).astype(BF)
    cn["SIN1"] = np.concatenate([SJ, SJ], 0).astype(BF)
    p = np.arange(128)[:, None]
    f = np.arange(TQ)[None, :]
    masks = [(f >= p + 128 * d).astype(np.float32) for d in range(4)]
    cn["CMASK"] = np.concatenate(masks, 1).astype(BF)
    PAT = np.zeros((128, 128), np.float32)
    PAT[0:64, 0] = 1.0
    PAT[64:128, 64] = 1.0
    cn["PAT2F"] = PAT.copy()
    cn["PAT2B"] = PAT.astype(BF)
    cn["ONESCB"] = np.ones((128, 1), np.float32).astype(BF)
    cn["ONESC"] = np.ones((128, 1), np.float32)
    cn["MEANCB"] = np.full((128, 1), 1.0 / D, np.float32).astype(BF)
    cn["IOTA8"] = np.arange(8, dtype=np.float32)[:, None]
    cn["IOTA8B"] = np.arange(8, dtype=np.float32)[:, None].astype(BF)
    cn["ONES8B"] = np.ones((8, 1), np.float32).astype(BF)
    cn["ONES8"] = np.ones((8, 1), np.float32)
    pmv = np.zeros((64, 8), np.float32)
    for r in range(core):
        for e in range(8):
            pmv[8 * r + e, e] = 1.0
    cn["PM64"] = pmv
    E = np.zeros((16, 16 * 128), np.float32)
    for e in range(16):
        E[e, 128 * e:128 * (e + 1)] = 1.0
    cn["E16"] = E.astype(BF)
    cn["ONESR"] = np.ones((1, 128), np.float32).astype(BF)
    E8 = np.zeros((8, 8 * 128), np.float32)
    for e in range(8):
        E8[e, 128 * e:128 * (e + 1)] = 1.0
    cn["E8"] = E8.astype(BF)
    cn["ONESRF"] = np.ones((1, 128), np.float32)
    return cn


TABLE_SHAPES = {
    "COS0": ([128, T], BF16), "SIN0": ([128, T], BF16),
    "COS1": ([128, T], BF16), "SIN1": ([128, T], BF16),
    "CMASK": ([128, 2048], BF16),
}

CONST_SHAPES = {
    "EYEB": ([128, 128], BF16), "EYEF": ([128, 128], F32),
    "PN": ([128, 128], BF16), "PJ": ([128, 128], BF16),
    "PAT2F": ([128, 128], F32),
    "PAT2B": ([128, 128], BF16), "ONESCB": ([128, 1], BF16),
    "ONESC": ([128, 1], F32), "MEANCB": ([128, 1], BF16),
    "IOTA8": ([8, 1], F32), "ONES8": ([8, 1], F32),
    "IOTA8B": ([8, 1], BF16), "ONES8B": ([8, 1], BF16),
    "PM64": ([64, 8], F32), "E16": ([16, 16 * 128], BF16),
    "ONESR": ([1, 128], BF16), "ONESRF": ([1, 128], F32),
    "E8": ([8, 8 * 128], BF16),
}

WEIGHTS = {
    "L0_attn_w_q": [D, D], "L0_attn_w_k": [64, D], "L0_attn_w_v": [64, D],
    "L0_attn_w_out": [D, D], "L0_attn_softmax_scale": [1],
    "L0_ffn_g1": [D], "L0_ffn_g2": [D],
    "L0_ffn_wq": [D, D], "L0_ffn_wk": [D, 256], "L0_ffn_wv": [D, 256],
    "L0_ffn_wo": [D, D], "L0_ffn_router_w": [D, 8], "L0_ffn_router_b": [8],
    "L0_ffn_w1": [8, D, 2048], "L0_ffn_w2": [8, D, 2048], "L0_ffn_w3": [8, 2048, D],
    "L1_attn_Wq": [D, D], "L1_attn_bq": [D], "L1_attn_Wk": [D, D], "L1_attn_bk": [D],
    "L1_attn_Wv": [D, D], "L1_attn_bv": [D], "L1_attn_Wp": [D, D], "L1_attn_bp": [D],
    "L1_ffn_Wm": [D, D], "L1_ffn_bm": [D], "L1_ffn_expert_emb": [128, 16],
    "L1_ffn_W1": [16, 128, 512], "L1_ffn_b1": [16, 512],
    "L1_ffn_W2": [16, 512, 128], "L1_ffn_b2": [16, 128],
    "L1_ffn_Wo": [D, D], "L1_ffn_bo": [D],
}

BLOCK_WEIGHTS = {
    "A": ["L0_attn_w_q", "L0_attn_w_k", "L0_attn_w_v", "L0_attn_w_out",
          "L0_attn_softmax_scale"],
    "B": ["L0_ffn_g1", "L0_ffn_g2", "L0_ffn_wq", "L0_ffn_wk", "L0_ffn_wv",
          "L0_ffn_wo", "L0_ffn_router_w", "L0_ffn_router_b", "L0_ffn_w1",
          "L0_ffn_w2", "L0_ffn_w3"],
    "C": ["L1_attn_Wq", "L1_attn_bq", "L1_attn_Wk", "L1_attn_bk", "L1_attn_Wv",
          "L1_attn_bv", "L1_attn_Wp", "L1_attn_bp"],
    "D": ["L1_ffn_Wm", "L1_ffn_bm", "L1_ffn_expert_emb", "L1_ffn_W1",
          "L1_ffn_b1", "L1_ffn_W2", "L1_ffn_b2", "L1_ffn_Wo", "L1_ffn_bo"],
}



def bk(ctx, i, w=TQ, p0=0, p=128):
    return ctx.t["PSUM"][p0:p0 + p, 512 * i:512 * i + w]


def bkspan(ctx, i, nb):
    return ctx.t["PSUM"][:, 512 * i:512 * (i + nb)]


def xsl(ctx, c, tqc, w=TQ, start=0):
    return ctx.t["X"][:, T * c + TQ * tqc + start: T * c + TQ * tqc + start + w]


def mm_chain(ctx, em, psum_view, lhsT_tiles, rhs_views, after=(), tag="mm",
             extra_first=None):
    def fn(t):
        inst = None
        if extra_first is not None:
            inst = extra_first(t)
        n = len(lhsT_tiles)
        for i, (lt, rv) in enumerate(zip(lhsT_tiles, rhs_views)):
            inst = t.matmul(psum_view, lt, rv,
                            start=(i == 0 and extra_first is None),
                            stop=(i == n - 1))
        return inst
    return em.op("tensor", fn, after=after, tag=tag)


def emit_weight_load_natural(ctx, em, wname, sbname, after=(), sub=None):
    src = ctx.dram[wname] if sub is None else sub
    rows, cols = src.shape
    dst = ctx.t[sbname]
    rp = min(128, rows)
    nro = max(rows // 128, 1)
    ap = bass.AP(src.tensor, src.offset, [[cols, rp], [rp * cols, nro], [1, cols]])
    return em.dma("gpsimd", dst[0:rp, 0:nro * cols],
                  ap, after=after, tag=f"wload:{wname}")


def emit_weight_load_transposed(ctx, em, wname, sbname, rows, cols, after=(),
                                psb_after=None):
    raw = ctx.t[sbname + "_raw"]
    dst = ctx.t[sbname]
    nro = max(rows // 128, 1)
    rp = min(rows, 128)
    nci = cols // 128
    src_ap = ctx.dram[wname]
    halves = 2 if nci > 4 else 1
    cph = cols // halves
    ncih = nci // halves
    evs = []
    for hf in range(halves):
        ap = bass.AP(src_ap.tensor, cph * hf,
                     [[cols, rp], [rp * cols, nro], [1, cph]])
        dep = list(after) + (evs[-1:] if evs else [])
        ld = em.dma("gpsimd", raw[0:rp, 0:nro * cph], ap, after=dep,
                    tag=f"wload:{wname}:{hf}")
        for cl in range(ncih):
            ci = hf * ncih + cl
            trs = []
            for ro in range(nro):
                def tfn(t, cl=cl, ro=ro):
                    return t.transpose(
                        ctx.t["PSB"][:, rp * ro:rp * ro + rp],
                        raw[0:rp, cph * ro + 128 * cl: cph * ro + 128 * cl + 128],
                        ctx.t["EYEB"][0:rp, 0:rp])
                dep = [ld] + (evs[-1:] if evs else ([psb_after] if psb_after else []))
                trs.append(em.op("tensor", tfn, after=dep, tag="wtr"))
            def efn(v, ci=ci, nro=nro, rp=rp):
                return v.tensor_copy(dst[:, rp * nro * ci: rp * nro * (ci + 1)],
                                     ctx.t["PSB"][:, 0:rp * nro])
            evs.append(em.op("vector", efn, after=[trs[-1]], tag="wtr_ev"))
    return evs[-1]


def emit_ln_update(ctx, em, tqc, pre, deps, sc, src_bf16=False):
    stat_ops = []
    mean_last = None
    msq_last = None
    for c in range(C):
        psrc = pre(c)
        if src_bf16:
            bsrc = psrc
            cp = None
        else:
            cp = em.op("scalar", lambda s, psrc=psrc: s.copy(
                sc["PREBC"][:, 0:TQ], psrc), after=deps, tag="ln_cp")
            bsrc = sc["PREBC"][:, 0:TQ]
        sqo = em.op("scalar", lambda s, c=c, bsrc=bsrc: s.activation(
            sc["SQBC"][:, 0:TQ], bsrc, AF.Square), after=[cp] if cp else deps,
            tag="ln_sq")
        mean_last = em.op("tensor", lambda t, c=c, bsrc=bsrc: t.matmul(
            bk(ctx, 4, p=1), ctx.t["MEANCB"][:], bsrc,
            start=(c == 0), stop=(c == C - 1)),
            after=[cp if cp else None, sqo] + (deps if c == 0 else []),
            tag="ln_mean")
        msq_last = em.op("tensor", lambda t, c=c: t.matmul(
            bk(ctx, 5, p=1), ctx.t["MEANCB"][:], sc["SQBC"][:, 0:TQ],
            start=(c == 0), stop=(c == C - 1)), after=[sqo], tag="ln_msq")
        deps = deps + [msq_last] if src_bf16 else [mean_last, msq_last]
    r_m2 = em.op("scalar", lambda s: s.activation(
        sc["ROWA"][0:1, 0:TQ], bk(ctx, 4, p=1), AF.Square),
        after=[mean_last], tag="ln_m2")
    r_var = em.op("vector", lambda v: v.scalar_tensor_tensor(
        sc["ROWB"][0:1, 0:TQ], bk(ctx, 5, p=1), 1e-5, sc["ROWA"][0:1, 0:TQ],
        ALU.add, ALU.subtract), after=[msq_last, r_m2], tag="ln_var")
    r_lv = em.op("scalar", lambda s: s.activation(
        sc["ROWA"][0:1, 0:TQ], sc["ROWB"][0:1, 0:TQ], AF.Ln),
        after=[r_var], tag="ln_lv")
    r_rstd = em.op("scalar", lambda s: s.activation(
        sc["RSTDB"][0:1, 0:TQ], sc["ROWA"][0:1, 0:TQ], AF.Exp, scale=-0.5),
        after=[r_lv], tag="ln_rstd")
    r_mr = em.op("vector", lambda v: v.tensor_tensor(
        sc["MRB"][0:1, 0:TQ], bk(ctx, 4, p=1), sc["RSTDB"][0:1, 0:TQ], ALU.mult),
        after=[mean_last, r_rstd], tag="ln_mr")
    b_r = em.op("tensor", lambda t: t.matmul(
        bk(ctx, 2), ctx.t["ONESR"][:], sc["RSTDB"][0:1, 0:TQ], start=True, stop=True),
        after=[r_rstd], tag="ln_bcr")
    b_mr = em.op("tensor", lambda t: t.matmul(
        bk(ctx, 3), ctx.t["ONESR"][:], sc["MRB"][0:1, 0:TQ], start=True, stop=True),
        after=[r_mr], tag="ln_bcmr")
    outs = []
    for c in range(C):
        psrc = pre(c)
        o1 = em.op("vector", lambda v, psrc=psrc: v.tensor_tensor(
            sc["APPA"][:, 0:TQ], psrc, bk(ctx, 2), ALU.mult),
            after=[b_r] + (outs[-1:] if outs else []), tag="ln_ap1")
        o2 = em.op("vector", lambda v, c=c: v.tensor_tensor(
            xsl(ctx, c, tqc), sc["APPA"][:, 0:TQ], bk(ctx, 3), ALU.subtract),
            after=[o1, b_mr], tag="ln_ap2")
        outs.append(o2)
    return outs



def emit_load_constants_and_x(ctx):
    nc = ctx.nc
    lst = ExitStack()
    for i in range(2):
        ctx.sb(f"XT{i}", [128, D], F32, lst)
        ctx.sb(f"XB{i}", [128, D], BF16, lst)
    em = Em(ctx, "init")
    lds = []
    for cname in CONST_SHAPES:
        lds.append(em.dma("sync", ctx.t[cname][:], ctx.dram[cname],
                          tag=f"c:{cname}"))
    xt = [ctx.t[f"XT{i}"] for i in range(2)]
    xb = [ctx.t[f"XB{i}"] for i in range(2)]
    prev_tr = [None, None]
    prev_ev = None
    eyeld = lds[0]
    for tt in range(NT):
        b = tt % 2
        ld = em.dma("sync", xt[b][:], ctx.dram["xin"][128 * tt:128 * (tt + 1), :],
                    after=[prev_tr[b]], tag=f"xld{tt}")
        cast = em.op("vector", lambda v, b=b: v.tensor_copy(xb[b][:], xt[b][:]),
                     after=[ld, prev_tr[b]], tag="xcast")
        trs = []
        for c in range(C):
            def tfn(t, b=b, c=c):
                return t.transpose(ctx.t["PSB"][:, 128 * c:128 * (c + 1)],
                                   xb[b][:, 128 * c:128 * (c + 1)],
                                   ctx.t["EYEB"][:])
            trs.append(em.op("tensor", tfn,
                             after=[cast, eyeld, prev_ev], tag="xtr"))
        def efn(v, tt=tt):
            ap = bass.AP(ctx.t["X"], 128 * tt, [[C * T, 128], [T, 8], [1, 128]])
            return v.tensor_copy(ap, ctx.t["PSB"][:, 0:1024])
        ev = em.op("vector", efn, after=[trs[-1]], tag="xev")
        prev_ev = ev
        prev_tr[b] = trs[-1]
    em.emit()
    lst.close()


def emit_store_x(ctx):
    nc = ctx.nc
    lst = ExitStack()
    for i in range(2):
        ctx.sb(f"STG{i}", [128, D], F32, lst)
    em = Em(ctx, "store")
    stg = [ctx.t["STG0"], ctx.t["STG1"]]
    prev_dma = [None, None]
    prev_ev = None
    last = None
    for tt in range(NT):
        b = tt % 2
        trs = []
        for c in range(C):
            def tfn(t, tt=tt, c=c):
                return t.transpose(
                    ctx.t["PSB"][:, 128 * c:128 * (c + 1)],
                    ctx.t["X"][:, T * c + 128 * tt: T * c + 128 * (tt + 1)],
                    ctx.t["EYEB"][:])
            trs.append(em.op("tensor", tfn, after=[prev_ev], tag="str"))
        ev = em.op("vector",
                   lambda v, b=b: v.tensor_copy(stg[b][:], ctx.t["PSB"][:, 0:1024]),
                   after=[trs[-1], prev_dma[b]], tag="sev")
        dma = em.dma("sync", ctx.dram["xout"][128 * tt:128 * (tt + 1), :],
                     stg[b][:], after=[ev], tag="sdma")
        prev_dma[b] = dma
        prev_ev = ev
        last = dma
    em.op("gpsimd", lambda g: None, after=[last], tag="final_wait")
    em.emit()
    lst.close()


def emit_taps(ctx, em, tname, src_ap, after):
    if tname in ctx.taps:
        return em.dma("sync", ctx.taps[tname], src_ap, after=after, tag=f"tap:{tname}")
    return None




def emit_attn_sweep(ctx, em, sc, nheads, jn_fn, k2_fn, vp_fn, q2_fn, aon_fn,
                    masked, first_deps, tap_pt=False):
    pv_op = None
    last_exp = None
    prev_n1 = prev_n6 = None
    for h in range(nheads):
        jn = jn_fn(h)
        ngrp = jn // 4
        for jb in range(ngrp):
            mm_ops = []
            for j4 in range(4):
                j = 4 * jb + j4
                def sfn(t, h=h, j=j, j4=j4):
                    return t.matmul(bk(ctx, j4), k2_fn(h, j), q2_fn(h),
                                    start=True, stop=True)
                dep = [last_exp]
                if h == 0 and jb == 0:
                    dep = dep + list(first_deps)
                mm_ops.append(em.op("tensor", sfn, after=dep, tag="smm"))
            def efn(s):
                return s.activation(sc["PT"][:, 0:2048], bkspan(ctx, 0, 4), AF.Exp)
            eop = em.op("scalar", efn, after=mm_ops + [pv_op], tag="exp")
            last_exp = eop
            consume = eop
            if masked and jb == ngrp - 1:
                consume = em.op("vector", lambda v: v.tensor_tensor(
                    sc["PT"][:, 0:2048], sc["PT"][:, 0:2048],
                    ctx.t["CMASK"][:, 0:2048], ALU.mult), after=[eop], tag="cmask")
                if tap_pt and h == 0 and "PT0" in ctx.taps:
                    tp = emit_taps(ctx, em, "PT0", sc["PT"][:, 0:2048],
                                   [consume])
                    consume = em.op("vector", lambda v: None, after=[tp, consume],
                                    tag="pt_tap_gate")
            for j4 in range(4):
                j = 4 * jb + j4
                def pvfn(t, h=h, j=j, j4=j4, jn=jn):
                    return t.matmul(bk(ctx, 6, p=65), vp_fn(h, j),
                                    sc["PT"][:, TQ * j4:TQ * (j4 + 1)],
                                    start=(j == 0), stop=(j == jn - 1))
                dep = [consume]
                if j == 0:
                    dep += [prev_n1, prev_n6]
                pv_op = em.op("tensor", pvfn, after=dep, tag="pv")
        n1 = em.op("vector", lambda v: v.tensor_scalar(
            sc["SROW"][0:1, :], bk(ctx, 6, p0=64, p=1), 1e-30, None,
            ALU.max), after=[pv_op], tag="n1")
        n2 = em.op("scalar", lambda s: s.activation(sc["SROW2"][0:1, :],
                                                    sc["SROW"][0:1, :], AF.Ln),
                   after=[n1], tag="n2")
        n3 = em.op("scalar", lambda s: s.activation(sc["RROW"][0:1, :],
                                                    sc["SROW2"][0:1, :],
                                                    AF.Exp, scale=-1.0),
                   after=[n2], tag="n3")
        n4 = em.op("tensor", lambda t: t.matmul(
            bk(ctx, 5, p=64), ctx.t["ONESR"][0:1, 0:64], sc["RROW"][0:1, :],
            start=True, stop=True), after=[n3], tag="n4")
        n5 = em.op("scalar", lambda s: s.copy(sc["RBC"][:], bk(ctx, 5, p=64)),
                   after=[n4], tag="n5")
        def n6fn(v, h=h):
            return v.tensor_tensor(aon_fn(h), bk(ctx, 6, p0=0, p=64),
                                   sc["RBC"][:], ALU.mult)
        n6 = em.op("vector", n6fn, after=[n5, pv_op, n1], tag="n6")
        prev_n1, prev_n6 = n1, n6
    return prev_n1, prev_n6, last_exp


def emit_rms_chunkstats(ctx, em, sc, src_fn, eps, deps):
    msq_last = None
    dd = list(deps)
    for c in range(C):
        src_ap = src_fn(c)
        sqo = em.op("scalar", lambda s, src_ap=src_ap: s.activation(
            sc["SQBC"][:, 0:TQ], src_ap, AF.Square), after=dd, tag="rms_sq")
        msq_last = em.op("tensor", lambda t, c=c: t.matmul(
            bk(ctx, 4, p=1), ctx.t["MEANCB"][:], sc["SQBC"][:, 0:TQ],
            start=(c == 0), stop=(c == C - 1)),
            after=[sqo] + (list(deps) if c == 0 else []), tag="rms_mm")
        dd = [msq_last]
    r_eps = em.op("vector", lambda v: v.tensor_scalar(
        sc["ROWB"][0:1, 0:TQ], bk(ctx, 4, p=1), eps, None, ALU.add),
        after=[msq_last], tag="rms_eps")
    r_ln = em.op("scalar", lambda s: s.activation(
        sc["ROWA"][0:1, 0:TQ], sc["ROWB"][0:1, 0:TQ], AF.Ln),
        after=[r_eps], tag="rms_ln")
    r_rstd = em.op("scalar", lambda s: s.activation(
        sc["RSTDB"][0:1, 0:TQ], sc["ROWA"][0:1, 0:TQ], AF.Exp, scale=-0.5),
        after=[r_ln], tag="rms_rstd")
    return r_rstd


def emit_cnorm(ctx, em, sc, q2r0, q2r1, kc0, kc1, cdst0, cdst1, deps,
               prev_crows):
    sq0 = em.op("scalar", lambda s: s.activation(
        sc["SQS"][0:64, 0:TQ], q2r0, AF.Square), after=deps, tag="cn_sq0")
    sq1 = em.op("scalar", lambda s: s.activation(
        sc["SQS"][64:128, 0:TQ], q2r1, AF.Square), after=deps, tag="cn_sq1")
    nmm = em.op("tensor", lambda t: t.matmul(
        bk(ctx, 4), ctx.t["PAT2B"][:], sc["SQS"][:, 0:TQ],
        start=True, stop=True), after=[sq0, sq1] + list(prev_crows),
        tag="cn_mm")
    s0r = em.op("scalar", lambda s: s.activation(
        sc["SNQ0"][0:1, :], bk(ctx, 4, p=1), AF.Sqrt),
        after=[nmm, prev_crows[0]], tag="cn_s0")
    s1r = em.op("scalar", lambda s: s.activation(
        sc["SNQ1"][0:1, :], bk(ctx, 4, p0=64, p=1), AF.Sqrt),
        after=[nmm, prev_crows[1]], tag="cn_s1")
    c0 = em.op("vector", lambda v: v.tensor_scalar(
        cdst0, sc["SNQ0"][0:1, :], kc0, SHIFT, ALU.mult, ALU.add),
        after=[s0r], tag="cn_c0")
    c1 = em.op("vector", lambda v: v.tensor_scalar(
        cdst1, sc["SNQ1"][0:1, :], kc1, SHIFT, ALU.mult, ALU.add),
        after=[s1r], tag="cn_c1")
    return c0, c1



A_PHASES = "012"


def emit_block_a(ctx):
    nc = ctx.nc
    st = ExitStack()
    ctx.sb("WRAW", [128, C * D // 2], BF16, st)
    ctx.t["WQT_raw"] = ctx.t["WRAW"]
    ctx.t["WKT_raw"] = ctx.t["WRAW"]
    ctx.t["WVT_raw"] = ctx.t["WRAW"]
    ctx.t["WOT_raw"] = ctx.t["WRAW"]
    ctx.sb("WQT", [128, C * D], BF16, st)
    ctx.sb("WKT", [128, C * 64], BF16, st)
    ctx.sb("WVT", [128, C * 64], BF16, st)
    ctx.sb("WOT", [128, C * D], BF16, st)
    ctx.sb("K2", [65, T], BF16, st)
    ctx.sb("VP", [128, NT * 65], BF16, st)
    ctx.sb("VTMP", [64, TQ], BF16, st)
    ctx.sb("KMAX", [1, 1], F32, st)
    ctx.sb("KROW", [1, TQ], F32, st)
    ctx.sb("S0SB", [1, 1], F32, st)
    ctx.sb("S0COL", [128, 1], F32, st)
    ctx.sb("KCSN", [1, 1], F32, st)
    ctx.sb("Q2", [65, 16 * TQ], BF16, st)
    ctx.sb("SQS", [128, TQ], F32, st)
    ctx.sb("SNQ0", [1, TQ], F32, st)
    ctx.sb("SNQ1", [1, TQ], F32, st)
    ctx.sb("PT", [128, 2048], BF16, st)
    ctx.sb("SROW", [1, TQ], F32, st)
    ctx.sb("SROW2", [1, TQ], F32, st)
    ctx.sb("RROW", [1, TQ], BF16, st)
    ctx.sb("RBC", [64, TQ], F32, st)
    ctx.sb("AON", [128, C * TQ], BF16, st)
    ctx.sb("PREF", [128, C * TQ], F32, st)

    sc = ctx.t

    em = Em(ctx, "A0")
    wq = emit_weight_load_transposed(ctx, em, "L0_attn_w_q", "WQT", D, D)
    wk = emit_weight_load_transposed(ctx, em, "L0_attn_w_k", "WKT", 64, D,
                                     after=[wq], psb_after=wq)
    wv = emit_weight_load_transposed(ctx, em, "L0_attn_w_v", "WVT", 64, D,
                                     after=[wk], psb_after=wk)
    wo = emit_weight_load_transposed(ctx, em, "L0_attn_w_out", "WOT", D, D,
                                     after=[wv], psb_after=wv)
    s0ld = em.dma("sync", sc["S0SB"][:],
                  bass.AP(ctx.dram["L0_attn_softmax_scale"].tensor, 0,
                          [[1, 1], [1, 1]]), tag="s0")
    s0mm = em.op("tensor", lambda t: t.matmul(bk(ctx, 6, w=1), ctx.t["ONESRF"][:],
                                              sc["S0SB"][:], start=True, stop=True),
                 after=[s0ld], tag="s0mm")
    s0ev = em.op("vector", lambda v: v.tensor_copy(sc["S0COL"][:], bk(ctx, 6, w=1)),
                 after=[s0mm], tag="s0ev")
    ones_vp = em.op("gpsimd", lambda g: g.memset(sc["VP"][:], 1.0), tag="vp1")
    ones_k2 = em.op("gpsimd", lambda g: g.memset(sc["K2"][64:65, :], 1.0), tag="k21")
    em.emit()

    if "1" not in A_PHASES:
        st.close()
        return
    em = Em(ctx, "A1")
    kmax_ops = []
    km0 = em.op("gpsimd", lambda g: g.memset(sc["KMAX"][:], 0.0), tag="km0")
    prev = {}
    for tqc in range(NTQ):
        kmm = mm_chain(ctx, em, bk(ctx, 0, p=64),
                       [sc["WKT"][:, 64 * ci:64 * (ci + 1)] for ci in range(C)],
                       [xsl(ctx, ci, tqc) for ci in range(C)],
                       after=[prev.get("kev"), prev.get("ksq")], tag="kmm")
        kev = em.op("scalar", lambda s, tqc=tqc: s.copy(
            sc["K2"][0:64, TQ * tqc:TQ * (tqc + 1)], bk(ctx, 0, p=64)),
            after=[kmm], tag="kev")
        ksq = em.op("scalar", lambda s: s.activation(
            sc["SQS"][0:64, 0:TQ], bk(ctx, 0, p=64), AF.Square), after=[kmm], tag="ksq")
        knm = em.op("tensor", lambda t: t.matmul(
            bk(ctx, 1, p=1), ctx.t["ONESC"][0:64, :], sc["SQS"][0:64, 0:TQ],
            start=True, stop=True), after=[ksq, prev.get("kred")], tag="knm")
        kred = em.op("vector", lambda v: v.tensor_reduce(
            sc["KROW"][0:1, 0:1], bk(ctx, 1, p=1), AX.X, ALU.max),
            after=[knm, prev.get("kacc")], tag="kred")
        kacc = em.op("vector", lambda v: v.tensor_tensor(
            sc["KMAX"][:], sc["KMAX"][:], sc["KROW"][0:1, 0:1], ALU.max),
            after=[kred, km0], tag="kacc")
        vmm = mm_chain(ctx, em, bk(ctx, 2, p=64),
                       [sc["WVT"][:, 64 * ci:64 * (ci + 1)] for ci in range(C)],
                       [xsl(ctx, ci, tqc) for ci in range(C)],
                       after=[prev.get("vev")], tag="vmm")
        vev = em.op("scalar", lambda s: s.copy(sc["VTMP"][:], bk(ctx, 2, p=64)),
                    after=[vmm], tag="vev")
        vpe = prev.get("vpe")
        for j in range(4):
            def tfn(t, j=j, tqc=tqc):
                return t.transpose(ctx.t["PSB"][:, 64 * j:64 * (j + 1)],
                                   sc["VTMP"][0:64, 128 * j:128 * (j + 1)],
                                   ctx.t["EYEB"][0:64, 0:64])
            vtr = em.op("tensor", tfn, after=[vev, vpe], tag="vtr")
            def efn(v, j=j, tqc=tqc):
                jj = 4 * tqc + j
                return v.tensor_copy(sc["VP"][:, 65 * jj:65 * jj + 64],
                                     ctx.t["PSB"][:, 64 * j:64 * (j + 1)])
            vpe = em.op("vector", efn, after=[vtr, ones_vp], tag="vpe")
        prev = dict(kev=kev, ksq=ksq, kred=kred, vev=vev, vpe=vpe, kacc=kacc)
        kmax_ops.append(kacc)
    kroot = em.op("scalar", lambda s: s.activation(sc["KROW"][0:1, 0:1],
                                                   sc["KMAX"][:], AF.Sqrt),
                  after=[kmax_ops[-1]], tag="kroot")
    kcsn = em.op("vector", lambda v: v.tensor_scalar(
        sc["KCSN"][:], sc["KROW"][0:1, 0:1], sc["S0SB"][:], -1.0,
        ALU.mult, ALU.mult), after=[kroot, s0ev], tag="kcsn")
    em.emit()

    if "2" not in A_PHASES and "q" not in A_PHASES and "a" not in A_PHASES \
            and "f" not in A_PHASES:
        st.close()
        return
    for tqc in range(NTQ):
        em = Em(ctx, f"A2_{tqc}")
        qdeps = []
        bank_readers = {0: [], 1: [], 4: [], 5: []}
        prev_c0 = prev_c1 = None
        for ro in range(C):
            b = ro % 2
            qmm = mm_chain(
                ctx, em, bk(ctx, b),
                [sc["WQT"][:, D * ci + 128 * ro: D * ci + 128 * (ro + 1)]
                 for ci in range(C)],
                [xsl(ctx, ci, tqc) for ci in range(C)],
                after=bank_readers[b], tag=f"qmm{ro}")
            bank_readers[b] = []
            ev0 = em.op("vector", lambda v, ro=ro, b=b: v.tensor_scalar(
                sc["Q2"][0:64, TQ * 2 * ro:TQ * (2 * ro + 1)],
                bk(ctx, b, p0=0, p=64), sc["S0COL"][0:64, :], None, ALU.mult),
                after=[qmm], tag="qev0")
            ev1 = em.op("vector", lambda v, ro=ro, b=b: v.tensor_scalar(
                sc["Q2"][0:64, TQ * (2 * ro + 1):TQ * (2 * ro + 2)],
                bk(ctx, b, p0=64, p=64), sc["S0COL"][64:128, :], None, ALU.mult),
                after=[qmm], tag="qev1")
            if "n" in A_PHASES:
                bank_readers[b] = [ev0, ev1]
                qdeps += [ev0, ev1]
                continue
            sq = em.op("scalar", lambda s, b=b: s.activation(
                sc["SQS"][:], bk(ctx, b), AF.Square), after=[qmm, ev0, ev1],
                tag="qsq")
            bank_readers[b] = [ev0, ev1, sq]
            nb = 4 + b
            if "x" in A_PHASES:
                bank_readers[nb] = []
                qdeps += [ev0, ev1, sq]
                continue
            nmm = em.op("tensor", lambda t, b=b, nb=nb: t.matmul(
                bk(ctx, nb), ctx.t["PAT2F"][:], sc["SQS"][:],
                start=True, stop=True), after=[sq] + bank_readers[nb],
                tag="qnmm")
            if "m" in A_PHASES:
                bank_readers[nb] = []
                qdeps += [ev0, ev1]
                ctx._nmmlast = nmm
                continue
            s0r = em.op("scalar", lambda s, nb=nb: s.activation(
                sc["SNQ0"][0:1, :], bk(ctx, nb, p=1), AF.Sqrt),
                after=[nmm, prev_c0], tag="qsn0")
            s1r = em.op("scalar", lambda s, nb=nb: s.activation(
                sc["SNQ1"][0:1, :], bk(ctx, nb, p0=64, p=1), AF.Sqrt),
                after=[nmm, prev_c1], tag="qsn1")
            bank_readers[nb] = [s0r, s1r]
            if "s" in A_PHASES:
                bank_readers[nb] = [s0r, s1r]
                qdeps += [ev0, ev1, s0r, s1r]
                continue
            c0 = em.op("vector", lambda v, ro=ro: v.tensor_scalar(
                sc["Q2"][64:65, TQ * 2 * ro:TQ * (2 * ro + 1)],
                sc["SNQ0"][0:1, :], sc["KCSN"][:], SHIFT, ALU.mult, ALU.add),
                after=[s0r], tag="qc0")
            c1 = em.op("vector", lambda v, ro=ro: v.tensor_scalar(
                sc["Q2"][64:65, TQ * (2 * ro + 1):TQ * (2 * ro + 2)],
                sc["SNQ1"][0:1, :], sc["KCSN"][:], SHIFT, ALU.mult, ALU.add),
                after=[s1r], tag="qc1")
            qdeps += [ev0, ev1, c0, c1]
            prev_c0, prev_c1 = c0, c1
        qdeps += bank_readers[0] + bank_readers[1] + bank_readers[4] + bank_readers[5]

        if "q" in A_PHASES and "a" not in A_PHASES and "f" not in A_PHASES:
            em.emit()
            continue
        pv_op = None
        last_exp = None
        prev_n1 = None
        prev_n6 = None
        for h in range(16):
            first_mm_dep = qdeps + [prev_n1, prev_n6]
            for jb in range(8):
                mm_ops = []
                for j4 in range(4):
                    j = 4 * jb + j4
                    def sfn(t, j=j, j4=j4, h=h):
                        return t.matmul(
                            bk(ctx, j4),
                            sc["K2"][:, 128 * j:128 * (j + 1)],
                            sc["Q2"][:, TQ * h:TQ * (h + 1)],
                            start=True, stop=True)
                    dep = [last_exp]
                    if h == 0 and jb == 0:
                        dep = dep + first_mm_dep
                    mm_ops.append(em.op("tensor", sfn, after=dep, tag="smm"))
                pb = jb % 2
                def efn(s, pb=pb):
                    return s.activation(sc["PT"][:, 0:2048],
                                        bkspan(ctx, 0, 4), AF.Exp)
                eop = em.op("scalar", efn, after=mm_ops + [pv_op], tag="exp")
                last_exp = eop
                for j4 in range(4):
                    j = 4 * jb + j4
                    def pvfn(t, j=j, j4=j4, pb=pb):
                        return t.matmul(
                            bk(ctx, 6, p=65),
                            sc["VP"][:, 65 * j:65 * (j + 1)],
                            sc["PT"][:, TQ * j4:TQ * (j4 + 1)],
                            start=(j == 0), stop=(j == NT - 1))
                    dep = [eop]
                    if j == 0:
                        dep += [prev_n1, prev_n6]
                    pv_op = em.op("tensor", pvfn, after=dep, tag="pv")
            n1 = em.op("vector", lambda v: v.tensor_scalar(
                sc["SROW"][0:1, :], bk(ctx, 6, p0=64, p=1), 1e-30, None,
                ALU.max), after=[pv_op], tag="n1")
            n2 = em.op("scalar", lambda s: s.activation(sc["SROW2"][0:1, :],
                                                        sc["SROW"][0:1, :], AF.Ln),
                       after=[n1], tag="n2")
            n3 = em.op("scalar", lambda s: s.activation(sc["RROW"][0:1, :],
                                                        sc["SROW2"][0:1, :],
                                                        AF.Exp, scale=-1.0),
                       after=[n2], tag="n3")
            n4 = em.op("tensor", lambda t: t.matmul(
                bk(ctx, 5, p=64), ctx.t["ONESR"][0:1, 0:64], sc["RROW"][0:1, :],
                start=True, stop=True), after=[n3], tag="n4")
            n5 = em.op("scalar", lambda s: s.copy(sc["RBC"][:], bk(ctx, 5, p=64)),
                       after=[n4], tag="n5")
            def n6fn(v, h=h):
                ro, half = h // 2, h % 2
                return v.tensor_tensor(
                    sc["AON"][64 * half:64 * half + 64, TQ * ro:TQ * (ro + 1)],
                    bk(ctx, 6, p0=0, p=64), sc["RBC"][:], ALU.mult)
            n6 = em.op("vector", n6fn, after=[n5, pv_op], tag="n6")
            prev_n1, prev_n6 = n1, n6

        if "a" in A_PHASES and "f" not in A_PHASES:
            em.emit()
            continue
        pre_deps = []
        for co in range(C):
            b = co % 2
            dep = [prev_n6, last_exp]
            if len(pre_deps) >= 2:
                dep.append(pre_deps[-2])
            omm = mm_chain(
                ctx, em, bk(ctx, b),
                [sc["WOT"][:, D * ci + 128 * co: D * ci + 128 * (co + 1)]
                 for ci in range(C)],
                [sc["AON"][:, TQ * ci:TQ * (ci + 1)] for ci in range(C)],
                after=dep, tag=f"omm{co}")
            r1 = em.op("vector", lambda v, co=co, b=b: v.tensor_tensor(
                sc["PREF"][:, TQ * co:TQ * (co + 1)], bk(ctx, b),
                xsl(ctx, co, tqc), ALU.add), after=[omm], tag="resadd")
            pre_deps.append(r1)
        emit_ln_update(ctx, em, tqc,
                       lambda c: sc["PREF"][:, TQ * c:TQ * (c + 1)],
                       pre_deps + [prev_n1], sc)
        em.emit()
    st.close()



def build_program(blocks, core):
    nc = bass.Bass("TRN2", target_bir_lowering=False)
    stack = ExitStack()
    ctx = Ctx(nc, stack)
    ctx.core = core
    ctx.blocks = blocks
    ctx.dram = {}

    ctx.dram["xin"] = nc.dram_tensor("xin", [T, D], F32, kind="ExternalInput").ap()
    ctx.dram["xout"] = nc.dram_tensor("xout", [T, D], F32, kind="ExternalOutput").ap()
    input_names = ["xin"]
    for cname, (shape, dt) in CONST_SHAPES.items():
        ctx.dram[cname] = nc.dram_tensor(cname, shape, dt, kind="ExternalInput").ap()
        input_names.append(cname)
    tabs = []
    if "B" in blocks:
        tabs += ["COS0", "SIN0"]
    if "C" in blocks:
        tabs += ["COS1", "SIN1"]
    if "B" in blocks or "C" in blocks:
        tabs += ["CMASK"]
    for cname in tabs:
        shape, dt = TABLE_SHAPES[cname]
        ctx.dram[cname] = nc.dram_tensor(cname, shape, dt, kind="ExternalInput").ap()
        input_names.append(cname)
    needed = []
    for b in blocks:
        needed += BLOCK_WEIGHTS[b]
    for wname in needed:
        ctx.dram[wname] = nc.dram_tensor(wname, WEIGHTS[wname], F32,
                                         kind="ExternalInput").ap()
        input_names.append(wname)
    ctx.taps = {}
    for tname, shape, dt in DEBUG_TAPS:
        ctx.taps[tname] = nc.dram_tensor(tname, shape, dt, kind="ExternalOutput").ap()
    ctx.dram["rowscratch"] = nc.dram_tensor("rowscratch", [32, 128], F32).ap()
    if "B" in blocks:
        ctx.dram["cc_in"] = nc.dram_tensor("cc_in", [8, 2], F32).ap()
        ctx.dram["cc_out"] = nc.dram_tensor("cc_out", [64, 2], F32).ap()

    ctx.sb("X", [128, C * T], BF16)
    ctx.sb("PREBC", [128, TQ], BF16)
    ctx.sb("SQBC", [128, TQ], BF16)
    ctx.sb("ROWA", [1, TQ], F32)
    ctx.sb("ROWB", [1, TQ], F32)
    ctx.sb("RSTDB", [1, TQ], BF16)
    ctx.sb("MRB", [1, TQ], BF16)
    ctx.sb("APPA", [128, TQ], F32)
    ctx.t["PSUM"] = stack.enter_context(nc.psum_tensor("PSUM", [128, 3584], F32))
    ctx.t["PSB"] = stack.enter_context(nc.psum_tensor("PSB", [128, 1024], BF16))
    for cname, (shape, dt) in CONST_SHAPES.items():
        ctx.sb(cname, shape, dt)
    names = ["Dcc"]
    for e in ENGINES:
        names.append("E" + e)
        for ch in range(DMA_CHANNELS[e]):
            names.append(f"D{e}{ch}")
    ctx.sems = Sems(nc, stack, names)
    ctx.dma_rr = {e: 0 for e in ENGINES}

    emit_load_constants_and_x(ctx)
    for b in blocks:
        if b == "A":
            emit_block_a(ctx)
        elif b == "B":
            emit_block_b(ctx)
        elif b == "C":
            emit_block_c(ctx)
        elif b == "D":
            emit_block_d(ctx)
    emit_store_x(ctx)
    stack.close()
    return nc, input_names


B_PHASES = "all"


def emit_block_b(ctx):
    nc = ctx.nc
    st = ExitStack()
    sc = ctx.t
    ctx.sb("G1C", [128, C], F32, st)
    ctx.sb("G2C", [128, C], F32, st)
    ctx.sb("RW", [128, C * 8], F32, st)
    ctx.sb("RB8", [8, 1], F32, st)
    ctx.sb("SCR8", [128, C * T], BF16, st)

    st0 = ExitStack()
    ctx.sb("GROW", [1, D], F32, st0)
    em = Em(ctx, "B0")
    emit_weight_load_natural(ctx, em, "L0_ffn_router_w", "RW")
    em.dma("sync", sc["RB8"][:], bass.AP(ctx.dram["L0_ffn_router_b"].tensor, 0,
                                         [[1, 8], [1, 1]]), tag="rb")
    gprev = None
    for gname, gsb in [("L0_ffn_g1", "G1C"), ("L0_ffn_g2", "G2C")]:
        gld = em.dma("sync", sc["GROW"][:],
                     bass.AP(ctx.dram[gname].tensor, 0, [[D, 1], [1, D]]),
                     after=[gprev], tag=f"g:{gname}")
        gtr = []
        for c in range(C):
            def tfn(t, c=c):
                return t.transpose(ctx.t["PSUM"][:, c:c + 1],
                                   sc["GROW"][0:1, 128 * c:128 * (c + 1)],
                                   ctx.t["EYEF"][0:1, 0:1])
            gtr.append(em.op("tensor", tfn,
                             after=[gld] + (gtr[-1:] or [gprev]), tag="gtr"))
        gprev = em.op("vector", lambda v, gsb=gsb: v.tensor_copy(
            sc[gsb][:, 0:C], ctx.t["PSUM"][:, 0:C]), after=[gtr[-1]], tag="gev")
    em.op("gpsimd", lambda g: g.memset(sc["SCR8"][:], 0.0), tag="scr0")
    em.emit()
    st0.close()

    stA = ExitStack()
    ctx.sb("COS0S", [128, TQ], BF16, stA)
    ctx.sb("SIN0S", [128, TQ], BF16, stA)
    ctx.sb("CMASK", [128, 2048], BF16, stA)
    ctx.sb("K2G", [65, T], BF16, stA)
    ctx.sb("VPG", [128, 65 * NT], BF16, stA)
    ctx.sb("WQG", [128, C * 256], BF16, stA)
    ctx.sb("WOG", [128, 2 * D], BF16, stA)
    ctx.sb("WKG", [128, C * 64], BF16, stA)
    ctx.sb("WVG", [128, C * 64], BF16, stA)
    ctx.sb("XR1C", [128, C * TQ], BF16, stA)
    ctx.sb("QA", [128, TQ], BF16, stA)
    ctx.sb("QAS", [128, TQ], BF16, stA)
    ctx.sb("RT1", [128, TQ], F32, stA)
    ctx.sb("Q2G", [65, 4 * TQ], BF16, stA)
    ctx.sb("SQS", [128, TQ], BF16, stA)
    ctx.t["SNQ0"] = ctx.t["ROWA"]
    ctx.t["SNQ1"] = ctx.t["ROWB"]
    ctx.t["SROW"] = ctx.t["ROWB"]
    ctx.t["SROW2"] = ctx.t["ROWA"]
    ctx.sb("KM4", [1, 4], F32, stA)
    ctx.sb("KC4", [1, 4], F32, stA)
    ctx.sb("PT", [128, 2048], BF16, stA)
    ctx.sb("RROW", [1, TQ], BF16, stA)
    ctx.sb("RBC", [64, TQ], BF16, stA)
    ctx.sb("AONG", [128, 2 * TQ], BF16, stA)

    em = Em(ctx, "Bpre")
    em.dma("sync", sc["CMASK"][:], ctx.dram["CMASK"], tag="cmask")
    em.op("gpsimd", lambda g: g.memset(sc["K2G"][64:65, :], 1.0), tag="k2g1")
    em.op("gpsimd", lambda g: g.memset(sc["VPG"][:], 1.0), tag="vpg1")
    em.op("gpsimd", lambda g: g.memset(sc["KM4"][:], 0.0), tag="km4")
    em.emit()

    def rope_pair(em, psum_bank, tq0, dst0, dst1, deps, scale, prev, p=128,
                  extra_ev=()):
        ev = em.op("scalar", lambda s: s.activation(
            sc["QA"][0:p, 0:TQ], bk(ctx, psum_bank, p=p), AF.Copy, scale=scale),
            after=deps + [prev.get("t1"), prev.get("rot")] + list(extra_ev),
            tag="rp_ev")
        rot = em.op("tensor", lambda t: t.matmul(
            bk(ctx, 3, p=p), ctx.t["PN"][0:p, 0:p], sc["QA"][0:p, 0:TQ],
            start=True, stop=True), after=[ev, prev.get("t2")], tag="rp_rot")
        t1 = em.op("vector", lambda v: v.tensor_tensor(
            sc["RT1"][0:p, 0:TQ], sc["QA"][0:p, 0:TQ],
            sc["COS0S"][0:p, 0:TQ], ALU.mult),
            after=[ev, prev.get("o0"), prev.get("o1"), prev.get("tbl")],
            tag="rp_t1")
        t2 = em.op("vector", lambda v: v.tensor_tensor(
            sc["QAS"][0:p, 0:TQ], bk(ctx, 3, p=p),
            sc["SIN0S"][0:p, 0:TQ], ALU.mult),
            after=[rot, prev.get("o0"), prev.get("o1"), prev.get("tbl")],
            tag="rp_t2")
        o0 = em.op("vector", lambda v: v.tensor_tensor(
            dst0, sc["RT1"][0:64, 0:TQ], sc["QAS"][0:64, 0:TQ], ALU.add),
            after=[t1, t2], tag="rp_o0")
        o1 = o0
        if p == 128:
            o1 = em.op("vector", lambda v: v.tensor_tensor(
                dst1, sc["RT1"][64:128, 0:TQ], sc["QAS"][64:128, 0:TQ],
                ALU.add), after=[t1, t2], tag="rp_o1")
        return dict(ev=ev, rot=rot, t1=t1, t2=t2, o0=o0, o1=o1)

    wq_t = ctx.dram["L0_ffn_wq"].tensor
    wo_t = ctx.dram["L0_ffn_wo"].tensor
    wk_t = ctx.dram["L0_ffn_wk"].tensor
    wv_t = ctx.dram["L0_ffn_wv"].tensor

    for gp in range(4):
        em = Em(ctx, f"Bw_{gp}")
        for hl in range(4):
            em.dma("gpsimd",
                   bass.AP(sc["WQG"], 64 * hl, [[C * 256, 128], [256, C], [1, 64]]),
                   bass.AP(wq_t, 64 * (gp + 4 * hl),
                           [[D, 128], [128 * D, C], [1, 64]]),
                   tag=f"wqg{hl}")
        for hl in range(4):
            em.dma("gpsimd",
                   sc["WOG"][64 * (hl % 2):64 * (hl % 2) + 64,
                             D * (hl // 2):D * (hl // 2) + D],
                   bass.AP(wo_t, 64 * (gp + 4 * hl) * D,
                           [[D, 64], [1, 1], [1, D]]),
                   tag=f"wog{hl}")
        em.dma("gpsimd", sc["WKG"][:, 0:C * 64],
               bass.AP(wk_t, 64 * gp, [[256, 128], [128 * 256, C], [1, 64]]),
               tag="wkg")
        em.dma("gpsimd", sc["WVG"][:, 0:C * 64],
               bass.AP(wv_t, 64 * gp, [[256, 128], [128 * 256, C], [1, 64]]),
               tag="wvg")
        em.emit()

        em = Em(ctx, f"Bkv_{gp}")
        prevb = {}
        rp = {}
        kn_prev = None
        for tqc in range(NTQ):
            tq0 = TQ * tqc
            tl0 = em.dma("sync", sc["COS0S"][:],
                         ctx.dram["COS0"][:, tq0:tq0 + TQ],
                         after=[rp.get("t1")], tag="tbl_c")
            tl1 = em.dma("sync", sc["SIN0S"][:],
                         ctx.dram["SIN0"][:, tq0:tq0 + TQ],
                         after=[rp.get("t2")], tag="tbl_s")
            tbl = em.op("sync", lambda e: None, after=[tl0, tl1], tag="tblw")
            rp["tbl"] = tbl
            rstd = emit_rms_chunkstats(
                ctx, em, sc, lambda c: xsl(ctx, c, tqc), 1e-6,
                [prevb.get("xr_last")])
            bcm = em.op("tensor", lambda t: t.matmul(
                bk(ctx, 2), ctx.t["ONESR"][:], sc["RSTDB"][0:1, 0:TQ],
                start=True, stop=True), after=[rstd, prevb.get("xr_last")],
                tag="b1_bc")
            xrs = []
            for c in range(C):
                xin = xsl(ctx, c, tqc)
                xrs.append(em.op("vector", lambda v, c=c, xin=xin: v.scalar_tensor_tensor(
                    sc["XR1C"][:, TQ * c:TQ * (c + 1)], xin,
                    sc["G1C"][:, c:c + 1], bk(ctx, 2), ALU.mult, ALU.mult),
                    after=[bcm] + xrs[-1:], tag="b1_xr"))
            if gp == 0 and tqc == 0 and "XR1C" in ctx.taps:
                tp1 = emit_taps(ctx, em, "XR1C", sc["XR1C"][:, 0:C * TQ], xrs)
                tp2 = emit_taps(ctx, em, "G1CT", sc["G1C"][:, 0:C], xrs)
                tp3 = emit_taps(ctx, em, "WKGT", sc["WKG"][:, 0:C * 64], xrs)
                tp4 = emit_taps(ctx, em, "RSTDT", sc["RSTDB"][0:1, 0:TQ], xrs)
                tp5 = emit_taps(ctx, em, "XTAP", ctx.t["X"][:, 0:TQ], xrs)
                xrs = xrs + [em.op("vector", lambda v: None,
                                   after=[tp1, tp2, tp3, tp4, tp5, xrs[-1]],
                                   tag="tapgate")]
            kmm = mm_chain(ctx, em, bk(ctx, 0, p=64),
                           [sc["WKG"][:, 64 * ci:64 * (ci + 1)]
                            for ci in range(C)],
                           [sc["XR1C"][:, TQ * ci:TQ * (ci + 1)]
                            for ci in range(C)],
                           after=xrs + [rp.get("ev")], tag="kmm")
            rp["tbl"] = tbl
            rp = rope_pair(em, 0, tq0,
                           sc["K2G"][0:64, tq0:tq0 + TQ], None,
                           [kmm], 1.0, rp, p=64,
                           extra_ev=[prevb.get("vpe")])
            qq = em.op("scalar", lambda s, tq0=tq0: s.activation(
                sc["SQS"][0:64, 0:TQ], sc["K2G"][0:64, tq0:tq0 + TQ],
                AF.Square), after=[rp["o0"], kn_prev], tag="kn_sq")
            nm = em.op("tensor", lambda t: t.matmul(
                bk(ctx, 5, p=1), ctx.t["ONESCB"][0:64, :],
                sc["SQS"][0:64, 0:TQ], start=True, stop=True),
                after=[qq, kn_prev], tag="kn_mm")
            rd = em.op("vector", lambda v: v.tensor_reduce(
                sc["SNQ0"][0:1, 0:1], bk(ctx, 5, p=1), AX.X, ALU.max),
                after=[nm, kn_prev], tag="kn_rd")
            ka = em.op("vector", lambda v, gp=gp: v.tensor_tensor(
                sc["KM4"][0:1, gp:gp + 1], sc["KM4"][0:1, gp:gp + 1],
                sc["SNQ0"][0:1, 0:1], ALU.max), after=[rd], tag="kn_acc")
            kn_prev = ka
            vmm = mm_chain(ctx, em, bk(ctx, 1, p=64),
                           [sc["WVG"][:, 64 * ci:64 * (ci + 1)]
                            for ci in range(C)],
                           [sc["XR1C"][:, TQ * ci:TQ * (ci + 1)]
                            for ci in range(C)],
                           after=xrs + [prevb.get("vev")], tag="vmm")
            vev = em.op("scalar", lambda s: s.copy(sc["QA"][0:64, 0:TQ],
                                                   bk(ctx, 1, p=64)),
                        after=[vmm, rp.get("t1"), rp.get("rot"),
                               prevb.get("vpe")], tag="vev")
            prevb["vev"] = vev
            vpe = prevb.get("vpe")
            for j in range(4):
                def tfn(t, j=j):
                    return t.transpose(
                        ctx.t["PSB"][:, 64 * j:64 * (j + 1)],
                        sc["QA"][0:64, 128 * j:128 * (j + 1)],
                        ctx.t["EYEB"][0:64, 0:64])
                vtr = em.op("tensor", tfn, after=[vev, vpe], tag="vtr")
                def efn(v, j=j, tqc=tqc):
                    jj = 4 * tqc + j
                    return v.tensor_copy(
                        sc["VPG"][:, 65 * jj:65 * jj + 64],
                        ctx.t["PSB"][:, 64 * j:64 * (j + 1)])
                vpe = em.op("vector", efn, after=[vtr], tag="vpe")
            prevb["vpe"] = vpe
            prevb["xr_last"] = em.op("vector", lambda v: None,
                                     after=[kmm, vmm, xrs[-1]], tag="xr_gate")
        kq = em.op("scalar", lambda s, gp=gp: s.activation(
            sc["KC4"][0:1, gp:gp + 1], sc["KM4"][0:1, gp:gp + 1], AF.Sqrt),
            after=[kn_prev], tag="kq")
        em.op("vector", lambda v, gp=gp: v.tensor_scalar(
            sc["KC4"][0:1, gp:gp + 1], sc["KC4"][0:1, gp:gp + 1], -1.0, None,
            ALU.mult), after=[kq], tag="kneg")
        if B_PHASES == "kv" and "K2GD" in ctx.taps:
            emit_taps(ctx, em, "K2GD", sc["K2G"][0:65, 0:T], [kq, prevb["vpe"]])
            emit_taps(ctx, em, "VPGD", sc["VPG"][:, 0:65 * NT], [kq, prevb["vpe"]])
        em.emit()
        if B_PHASES == "kv":
            stA.close()
            st.close()
            return

        for tqc in range(NTQ):
            tq0 = TQ * tqc
            em = Em(ctx, f"Bq_{gp}_{tqc}")
            tl0 = em.dma("sync", sc["COS0S"][:],
                         ctx.dram["COS0"][:, tq0:tq0 + TQ], tag="tbl_c")
            tl1 = em.dma("sync", sc["SIN0S"][:],
                         ctx.dram["SIN0"][:, tq0:tq0 + TQ], tag="tbl_s")
            tbl = em.op("sync", lambda e: None, after=[tl0, tl1], tag="tblw")
            rstd = emit_rms_chunkstats(ctx, em, sc,
                                       lambda c: xsl(ctx, c, tqc), 1e-6, [])
            bcm = em.op("tensor", lambda t: t.matmul(
                bk(ctx, 2), ctx.t["ONESR"][:], sc["RSTDB"][0:1, 0:TQ],
                start=True, stop=True), after=[rstd], tag="b2_bc")
            xrs = []
            for c in range(C):
                xrs.append(em.op("vector", lambda v, c=c: v.scalar_tensor_tensor(
                    sc["XR1C"][:, TQ * c:TQ * (c + 1)], xsl(ctx, c, tqc),
                    sc["G1C"][:, c:c + 1], bk(ctx, 2), ALU.mult, ALU.mult),
                    after=[bcm] + xrs[-1:], tag="b2_xr"))
            rp2 = {"tbl": tbl}
            crows = [None, None]
            qdeps = []
            prev_rd = []
            for ro2 in range(2):
                qmm = mm_chain(ctx, em, bk(ctx, 0),
                               [sc["WQG"][:, 256 * ci + 128 * ro2:
                                         256 * ci + 128 * (ro2 + 1)]
                                for ci in range(C)],
                               [sc["XR1C"][:, TQ * ci:TQ * (ci + 1)]
                                for ci in range(C)],
                               after=xrs + prev_rd + [rp2.get("ev")],
                               tag=f"b2_qmm{ro2}")
                hl0, hl1 = 2 * ro2, 2 * ro2 + 1
                rp2 = rope_pair(em, 0, tq0,
                                sc["Q2G"][0:64, TQ * hl0:TQ * hl0 + TQ],
                                sc["Q2G"][0:64, TQ * hl1:TQ * hl1 + TQ],
                                [qmm], 0.125, rp2)
                prev_rd = [rp2["ev"]]
                c0, c1 = emit_cnorm(
                    ctx, em, sc,
                    sc["Q2G"][0:64, TQ * hl0:TQ * hl0 + TQ],
                    sc["Q2G"][0:64, TQ * hl1:TQ * hl1 + TQ],
                    sc["KC4"][0:1, gp:gp + 1], sc["KC4"][0:1, gp:gp + 1],
                    sc["Q2G"][64:65, TQ * hl0:TQ * hl0 + TQ],
                    sc["Q2G"][64:65, TQ * hl1:TQ * hl1 + TQ],
                    [rp2["o0"], rp2["o1"]], crows)
                crows = [c0, c1]
                qdeps += [rp2["o0"], rp2["o1"], c0, c1]
            jn = 4 * tqc + 4
            pn1, pn6, lexp = emit_attn_sweep(
                ctx, em, sc, 4, lambda h: jn,
                lambda h, j: sc["K2G"][:, 128 * j:128 * (j + 1)],
                lambda h, j: sc["VPG"][:, 65 * j:65 * (j + 1)],
                lambda h: sc["Q2G"][:, TQ * h:TQ * (h + 1)],
                lambda h: sc["AONG"][64 * (h % 2):64 * (h % 2) + 64,
                                     TQ * (h // 2):TQ * (h // 2) + TQ],
                True, qdeps, tap_pt=(gp == 0 and tqc == 0))
            if gp == 0 and tqc == 0 and "AONG0" in ctx.taps:
                emit_taps(ctx, em, "AONG0", sc["AONG"][:, 0:2 * TQ], [pn6])
                if "VPG0" in ctx.taps:
                    emit_taps(ctx, em, "VPG0", sc["VPG"][:, 0:130], [pn6])
                if "K2G0" in ctx.taps:
                    emit_taps(ctx, em, "K2G0", sc["K2G"][0:65, 0:512], [pn6])
            pre_prev = []
            for co in range(C):
                b = co % 2
                dep = [pn6, lexp] + pre_prev[-2:-1]
                omm = mm_chain(ctx, em, bk(ctx, b),
                               [sc["WOG"][:, D * r + 128 * co:
                                         D * r + 128 * (co + 1)]
                                for r in range(2)],
                               [sc["AONG"][:, TQ * r:TQ * (r + 1)]
                                for r in range(2)],
                               after=dep, tag=f"b2_omm{co}")
                r1 = em.op("vector", lambda v, co=co, b=b, tq0=tq0: v.tensor_tensor(
                    sc["SCR8"][:, T * co + tq0:T * co + tq0 + TQ],
                    sc["SCR8"][:, T * co + tq0:T * co + tq0 + TQ],
                    bk(ctx, b), ALU.add), after=[omm], tag="b2_acc")
                pre_prev.append(r1)
            em.emit()
    stA.close()

    em = Em(ctx, "B3")
    pr = []
    for tqc in range(NTQ):
        for co in range(C):
            tq0 = TQ * tqc
            r1 = em.op("vector", lambda v, co=co, tq0=tq0: v.tensor_tensor(
                sc["SCR8"][:, T * co + tq0:T * co + tq0 + TQ],
                sc["SCR8"][:, T * co + tq0:T * co + tq0 + TQ],
                xsl(ctx, co, tq0 // TQ), ALU.add), after=pr[-1:], tag="b3_h")
            r2 = em.op("vector", lambda v, co=co, tq0=tq0: v.tensor_tensor(
                xsl(ctx, co, tq0 // TQ), xsl(ctx, co, tq0 // TQ),
                sc["SCR8"][:, T * co + tq0:T * co + tq0 + TQ], ALU.add),
                after=[r1], tag="b3_s")
            pr.append(r2)
    em.emit()

    if B_PHASES == "att":
        stR = ExitStack()
        stR.close()
        st.close()
        return
    stR = ExitStack()
    ctx.sb("FE8", [8, T], BF16, stR)
    ctx.sb("WXB", [1, T], BF16, stR)
    st3 = ExitStack()
    ctx.sb("RT1", [128, TQ], F32, st3)
    ctx.sb("RT2", [128, TQ], F32, st3)
    ctx.sb("LT2", [8, T], F32, st3)
    em = Em(ctx, "B4")
    prevc = {}
    for tqc in range(NTQ):
        tq0 = TQ * tqc
        rstd = emit_rms_chunkstats(
            ctx, em, sc,
            lambda c: sc["SCR8"][:, T * c + tq0:T * c + tq0 + TQ], 1e-6,
            [prevc.get("x2_last"), prevc.get("lgev")])
        bcm = em.op("tensor", lambda t: t.matmul(
            bk(ctx, 2), ctx.t["ONESR"][:], sc["RSTDB"][0:1, 0:TQ],
            start=True, stop=True), after=[rstd, prevc.get("x2_last")],
            tag="b4_bc")
        lmms = []
        x2s = []
        for c in range(C):
            par = c % 2
            x2 = em.op("vector", lambda v, c=c, par=par, tq0=tq0: v.scalar_tensor_tensor(
                sc["RT1" if par == 0 else "RT2"][:, 0:TQ],
                sc["SCR8"][:, T * c + tq0:T * c + tq0 + TQ],
                sc["G2C"][:, c:c + 1], bk(ctx, 2), ALU.mult, ALU.mult),
                after=[bcm] + lmms[-2:-1] + ([prevc.get("lgev")] if c == 0 else []),
                tag="b4_x2")
            lmm = em.op("tensor", lambda t, c=c, par=par: t.matmul(
                bk(ctx, 3, p=8), sc["RW"][:, 8 * c:8 * (c + 1)],
                sc["RT1" if par == 0 else "RT2"][:, 0:TQ],
                start=(c == 0), stop=(c == C - 1)),
                after=[x2], tag="b4_lmm")
            lmms.append(lmm)
            x2s.append(x2)
        lgev = em.op("scalar", lambda s, tq0=tq0: s.activation(
            sc["LT2"][0:8, tq0:tq0 + TQ], bk(ctx, 3, p=8), AF.Identity,
            bias=sc["RB8"][:]), after=[lmms[-1]], tag="b4_lgev")
        prevc = dict(lgev=lgev, x2_last=x2s[-1])
        emit_taps(ctx, em, "LT2", sc["LT2"][0:8, :], [lgev])
    em.emit()
    if B_PHASES == "logits":
        st3.close()
        stR.close()
        st.close()
        return
    st3.close()

    st4 = ExitStack()
    st4a = ExitStack()
    ctx.sb("OH0T", [8, T], BF16, st4)
    ctx.sb("OH1T", [8, T], BF16, st4)
    ctx.sb("CUM0", [8, T], I16, st4)
    ctx.sb("CUM1", [8, T], I16, st4)
    ctx.sb("ONES8T", [8, TQ], F32, st4)
    ctx.sb("TOT", [8, 2], F32, st4)
    ctx.sb("GT64", [64, 2], F32, st4)
    ctx.sb("OFF", [8, 2], F32, st4)
    ctx.sb("OFFS", [8, 1], F32, st4)
    ctx.sb("DROW", [1, TQ], BF16, st4)
    ctx.t["RTA"] = ctx.t["ROWA"]
    ctx.t["RTB"] = ctx.t["ROWB"]
    ctx.t["MSLB"] = ctx.t["RSTDB"]
    ctx.t["WYB"] = ctx.t["MRB"]
    ctx.sb("G8S", [8, TQ], BF16, st4)
    ctx.sb("LTT", [128, NT * 8], F32, st4a)
    ctx.sb("V8", [128, 8], F32, st4a)
    ctx.sb("OHS", [128, 8], F32, st4a)
    ctx.sb("DD", [128, NT], F32, st4a)
    ctx.sb("TRS", [32, 128], F32, st4a)

    em = Em(ctx, "B5")
    evs = []
    trp = None
    for tqc in range(NTQ):
        trs = []
        for j in range(4):
            tt = 4 * tqc + j
            def tfn(t, tt=tt, j=j):
                return t.transpose(ctx.t["PSUM"][:, 8 * j:8 * (j + 1)],
                                   sc["LT2"][0:8, 128 * tt:128 * (tt + 1)],
                                   ctx.t["EYEF"][0:8, 0:8])
            trs.append(em.op("tensor", tfn, after=[trp], tag="b5_ltr"))
        ev = em.op("vector", lambda v, tqc=tqc: v.tensor_copy(
            sc["LTT"][:, 32 * tqc:32 * (tqc + 1)], ctx.t["PSUM"][:, 0:32]),
            after=[trs[-1]], tag="b5_lev")
        trp = ev
        evs.append(ev)
    ohdeps = []
    p_oh = {}
    for tt in range(NT):
        lt = sc["LTT"][:, 8 * tt:8 * (tt + 1)]
        mx8 = em.op("vector", lambda v, lt=lt: v.max(out=sc["V8"][:], in_=lt),
                    after=[evs[tt // 4], p_oh.get("dd")] + ohdeps[-1:],
                    tag="b5_max8")
        dd = em.op("vector", lambda v, tt=tt: v.tensor_tensor(
            sc["DD"][:, tt:tt + 1], sc["V8"][:, 1:2], sc["V8"][:, 0:1],
            ALU.subtract), after=[mx8], tag="b5_dd")
        p_oh["dd"] = dd
        for k, dst in ((0, "OH0T"), (1, "OH1T")):
            oh = em.op("vector", lambda v, lt=lt, k=k: v.tensor_scalar(
                sc["OHS"][:], lt, sc["V8"][:, k:k + 1], None, ALU.is_equal),
                after=[mx8, p_oh.get(("ev", k))], tag="b5_oh")
            tr = em.op("tensor", lambda t, k=k: t.transpose(
                ctx.t["PSUM"][0:8, 64 * k:64 * k + 128], sc["OHS"][:],
                ctx.t["EYEF"][:]), after=[oh, p_oh.get("ev_any")],
                tag="b5_ohtr")
            ev = em.op("vector", lambda v, tt=tt, k=k, dst=dst: v.tensor_copy(
                sc[dst][0:8, 128 * tt:128 * (tt + 1)],
                ctx.t["PSUM"][0:8, 64 * k:64 * k + 128]),
                after=[tr], tag="b5_ohev")
            p_oh[("ev", k)] = ev
            p_oh["ev_any"] = ev
            ohdeps.append(ev)
    dtr = em.op("tensor", lambda t: t.transpose(
        ctx.t["PSUM"][0:32, 512:512 + 128], sc["DD"][:], ctx.t["EYEF"][:]),
        after=[p_oh["dd"]], tag="b5_dtr")
    dte = em.op("vector", lambda v: v.tensor_copy(
        sc["TRS"][:], ctx.t["PSUM"][0:32, 512:512 + 128]), after=[dtr],
        tag="b5_dte")
    d_out = em.dma("gpsimd", ctx.dram["rowscratch"], sc["TRS"][:],
                   after=[dte], tag="b5_dout")
    d_in = d_out
    st4a.close()
    ctx.sb("RTC", [1, TQ], F32, st4)
    ctx.sb("RTD", [1, TQ], F32, st4)
    o8 = em.op("gpsimd", lambda g: g.memset(sc["ONES8T"][:], 1.0), tag="b5_o8")
    sc0p = sc1p = None
    for cname in ("CUM0", "CUM1"):
        oht = "OH0T" if cname == "CUM0" else "OH1T"
        prev_scan = sc0p if cname == "CUM1" else None
        for ch in range(NTQ):
            t0c = TQ * ch
            def scfn(v, cname=cname, oht=oht, ch=ch, t0c=t0c):
                init = 0.0 if ch == 0 else sc[cname][0:8, t0c - 1:t0c]
                return v.tensor_tensor_scan(
                    sc[cname][0:8, t0c:t0c + TQ], sc["ONES8T"][:],
                    sc[oht][0:8, t0c:t0c + TQ], init, ALU.mult, ALU.add)
            prev_scan = em.op("vector", scfn,
                             after=[o8, prev_scan] + (ohdeps if ch == 0 else []),
                             tag=f"b5_scan_{cname}_{ch}")
        if cname == "CUM0":
            sc0p = prev_scan
        else:
            sc1p = prev_scan
    tt0 = em.op("vector", lambda v: v.tensor_copy(
        sc["TOT"][:, 0:1], sc["CUM0"][:, T - 1:T]), after=[sc0p], tag="b5_t0")
    tt1 = em.op("vector", lambda v: v.tensor_copy(
        sc["TOT"][:, 1:2], sc["CUM1"][:, T - 1:T]), after=[sc1p], tag="b5_t1")
    if NO_CC:
        offe = em.op("gpsimd", lambda g: g.memset(sc["OFF"][:], 0.0),
                     after=[tt0, tt1], tag="b5_offz")
    else:
        ccout = em.dma("gpsimd", ctx.dram["cc_in"], sc["TOT"][:],
                       after=[tt0, tt1], tag="b5_ccout")
        def ccfn(g):
            return g.collective_compute(
                "AllGather", ALU.bypass, replica_groups=[list(range(8))],
                ins=[ctx.dram["cc_in"].opt()], outs=[ctx.dram["cc_out"].opt()])
        cc = em.op("gpsimd", ccfn, after=[ccout], dma=True, tag="b5_cc")
        cc.cc = True
        ccin = em.dma("gpsimd", sc["GT64"][:], ctx.dram["cc_out"], after=[cc],
                      tag="b5_ccin")
        pmld = em.op("tensor", lambda t: t.matmul(
            ctx.t["PSUM"][0:8, 0:2], ctx.t["PM64"][:], sc["GT64"][:],
            start=True, stop=True), after=[ccin, evs[-1]], tag="b5_offmm")
        offe = em.op("vector", lambda v: v.tensor_copy(sc["OFF"][:],
                                                       ctx.t["PSUM"][0:8, 0:2]),
                     after=[pmld], tag="b5_offev")
    offs = em.op("vector", lambda v: v.tensor_tensor(
        sc["OFFS"][:], sc["OFF"][:, 0:1], sc["OFF"][:, 1:2], ALU.add),
        after=[offe], tag="b5_offs")
    ss = em.op("vector", lambda v: v.tensor_tensor(
        sc["CUM1"][:], sc["CUM0"][:], sc["CUM1"][:], ALU.add),
        after=[sc1p, tt0, tt1], tag="b5_ss")
    p0 = em.op("vector", lambda v: v.tensor_scalar(
        sc["CUM0"][:], sc["CUM0"][:], sc["OFF"][:, 0:1], None, ALU.add),
        after=[ss, offe], tag="b5_p0")
    p1 = em.op("vector", lambda v: v.tensor_scalar(
        sc["CUM1"][:], sc["CUM1"][:], sc["OFFS"][:], None, ALU.add),
        after=[ss, offs], tag="b5_p1")
    ka0 = em.op("vector", lambda v: v.tensor_scalar(
        sc["CUM0"][:], sc["CUM0"][:], float(T), None, ALU.is_lt),
        after=[p0], tag="b5_ka0")
    ka1 = em.op("vector", lambda v: v.tensor_scalar(
        sc["CUM1"][:], sc["CUM1"][:], float(T), None, ALU.is_lt),
        after=[p1], tag="b5_ka1")
    e0 = em.op("vector", lambda v: v.tensor_tensor(
        sc["OH0T"][:], sc["CUM0"][:], sc["OH0T"][:], ALU.mult),
        after=[ka0], tag="b5_em0")
    e1 = em.op("vector", lambda v: v.tensor_tensor(
        sc["OH1T"][:], sc["CUM1"][:], sc["OH1T"][:], ALU.mult),
        after=[ka1], tag="b5_em1")
    pch = {}
    for tqc in range(NTQ):
        tq0 = TQ * tqc
        a0 = em.op("tensor", lambda t, tq0=tq0: t.matmul(
            bk(ctx, 3, p=1), ctx.t["IOTA8B"][:], sc["OH0T"][0:8, tq0:tq0 + TQ],
            start=True, stop=True), after=[e0, pch.get("fe")], tag="b5_as0")
        a1 = em.op("tensor", lambda t, tq0=tq0: t.matmul(
            bk(ctx, 4, p=1), ctx.t["IOTA8B"][:], sc["OH1T"][0:8, tq0:tq0 + TQ],
            start=True, stop=True), after=[e1, pch.get("fe")], tag="b5_as1")
        t0m = em.op("tensor", lambda t, tq0=tq0: t.matmul(
            bk(ctx, 5, p=1), ctx.t["ONES8B"][:], sc["OH0T"][0:8, tq0:tq0 + TQ],
            start=True, stop=True), after=[e0, pch.get("wy1")], tag="b5_tm0")
        t1m = em.op("tensor", lambda t, tq0=tq0: t.matmul(
            bk(ctx, 6, p=1), ctx.t["ONES8B"][:], sc["OH1T"][0:8, tq0:tq0 + TQ],
            start=True, stop=True), after=[e1, pch.get("wy2")], tag="b5_tm1")
        cpa = em.op("scalar", lambda s: s.copy(sc["RTA"][0:1, :],
                                               bk(ctx, 3, p=1)),
                    after=[a0, pch.get("cpb")], tag="b5_cpa")
        mrow = em.op("vector", lambda v: v.tensor_tensor(
            sc["RTB"][0:1, :], bk(ctx, 4, p=1), sc["RTA"][0:1, :], ALU.max),
            after=[a1, cpa], tag="b5_m")
        mrb = em.op("scalar", lambda s: s.activation(
            sc["MSLB"][0:1, :], sc["RTB"][0:1, :], AF.Identity),
            after=[mrow], tag="b5_mb")
        dch = em.dma("gpsimd", sc["DROW"][0:1, 0:TQ],
                     bass.AP(ctx.dram["rowscratch"].tensor, tq0,
                             [[TQ, 1], [1, TQ]]),
                     after=[d_in, pch.get("sg")], tag="b5_dch")
        sg = em.op("scalar", lambda s: s.activation(
            sc["RTC"][0:1, :], sc["DROW"][0:1, 0:TQ], AF.Sigmoid),
            after=[dch, pch.get("wy1")], tag="b5_sg")
        sg0 = em.op("vector", lambda v: v.tensor_scalar(
            sc["RTD"][0:1, :], sc["RTC"][0:1, :], -1.0, 1.0, ALU.mult,
            ALU.add), after=[sg, pch.get("wy1")], tag="b5_sg0")
        cpb = em.op("scalar", lambda s: s.copy(sc["RTA"][0:1, :],
                                               bk(ctx, 5, p=1)),
                    after=[t0m, mrow], tag="b5_cpb")
        wy1 = em.op("vector", lambda v: v.tensor_tensor(
            sc["RTD"][0:1, :], sc["RTD"][0:1, :], sc["RTA"][0:1, :],
            ALU.mult), after=[cpb, sg0], tag="b5_wy1")
        wy2 = em.op("vector", lambda v: v.tensor_tensor(
            sc["RTC"][0:1, :], sc["RTC"][0:1, :], bk(ctx, 6, p=1), ALU.mult),
            after=[t1m, sg, wy1], tag="b5_wy2")
        wyr = em.op("vector", lambda v: v.tensor_tensor(
            sc["RTD"][0:1, :], sc["RTD"][0:1, :], sc["RTC"][0:1, :], ALU.add),
            after=[wy1, wy2], tag="b5_wy")
        wyb = em.op("scalar", lambda s: s.activation(
            sc["WYB"][0:1, :], sc["RTD"][0:1, :], AF.Identity),
            after=[wyr], tag="b5_wyb")
        wxr = em.op("vector", lambda v, tq0=tq0: v.tensor_scalar(
            sc["WXB"][0:1, tq0:tq0 + TQ], sc["RTD"][0:1, :], -1.0, 1.0,
            ALU.mult, ALU.add), after=[wyr], tag="b5_wx")
        mb8 = em.op("tensor", lambda t: t.matmul(
            bk(ctx, 3, p=8), ctx.t["ONESR"][0:1, 0:8], sc["MSLB"][0:1, :],
            start=True, stop=True), after=[mrb, cpa, a0], tag="b5_mb8")
        ge8 = em.op("vector", lambda v: v.tensor_scalar(
            sc["G8S"][:], bk(ctx, 3, p=8), ctx.t["IOTA8"][:], None,
            ALU.is_equal), after=[mb8], tag="b5_ge8")
        wb8 = em.op("tensor", lambda t: t.matmul(
            bk(ctx, 4, p=8), ctx.t["ONESR"][0:1, 0:8], sc["WYB"][0:1, :],
            start=True, stop=True), after=[wyb, mrow, a1], tag="b5_wb8")
        fe = em.op("vector", lambda v, tq0=tq0: v.scalar_tensor_tensor(
            sc["FE8"][0:8, tq0:tq0 + TQ], sc["G8S"][:], 1.0,
            bk(ctx, 4, p=8), ALU.mult, ALU.mult), after=[ge8, wb8],
            tag="b5_fe")
        pch = dict(mrow=mrow, wy1=wy1, wy2=wy2, fe=fe, cpb=cpb, sg=sg)
        if tqc == NTQ - 1:
            emit_taps(ctx, em, "FE8", sc["FE8"][0:8, :], [fe])
            emit_taps(ctx, em, "WXB", sc["WXB"][0:1, :], [wxr])
            emit_taps(ctx, em, "EM0", sc["OH0T"][0:8, :], [fe])
            emit_taps(ctx, em, "EM1", sc["OH1T"][0:8, :], [fe])
    em.emit()
    st4.close()
    if B_PHASES == "route":
        stR.close()
        st.close()
        return

    st5 = ExitStack()
    ctx.sb("HM", [128, 16 * 1024], BF16, st5)
    ctx.sb("ASB", [128, 1024], BF16, st5)
    ctx.sb("FEB", [128, 1024], BF16, st5)
    ctx.sb("W1C", [128, 1024], BF16, st5)
    ctx.sb("W2C", [128, 1024], BF16, st5)
    ctx.sb("W3CO", [128, 1024], BF16, st5)
    ctx.t["RT1"] = ctx.t["APPA"]
    NQTR = 4
    TPQ = NTQ // NQTR
    w1t = ctx.dram["L0_ffn_w1"].tensor
    w2t = ctx.dram["L0_ffn_w2"].tensor
    w3t = ctx.dram["L0_ffn_w3"].tensor
    for qtr in range(NQTR):
        em = Em(ctx, f"B6_{qtr}")
        tqcs = [qtr * TPQ + i for i in range(TPQ)]
        conv = []
        for tqc in tqcs:
            tq0 = TQ * tqc
            rstd = emit_rms_chunkstats(
                ctx, em, sc,
                lambda c, tq0=tq0: sc["SCR8"][:, T * c + tq0:T * c + tq0 + TQ],
                1e-6, conv[-1:])
            bcm = em.op("tensor", lambda t: t.matmul(
                bk(ctx, 2), ctx.t["ONESR"][:], sc["RSTDB"][0:1, 0:TQ],
                start=True, stop=True), after=[rstd] + conv[-1:], tag="b6_bc")
            wxm = em.op("tensor", lambda t, tq0=tq0: t.matmul(
                bk(ctx, 3), ctx.t["ONESR"][:], sc["WXB"][0:1, tq0:tq0 + TQ],
                start=True, stop=True), after=conv[-2:-1], tag="b6_wxm")
            for c in range(C):
                cv = em.op("vector", lambda v, c=c, tq0=tq0: v.scalar_tensor_tensor(
                    sc["SCR8"][:, T * c + tq0:T * c + tq0 + TQ],
                    sc["SCR8"][:, T * c + tq0:T * c + tq0 + TQ],
                    sc["G2C"][:, c:c + 1], bk(ctx, 2), ALU.mult, ALU.mult),
                    after=[bcm] + conv[-1:], tag="b6_conv")
                t1 = em.op("vector", lambda v, c=c, tq0=tq0: v.tensor_tensor(
                    sc["RT1"][:, 0:TQ],
                    sc["SCR8"][:, T * c + tq0:T * c + tq0 + TQ],
                    bk(ctx, 3), ALU.mult), after=[cv, wxm], tag="b6_t1")
                ac = em.op("vector", lambda v, c=c, tq0=tq0: v.tensor_tensor(
                    xsl(ctx, c, tq0 // TQ), xsl(ctx, c, tq0 // TQ),
                    sc["RT1"][:, 0:TQ], ALU.add), after=[t1], tag="b6_ac")
                conv.append(ac)
        prev_e = {}
        for e in range(8):
            feb_ops = []
            for i, tqc in enumerate(tqcs):
                tq0 = TQ * tqc
                fm = em.op("tensor", lambda t, e=e, tq0=tq0: t.matmul(
                    bk(ctx, 5), ctx.t["E8"][:, 128 * e:128 * (e + 1)],
                    sc["FE8"][0:8, tq0:tq0 + TQ], start=True, stop=True),
                    after=conv[-1:] + feb_ops[-1:] + [prev_e.get("fe")],
                    tag="b6_febmm")
                fe = em.op("scalar", lambda s, i=i: s.activation(
                    sc["FEB"][:, TQ * i:TQ * (i + 1)], bk(ctx, 5),
                    AF.Identity), after=[fm], tag="b6_febev")
                feb_ops.append(fe)
            prev_g = {}
            for hc in range(16):
                w1l = em.dma("gpsimd", sc["W1C"][:, 0:1024],
                             bass.AP(w1t, e * D * 2048 + 128 * hc,
                                     [[2048, 128], [128 * 2048, C], [1, 128]]),
                             after=[prev_g.get("am")], tag="b6_w1l")
                w2l = em.dma("gpsimd", sc["W2C"][:, 0:1024],
                             bass.AP(w2t, e * D * 2048 + 128 * hc,
                                     [[2048, 128], [128 * 2048, C], [1, 128]]),
                             after=[prev_g.get("bm")], tag="b6_w2l")
                ams = []
                bms = []
                for i, tqc in enumerate(tqcs):
                    tq0 = TQ * tqc
                    cdep = conv[-1:] if (hc == 0 and e == 0) else []
                    am = mm_chain(ctx, em, bk(ctx, 0 + i),
                                  [sc["W1C"][:, 128 * ci:128 * (ci + 1)]
                                   for ci in range(C)],
                                  [sc["SCR8"][:, T * ci + tq0:
                                              T * ci + tq0 + TQ]
                                   for ci in range(C)],
                                  after=[w1l, prev_g.get("aev")] + cdep,
                                  tag="b6_amm")
                    bm = mm_chain(ctx, em, bk(ctx, 2 + i),
                                  [sc["W2C"][:, 128 * ci:128 * (ci + 1)]
                                   for ci in range(C)],
                                  [sc["SCR8"][:, T * ci + tq0:
                                              T * ci + tq0 + TQ]
                                   for ci in range(C)],
                                  after=[w2l, prev_g.get("prd")] + cdep,
                                  tag="b6_bmm")
                    ams.append(am)
                    bms.append(bm)
                aev = em.op("scalar", lambda s: s.activation(
                    sc["ASB"][:, 0:1024], bkspan(ctx, 0, 2), AF.Identity),
                    after=ams + [prev_g.get("prd")], tag="b6_aev")
                prd = em.op("vector", lambda v, hc=hc: v.tensor_tensor(
                    sc["HM"][:, 1024 * hc:1024 * (hc + 1)], bkspan(ctx, 2, 2),
                    sc["ASB"][:, 0:1024], ALU.mult),
                    after=bms + [aev, prev_g.get("gel"), prev_g.get("gate")],
                    tag="b6_prd")
                gel = em.op("scalar", lambda s, hc=hc: s.activation(
                    sc["HM"][:, 1024 * hc:1024 * (hc + 1)],
                    sc["HM"][:, 1024 * hc:1024 * (hc + 1)], AF.Gelu),
                    after=[prd, prev_g.get("gate")], tag="b6_gelu")
                gate = em.op("vector", lambda v, hc=hc: v.tensor_tensor(
                    sc["HM"][:, 1024 * hc:1024 * (hc + 1)],
                    sc["HM"][:, 1024 * hc:1024 * (hc + 1)],
                    sc["FEB"][:, 0:1024], ALU.mult), after=[gel] + feb_ops,
                    tag="b6_gate")
                prev_g = dict(am=ams[-1], bm=bms[-1], aev=aev, prd=prd,
                              gel=gel, gate=gate)
            prev_y = {}
            for co in range(C):
                for hh in range(2):
                    w3l = em.dma("gpsimd", sc["W3CO"][:, 0:1024],
                                 bass.AP(w3t, e * 2048 * D + 1024 * hh * D +
                                         128 * co,
                                         [[D, 128], [128 * D, 8], [1, 128]]),
                                 after=[prev_y.get("ym_last")], tag="b6_w3l")
                    for i, tqc in enumerate(tqcs):
                        tq0 = TQ * tqc
                        dep = [w3l, prev_g["gate"], prev_y.get(f"ya{i}")]
                        if co == 0 and hh == 0 and i == 0:
                            dep.append(prev_e.get("ya_last"))
                        ym = mm_chain(
                            ctx, em, bk(ctx, 4 + 2 * i),
                            [sc["W3CO"][:, 128 * hc:128 * (hc + 1)]
                             for hc in range(8)],
                            [sc["HM"][:, 1024 * (8 * hh + hc) + TQ * i:
                                      1024 * (8 * hh + hc) + TQ * (i + 1)]
                             for hc in range(8)],
                            after=dep, tag="b6_ymm")
                        ya = em.op("vector", lambda v, co=co, tqc=tqc, i=i:
                                   v.tensor_tensor(
                                       xsl(ctx, co, tqc), xsl(ctx, co, tqc),
                                       bk(ctx, 4 + 2 * i), ALU.add),
                                   after=[ym], tag="b6_yacc")
                        prev_y[f"ya{i}"] = ya
                        prev_y["ym_last"] = ym
            prev_e = dict(fe=feb_ops[-1], ya_last=prev_y.get("ya1") or
                          prev_y.get("ya0"))
        em.emit()
    st5.close()

    for tqc in range(NTQ):
        em = Em(ctx, f"B7_{tqc}")
        emit_ln_update(ctx, em, tqc, lambda c: xsl(ctx, c, tqc), [], sc,
                       src_bf16=True)
        em.emit()
    stR.close()
    st.close()


def emit_block_c(ctx):
    nc = ctx.nc
    st = ExitStack()
    sc = ctx.t
    ctx.sb("PREACC", [128, C * T], BF16, st)
    ctx.sb("BQC", [128, C], F32, st)
    ctx.sb("BKC", [128, C], F32, st)
    ctx.sb("BVC", [128, C], F32, st)
    ctx.sb("BPC", [128, C], F32, st)

    st0 = ExitStack()
    ctx.sb("GROW", [1, D], F32, st0)
    em = Em(ctx, "C0")
    gprev = None
    for gname, gsb, scl in [("L1_attn_bq", "BQC", 0.125), ("L1_attn_bk", "BKC", 1.0),
                            ("L1_attn_bv", "BVC", 1.0), ("L1_attn_bp", "BPC", 1.0)]:
        gld = em.dma("sync", sc["GROW"][:],
                     bass.AP(ctx.dram[gname].tensor, 0, [[D, 1], [1, D]]),
                     after=[gprev], tag=f"g:{gname}")
        gtr = []
        for c in range(C):
            def tfn(t, c=c):
                return t.transpose(ctx.t["PSUM"][:, c:c + 1],
                                   sc["GROW"][0:1, 128 * c:128 * (c + 1)],
                                   ctx.t["EYEF"][0:1, 0:1])
            gtr.append(em.op("tensor", tfn,
                             after=[gld] + (gtr[-1:] or [gprev]), tag="gtr"))
        gprev = em.op("vector", lambda v, gsb=gsb, scl=scl: v.tensor_scalar(
            sc[gsb][:, 0:C], ctx.t["PSUM"][:, 0:C], scl, None, ALU.mult),
            after=[gtr[-1]], tag="gev")
    em.op("gpsimd", lambda g: g.memset(sc["PREACC"][:], 0.0), tag="pre0")
    em.emit()
    st0.close()

    stA = ExitStack()
    ctx.sb("COS1S", [128, TQ], BF16, stA)
    ctx.sb("SIN1S", [128, TQ], BF16, stA)
    ctx.sb("CMASK", [128, 2048], BF16, stA)
    ctx.sb("K2A", [65, T], BF16, stA)
    ctx.sb("K2B", [65, T], BF16, stA)
    ctx.sb("VPA", [128, 65 * NT], BF16, stA)
    ctx.sb("VPB", [128, 65 * NT], BF16, stA)
    ctx.sb("WQP", [128, C * 128], BF16, stA)
    ctx.sb("WKP", [128, C * 128], BF16, stA)
    ctx.sb("WVP", [128, C * 128], BF16, stA)
    ctx.sb("WPP", [128, C * 128], BF16, stA)
    ctx.sb("QA", [128, TQ], BF16, stA)
    ctx.sb("QAS", [128, TQ], BF16, stA)
    ctx.sb("RT1", [128, TQ], F32, stA)
    ctx.sb("Q2P", [65, 2 * TQ], BF16, stA)
    ctx.sb("SQS", [128, TQ], BF16, stA)
    ctx.t["SNQ0"] = ctx.t["ROWA"]
    ctx.t["SNQ1"] = ctx.t["ROWB"]
    ctx.t["SROW"] = ctx.t["ROWB"]
    ctx.t["SROW2"] = ctx.t["ROWA"]
    ctx.sb("KM2", [1, 2], F32, stA)
    ctx.sb("KC2", [1, 2], F32, stA)
    ctx.sb("PT", [128, 2048], BF16, stA)
    ctx.sb("RROW", [1, TQ], BF16, stA)
    ctx.sb("RBC", [64, TQ], BF16, stA)
    ctx.sb("AONP", [128, TQ], BF16, stA)

    em = Em(ctx, "Cpre")
    em.dma("sync", sc["CMASK"][:], ctx.dram["CMASK"], tag="cmask")
    em.op("gpsimd", lambda g: g.memset(sc["K2A"][64:65, :], 1.0), tag="k2a1")
    em.op("gpsimd", lambda g: g.memset(sc["K2B"][64:65, :], 1.0), tag="k2b1")
    em.op("gpsimd", lambda g: g.memset(sc["VPA"][:], 1.0), tag="vpa1")
    em.op("gpsimd", lambda g: g.memset(sc["VPB"][:], 1.0), tag="vpb1")
    em.emit()

    def rope1(em, psum_bank, dst0, dst1, deps, scale, bias_col, prev,
              extra_ev=()):
        ev = em.op("scalar", lambda s: s.activation(
            sc["QA"][:, 0:TQ], bk(ctx, psum_bank), AF.Identity, scale=scale,
            bias=bias_col),
            after=deps + [prev.get("t1"), prev.get("rot")] + list(extra_ev),
            tag="c_ev")
        rot = em.op("tensor", lambda t: t.matmul(
            bk(ctx, 3), ctx.t["PJ"][:], sc["QA"][:, 0:TQ],
            start=True, stop=True), after=[ev, prev.get("t2")], tag="c_rot")
        t1 = em.op("vector", lambda v: v.tensor_tensor(
            sc["RT1"][:, 0:TQ], sc["QA"][:, 0:TQ], sc["COS1S"][:, 0:TQ],
            ALU.mult),
            after=[ev, prev.get("o0"), prev.get("o1"), prev.get("tbl")],
            tag="c_t1")
        t2 = em.op("vector", lambda v: v.tensor_tensor(
            sc["QAS"][:, 0:TQ], bk(ctx, 3), sc["SIN1S"][:, 0:TQ], ALU.mult),
            after=[rot, prev.get("o0"), prev.get("o1"), prev.get("tbl")],
            tag="c_t2")
        o0 = em.op("vector", lambda v: v.tensor_tensor(
            dst0, sc["RT1"][0:64, 0:TQ], sc["QAS"][0:64, 0:TQ], ALU.add),
            after=[t1, t2], tag="c_o0")
        o1 = em.op("vector", lambda v: v.tensor_tensor(
            dst1, sc["RT1"][64:128, 0:TQ], sc["QAS"][64:128, 0:TQ], ALU.add),
            after=[t1, t2], tag="c_o1")
        return dict(ev=ev, rot=rot, t1=t1, t2=t2, o0=o0, o1=o1)

    wq_t = ctx.dram["L1_attn_Wq"].tensor
    wk_t = ctx.dram["L1_attn_Wk"].tensor
    wv_t = ctx.dram["L1_attn_Wv"].tensor
    wp_t = ctx.dram["L1_attn_Wp"].tensor

    for ro in range(C):
        h0, h1 = 2 * ro, 2 * ro + 1
        em = Em(ctx, f"Cw_{ro}")
        for wt, sb in [(wq_t, "WQP"), (wk_t, "WKP"), (wv_t, "WVP")]:
            em.dma("gpsimd",
                   bass.AP(sc[sb], 0, [[C * 128, 128], [128, C], [1, 128]]),
                   bass.AP(wt, 128 * ro, [[D, 128], [128 * D, C], [1, 128]]),
                   tag=f"w:{sb}")
        em.dma("gpsimd", sc["WPP"][:, 0:C * 128],
               bass.AP(wp_t, 128 * ro * D, [[D, 128], [1, 1], [1, D]]),
               tag="w:WPP")
        em.emit()

        em = Em(ctx, f"Ckv_{ro}")
        prevb = {}
        rp = {}
        km0 = em.op("gpsimd", lambda g: g.memset(sc["KM2"][:], 0.0), tag="km20")
        kn_prev = {0: km0, 1: km0}
        for tqc in range(NTQ):
            tq0 = TQ * tqc
            tl0 = em.dma("sync", sc["COS1S"][:],
                         ctx.dram["COS1"][:, tq0:tq0 + TQ],
                         after=[rp.get("t1")], tag="tbl_c")
            tl1 = em.dma("sync", sc["SIN1S"][:],
                         ctx.dram["SIN1"][:, tq0:tq0 + TQ],
                         after=[rp.get("t2")], tag="tbl_s")
            tbl = em.op("sync", lambda e: None, after=[tl0, tl1], tag="tblw")
            rp["tbl"] = tbl
            kmm = mm_chain(ctx, em, bk(ctx, 0),
                           [sc["WKP"][:, 128 * ci:128 * (ci + 1)]
                            for ci in range(C)],
                           [xsl(ctx, ci, tqc) for ci in range(C)],
                           after=[rp.get("ev")], tag="kmm")
            rp = rope1(em, 0,
                       sc["K2A"][0:64, tq0:tq0 + TQ],
                       sc["K2B"][0:64, tq0:tq0 + TQ],
                       [kmm], 1.0, sc["BKC"][:, ro:ro + 1], rp,
                       extra_ev=[prevb.get("vpe")])
            for par, dep in ((0, rp["o0"]), (1, rp["o1"])):
                k2t = sc["K2A"] if par == 0 else sc["K2B"]
                qq = em.op("scalar", lambda s, k2t=k2t, tq0=tq0: s.activation(
                    sc["SQS"][0:64, 0:TQ], k2t[0:64, tq0:tq0 + TQ],
                    AF.Square), after=[dep, kn_prev[par], kn_prev[1 - par]],
                    tag="kn_sq")
                nm = em.op("tensor", lambda t: t.matmul(
                    bk(ctx, 5, p=1), ctx.t["ONESCB"][0:64, :],
                    sc["SQS"][0:64, 0:TQ], start=True, stop=True),
                    after=[qq, kn_prev[par], kn_prev[1 - par]], tag="kn_mm")
                rd = em.op("vector", lambda v: v.tensor_reduce(
                    sc["SNQ0"][0:1, 0:1], bk(ctx, 5, p=1), AX.X, ALU.max),
                    after=[nm, kn_prev[par], kn_prev[1 - par]], tag="kn_rd")
                ka = em.op("vector", lambda v, par=par: v.tensor_tensor(
                    sc["KM2"][0:1, par:par + 1], sc["KM2"][0:1, par:par + 1],
                    sc["SNQ0"][0:1, 0:1], ALU.max), after=[rd], tag="kn_acc")
                kn_prev[par] = ka
            vmm = mm_chain(ctx, em, bk(ctx, 1),
                           [sc["WVP"][:, 128 * ci:128 * (ci + 1)]
                            for ci in range(C)],
                           [xsl(ctx, ci, tqc) for ci in range(C)],
                           after=[prevb.get("vev")], tag="vmm")
            vev = em.op("scalar", lambda s, ro=ro: s.activation(
                sc["QAS"][:, 0:TQ], bk(ctx, 1), AF.Identity,
                bias=sc["BVC"][:, ro:ro + 1]),
                after=[vmm, rp.get("t2"), rp.get("o0"), rp.get("o1"),
                       prevb.get("vpe")], tag="vev")
            prevb["vev"] = vev
            vpe = prevb.get("vpe")
            for par in range(2):
                vpt = sc["VPA"] if par == 0 else sc["VPB"]
                for j in range(4):
                    def tfn(t, par=par, j=j):
                        return t.transpose(
                            ctx.t["PSB"][:, 64 * j:64 * (j + 1)],
                            sc["QAS"][64 * par:64 * par + 64,
                                      128 * j:128 * (j + 1)],
                            ctx.t["EYEB"][64 * par:64 * par + 64,
                                          64 * par:64 * par + 64])
                    vtr = em.op("tensor", tfn, after=[vev, vpe], tag="vtr")
                    def efn(v, vpt=vpt, j=j, tqc=tqc):
                        jj = 4 * tqc + j
                        return v.tensor_copy(
                            vpt[:, 65 * jj:65 * jj + 64],
                            ctx.t["PSB"][:, 64 * j:64 * (j + 1)])
                    vpe = em.op("vector", efn, after=[vtr], tag="vpe")
            prevb["vpe"] = vpe
        for par in range(2):
            kq = em.op("scalar", lambda s, par=par: s.activation(
                sc["KC2"][0:1, par:par + 1], sc["KM2"][0:1, par:par + 1],
                AF.Sqrt), after=[kn_prev[0], kn_prev[1]], tag="kq")
            em.op("vector", lambda v, par=par: v.tensor_scalar(
                sc["KC2"][0:1, par:par + 1], sc["KC2"][0:1, par:par + 1],
                -1.0, None, ALU.mult), after=[kq], tag="kneg")
        em.emit()

        for tqc in range(NTQ):
            tq0 = TQ * tqc
            em = Em(ctx, f"Cq_{ro}_{tqc}")
            tl0 = em.dma("sync", sc["COS1S"][:],
                         ctx.dram["COS1"][:, tq0:tq0 + TQ], tag="tbl_c")
            tl1 = em.dma("sync", sc["SIN1S"][:],
                         ctx.dram["SIN1"][:, tq0:tq0 + TQ], tag="tbl_s")
            tbl = em.op("sync", lambda e: None, after=[tl0, tl1], tag="tblw")
            qmm = mm_chain(ctx, em, bk(ctx, 0),
                           [sc["WQP"][:, 128 * ci:128 * (ci + 1)]
                            for ci in range(C)],
                           [xsl(ctx, ci, tqc) for ci in range(C)],
                           after=[], tag="qmm")
            rp2 = rope1(em, 0,
                        sc["Q2P"][0:64, 0:TQ],
                        sc["Q2P"][0:64, TQ:2 * TQ],
                        [qmm], 0.125, sc["BQC"][:, ro:ro + 1], {"tbl": tbl})
            c0, c1 = emit_cnorm(
                ctx, em, sc,
                sc["Q2P"][0:64, 0:TQ], sc["Q2P"][0:64, TQ:2 * TQ],
                sc["KC2"][0:1, 0:1], sc["KC2"][0:1, 1:2],
                sc["Q2P"][64:65, 0:TQ], sc["Q2P"][64:65, TQ:2 * TQ],
                [rp2["o0"], rp2["o1"]], [None, None])
            qdeps = [rp2["o0"], rp2["o1"], c0, c1]
            jn = 4 * tqc + 4
            pn1, pn6, lexp = emit_attn_sweep(
                ctx, em, sc, 2, lambda h: jn,
                lambda h, j: (sc["K2A"] if h == 0 else sc["K2B"])[:, 128 * j:128 * (j + 1)],
                lambda h, j: (sc["VPA"] if h == 0 else sc["VPB"])[:, 65 * j:65 * (j + 1)],
                lambda h: sc["Q2P"][:, TQ * h:TQ * (h + 1)],
                lambda h: sc["AONP"][64 * h:64 * h + 64, 0:TQ],
                True, qdeps)
            pre_prev = []
            for co in range(C):
                b = co % 2
                dep = [pn6, lexp] + pre_prev[-2:-1]
                omm = em.op("tensor", lambda t, co=co, b=b: t.matmul(
                    bk(ctx, b), sc["WPP"][:, 128 * co:128 * (co + 1)],
                    sc["AONP"][:, 0:TQ], start=True, stop=True),
                    after=dep, tag=f"c_omm{co}")
                r1 = em.op("vector", lambda v, co=co, b=b, tq0=tq0: v.tensor_tensor(
                    sc["PREACC"][:, T * co + tq0:T * co + tq0 + TQ],
                    sc["PREACC"][:, T * co + tq0:T * co + tq0 + TQ],
                    bk(ctx, b), ALU.add), after=[omm], tag="c_acc")
                pre_prev.append(r1)
            em.emit()
    stA.close()

    st4 = ExitStack()
    ctx.sb("PREF", [128, C * TQ], F32, st4)
    for tqc in range(NTQ):
        em = Em(ctx, f"C4_{tqc}")
        deps = []
        for co in range(C):
            tq0 = TQ * tqc
            d1 = em.op("vector", lambda v, co=co, tq0=tq0: v.tensor_tensor(
                sc["PREF"][:, TQ * co:TQ * (co + 1)],
                sc["PREACC"][:, T * co + tq0:T * co + tq0 + TQ],
                xsl(ctx, co, tqc), ALU.add), after=[], tag="c4_add")
            d2 = em.op("vector", lambda v, co=co: v.tensor_scalar(
                sc["PREF"][:, TQ * co:TQ * (co + 1)],
                sc["PREF"][:, TQ * co:TQ * (co + 1)],
                sc["BPC"][:, co:co + 1], None, ALU.add), after=[d1],
                tag="c4_bp")
            deps.append(d2)
        emit_ln_update(ctx, em, tqc,
                       lambda c: sc["PREF"][:, TQ * c:TQ * (c + 1)],
                       deps, sc)
        em.emit()
    st4.close()
    st.close()


def emit_block_d(ctx):
    nc = ctx.nc
    st = ExitStack()
    sc = ctx.t
    NH = 4
    TPH = NTQ // NH
    HW = TPH * TQ
    ctx.sb("BMC", [128, C], F32, st)
    ctx.sb("BOC", [128, C], F32, st)
    ctx.sb("EMB", [128, 16], F32, st)
    ctx.sb("B2T", [16, 128], BF16, st)
    ctx.sb("B1C", [128, 64], F32, st)
    ctx.sb("W1S", [128, 16 * 512], BF16, st)
    ctx.sb("W2S", [128, 16 * 512], BF16, st)
    ctx.sb("H2TH", [128, C * HW], BF16, st)
    ctx.sb("GTH", [16, C * HW], BF16, st)

    st0 = ExitStack()
    ctx.sb("GROW", [1, D], F32, st0)
    ctx.sb("B1R", [16, 512], F32, st0)
    em = Em(ctx, "D0")
    em.dma("sync", sc["EMB"][:], ctx.dram["L1_ffn_expert_emb"], tag="emb")
    em.dma("gpsimd", sc["B2T"][:], ctx.dram["L1_ffn_b2"], tag="b2")
    w1_t = ctx.dram["L1_ffn_W1"].tensor
    em.dma("gpsimd", sc["W1S"][:, 0:16 * 512],
           bass.AP(w1_t, 0, [[512, 128], [128 * 512, 16], [1, 512]]),
           tag="w1")
    w2_t = ctx.dram["L1_ffn_W2"].tensor
    for ic in range(4):
        em.dma("gpsimd",
               bass.AP(sc["W2S"], 128 * ic, [[16 * 512, 128], [512, 16], [1, 128]]),
               bass.AP(w2_t, 128 * 128 * ic,
                       [[128, 128], [512 * 128, 16], [1, 128]]),
               tag=f"w2_{ic}")
    gprev = None
    for gname, gsb in [("L1_ffn_bm", "BMC"), ("L1_ffn_bo", "BOC")]:
        gld = em.dma("sync", sc["GROW"][:],
                     bass.AP(ctx.dram[gname].tensor, 0, [[D, 1], [1, D]]),
                     after=[gprev], tag=f"g:{gname}")
        gtr = []
        for c in range(C):
            def tfn(t, c=c):
                return t.transpose(ctx.t["PSUM"][:, c:c + 1],
                                   sc["GROW"][0:1, 128 * c:128 * (c + 1)],
                                   ctx.t["EYEF"][0:1, 0:1])
            gtr.append(em.op("tensor", tfn,
                             after=[gld] + (gtr[-1:] or [gprev]), tag="gtr"))
        gprev = em.op("vector", lambda v, gsb=gsb: v.tensor_copy(
            sc[gsb][:, 0:C], ctx.t["PSUM"][:, 0:C]), after=[gtr[-1]], tag="gev")
    b1ld = em.dma("sync", sc["B1R"][:], ctx.dram["L1_ffn_b1"],
                  after=[gprev], tag="b1")
    btr = []
    for ic in range(4):
        def tfn(t, ic=ic):
            return t.transpose(ctx.t["PSUM"][:, 16 * ic:16 * (ic + 1)],
                               sc["B1R"][0:16, 128 * ic:128 * (ic + 1)],
                               ctx.t["EYEF"][0:16, 0:16])
        btr.append(em.op("tensor", tfn,
                         after=[b1ld] + (btr[-1:] or [gprev]), tag="btr"))
    em.op("vector", lambda v: v.tensor_copy(sc["B1C"][:, 0:64],
                                            ctx.t["PSUM"][:, 0:64]),
          after=[btr[-1]], tag="b1ev")
    em.emit()
    st0.close()

    wm_t = ctx.dram["L1_ffn_Wm"].tensor
    wo_t = ctx.dram["L1_ffn_Wo"].tensor

    for hf in range(NH):
        tqcs = [hf * TPH + i for i in range(TPH)]
        st1 = ExitStack()
        ctx.sb("WM", [128, C * D], BF16, st1)
        ctx.sb("RT1", [128, TQ], F32, st1)
        ctx.sb("RT2", [128, TQ], F32, st1)
        ctx.sb("LTS", [16, TQ], F32, st1)
        ctx.sb("LTT", [128, 16], F32, st1)
        ctx.sb("GTM", [128, 16], F32, st1)
        ctx.sb("E16T", [128, 16], F32, st1)
        ctx.sb("MXC", [128, 1], F32, st1)
        ctx.sb("SSC", [128, 1], F32, st1)
        ctx.sb("V8D", [128, 8], F32, st1)
        em = Em(ctx, f"D1_{hf}")
        wmld = em.dma("gpsimd", sc["WM"][:, 0:C * D],
                      bass.AP(wm_t, 0, [[D, 128], [128 * D, C], [1, D]]),
                      tag="wm")
        pv = {}
        for ti, tqc in enumerate(tqcs):
            th0 = TQ * ti
            for j in range(C):
                par = j % 2
                rt = "RT1" if par == 0 else "RT2"
                hmm = mm_chain(ctx, em, bk(ctx, par),
                               [sc["WM"][:, D * ci + 128 * j:
                                         D * ci + 128 * (j + 1)]
                                for ci in range(C)],
                               [xsl(ctx, ci, tqc) for ci in range(C)],
                               after=[pv.get(f"ev{par}"), wmld],
                               tag="d1_hmm")
                evb = em.op("scalar", lambda s, j=j, th0=th0, par=par:
                            s.activation(
                                sc["H2TH"][:, HW * j + th0:HW * j + th0 + TQ],
                                bk(ctx, par), AF.Identity,
                                bias=sc["BMC"][:, j:j + 1]),
                            after=[hmm], tag="d1_evb")
                evf = em.op("vector", lambda v, par=par, j=j, rt=rt:
                            v.tensor_scalar(
                                sc[rt][:, 0:TQ], bk(ctx, par),
                                sc["BMC"][:, j:j + 1], None, ALU.add),
                            after=[hmm, evb, pv.get(f"lgw{par}")],
                            tag="d1_evf")
                pv[f"ev{par}"] = evf
                lgm = em.op("tensor", lambda t, rt=rt: t.matmul(
                    bk(ctx, 3, p=16), sc["EMB"][:], sc[rt][:, 0:TQ],
                    start=True, stop=True),
                    after=[evf, pv.get("lge")], tag="d1_lgm")
                pv[f"lgw{par}"] = lgm
                lte0 = em.op("vector", lambda v: v.tensor_copy(
                    sc["LTS"][0:16, 0:TQ], bk(ctx, 3, p=16)),
                    after=[lgm, pv.get("ltr_last")], tag="d1_lts")
                pv["lge"] = lte0
                for tt in range(4):
                    ltr = em.op("tensor", lambda t, tt=tt: t.transpose(
                        bk(ctx, 4, w=16),
                        sc["LTS"][0:16, 128 * tt:128 * (tt + 1)],
                        ctx.t["EYEF"][0:16, 0:16]),
                        after=[lte0, pv.get("lte")], tag="d1_ltr")
                    lte = em.op("vector", lambda v: v.tensor_copy(
                        sc["LTT"][:], bk(ctx, 4, w=16)),
                        after=[ltr, pv.get("gtm")], tag="d1_lte")
                    pv["lte"] = lte
                    if tt == 3:
                        pv["ltr_last"] = lte
                    mx = em.op("vector", lambda v: v.tensor_reduce(
                        sc["MXC"][:], sc["LTT"][:], AX.X, ALU.max,
                        negate=True), after=[lte, pv.get("e16d")], tag="d1_mx")
                    e16d = em.op("scalar", lambda s: s.activation(
                        sc["E16T"][:], sc["LTT"][:], AF.Exp,
                        bias=sc["MXC"][:]), after=[mx, pv.get("gtm")],
                        tag="d1_e16")
                    pv["e16d"] = e16d
                    ss = em.op("vector", lambda v: v.tensor_reduce(
                        sc["SSC"][:], sc["E16T"][:], AX.X, ALU.add),
                        after=[e16d], tag="d1_ss")
                    rs = em.op("vector", lambda v: v.reciprocal(
                        sc["SSC"][:], sc["SSC"][:]), after=[ss], tag="d1_rs")
                    m8 = em.op("vector", lambda v: v.max(
                        out=sc["V8D"][:], in_=sc["LTT"][:]),
                        after=[lte, rs], tag="d1_m8")
                    gtm = em.op("vector", lambda v: v.tensor_scalar(
                        sc["GTM"][:], sc["LTT"][:], sc["V8D"][:, 1:2], None,
                        ALU.is_ge), after=[m8, pv.get("gtr2")], tag="d1_ge")
                    gtm2 = em.op("vector", lambda v: v.tensor_tensor(
                        sc["GTM"][:], sc["GTM"][:], sc["E16T"][:], ALU.mult),
                        after=[gtm], tag="d1_gmul")
                    gtm3 = em.op("vector", lambda v: v.tensor_scalar(
                        sc["GTM"][:], sc["GTM"][:], sc["SSC"][:], None,
                        ALU.mult), after=[gtm2], tag="d1_gnorm")
                    pv["gtm"] = gtm3
                    gtr2 = em.op("tensor", lambda t: t.transpose(
                        ctx.t["PSUM"][0:16, 2560:2560 + 128], sc["GTM"][:],
                        ctx.t["EYEF"][:]), after=[gtm3, pv.get("gte")],
                        tag="d1_gtr")
                    pv["gtr2"] = gtr2
                    gte = em.op("vector", lambda v, j=j, th0=th0, tt=tt:
                                v.tensor_copy(
                                    sc["GTH"][0:16, HW * j + th0 + 128 * tt:
                                              HW * j + th0 + 128 * (tt + 1)],
                                    ctx.t["PSUM"][0:16, 2560:2560 + 128]),
                                after=[gtr2], tag="d1_gte")
                    pv["gte"] = gte
        em.emit()
        st1.close()

        st2 = ExitStack()
        ctx.sb("HS", [128, 4 * TQ], BF16, st2)
        ctx.sb("FEB", [128, TQ], BF16, st2)
        ctx.sb("MOH", [128, C * TQ], BF16, st2)
        ctx.sb("WO", [128, C * D], BF16, st2)
        ctx.sb("PREF", [128, C * TQ], BF16, st2)
        emw = Em(ctx, f"D2w_{hf}")
        emw.dma("gpsimd", sc["WO"][:, 0:C * D],
                bass.AP(wo_t, 0, [[D, 128], [128 * D, C], [1, D]]), tag="wo")
        emw.emit()
        for ti, tqc in enumerate(tqcs):
            th0 = TQ * ti
            em = Em(ctx, f"D2_{hf}_{tqc}")
            mo_deps = []
            prev_e = {}
            for j in range(C):
                b2m = em.op("tensor", lambda t, j=j, th0=th0: t.matmul(
                    bk(ctx, 4), sc["B2T"][:],
                    sc["GTH"][0:16, HW * j + th0:HW * j + th0 + TQ],
                    start=True, stop=False),
                    after=mo_deps[-1:], tag="d2_b2")
                last_w2 = None
                for e in range(16):
                    fm = em.op("tensor", lambda t, e=e, j=j, th0=th0: t.matmul(
                        bk(ctx, 5), ctx.t["E16"][:, 128 * e:128 * (e + 1)],
                        sc["GTH"][0:16, HW * j + th0:HW * j + th0 + TQ],
                        start=True, stop=True),
                        after=[prev_e.get("fe")], tag="d2_fm")
                    fe = em.op("scalar", lambda s: s.activation(
                        sc["FEB"][:, 0:TQ], bk(ctx, 5), AF.Identity),
                        after=[fm, prev_e.get("gate_3")], tag="d2_fe")
                    for ic in range(4):
                        par = ic % 2
                        w1m = em.op("tensor",
                                    lambda t, e=e, ic=ic, j=j, th0=th0, par=par:
                                    t.matmul(
                                        bk(ctx, par),
                                        sc["W1S"][:, 512 * e + 128 * ic:
                                                  512 * e + 128 * (ic + 1)],
                                        sc["H2TH"][:, HW * j + th0:
                                                   HW * j + th0 + TQ],
                                        start=True, stop=True),
                                    after=[prev_e.get(f"gel{par}")],
                                    tag="d2_w1")
                        gel = em.op("scalar", lambda s, e=e, ic=ic, par=par:
                                    s.activation(
                                        sc["HS"][:, TQ * ic:TQ * (ic + 1)],
                                        bk(ctx, par), AF.Gelu,
                                        bias=sc["B1C"][:, 16 * ic + e:
                                                       16 * ic + e + 1]),
                                    after=[w1m, prev_e.get("w2l")],
                                    tag="d2_gelu")
                        gate = em.op("vector", lambda v, ic=ic: v.tensor_tensor(
                            sc["HS"][:, TQ * ic:TQ * (ic + 1)],
                            sc["HS"][:, TQ * ic:TQ * (ic + 1)],
                            sc["FEB"][:, 0:TQ], ALU.mult),
                            after=[gel, fe], tag="d2_gate")
                        prev_e[f"gel{par}"] = gate
                        prev_e[f"gate_{ic}"] = gate
                    def w2fn(t, e=e):
                        inst = None
                        for ic in range(4):
                            inst = t.matmul(
                                bk(ctx, 4),
                                sc["W2S"][:, 512 * e + 128 * ic:
                                          512 * e + 128 * (ic + 1)],
                                sc["HS"][:, TQ * ic:TQ * (ic + 1)],
                                start=False, stop=(e == 15 and ic == 3))
                        return inst
                    last_w2 = em.op("tensor", w2fn,
                                    after=[prev_e[f"gate_{ic2}"]
                                           for ic2 in range(4)] + [b2m],
                                    tag="d2_w2")
                    prev_e["fe"] = fe
                    prev_e["w2l"] = last_w2
                moev = em.op("scalar", lambda s, j=j: s.activation(
                    sc["MOH"][:, TQ * j:TQ * (j + 1)], bk(ctx, 4),
                    AF.Identity), after=[last_w2], tag="d2_moev")
                mo_deps.append(moev)
            pre_deps = []
            for co in range(C):
                b = co % 2
                omm = mm_chain(ctx, em, bk(ctx, b),
                               [sc["WO"][:, D * ci + 128 * co:
                                         D * ci + 128 * (co + 1)]
                                for ci in range(C)],
                               [sc["MOH"][:, TQ * ci:TQ * (ci + 1)]
                                for ci in range(C)],
                               after=mo_deps + pre_deps[-2:-1], tag="d3_omm")
                r1 = em.op("vector", lambda v, co=co, b=b, tqc=tqc:
                           v.tensor_tensor(
                               sc["PREF"][:, TQ * co:TQ * (co + 1)], bk(ctx, b),
                               xsl(ctx, co, tqc), ALU.add), after=[omm],
                           tag="d3_add")
                r2 = em.op("vector", lambda v, co=co: v.tensor_scalar(
                    sc["PREF"][:, TQ * co:TQ * (co + 1)],
                    sc["PREF"][:, TQ * co:TQ * (co + 1)],
                    sc["BOC"][:, co:co + 1], None, ALU.add), after=[r1],
                    tag="d3_bo")
                pre_deps.append(r2)
            emit_ln_update(ctx, em, tqc,
                           lambda c: sc["PREF"][:, TQ * c:TQ * (c + 1)],
                           pre_deps, sc, src_bf16=True)
            em.emit()
        st2.close()
    st.close()

def _in_map(core, inputs, names, x_state):
    cn = host_constants(core)
    m = {}
    for n in names:
        if n == "xin":
            m[n] = x_state[core]
        elif n in cn:
            m[n] = cn[n]
        else:
            m[n] = np.ascontiguousarray(inputs[n], dtype=np.float32)
    return m


def kernel(**inputs):
    x = np.ascontiguousarray(inputs["L0_attn_x"], dtype=np.float32)
    x_state = [x[i] for i in range(8)]
    launches = [("A", "B", "C", "D")]
    for blocks in launches:
        nc, names = build_program(list(blocks), 0)
        in_maps = []
        for core in range(8):
            if any(b for b in blocks if b == "B"):
                pass
            in_maps.append(_in_map(core, inputs, names, x_state))
        res = run_bass_kernel_spmd(nc, in_maps, core_ids=list(range(8)))
        x_state = [np.asarray(res.results[i]["xout"], dtype=np.float32)
                   for i in range(8)]
    return np.stack(x_state, axis=0)
